# version 01
# baseline (speedup 1.0000x reference)
import os
import numpy as np
import ml_dtypes
import concourse.bass as bass
import concourse.mybir as mybir
from concourse.bass_utils import run_bass_kernel_spmd
from contextlib import ExitStack

F32 = mybir.dt.float32
BF16 = mybir.dt.bfloat16
Alu = mybir.AluOpType
Act = mybir.ActivationFunctionType
AxX = mybir.AxisListType.X

KPH = int(os.environ.get("KPH", "8"))
KDBG = os.environ.get("KDBG", "0") == "1"

NCORES = 8
BPC = 2
L = 2048
D = 1024
H = 16
DH = 64
T = BPC * L
NTT = T // 128
G1 = 512
FI = 4096
NQT = L // 512
NKC = L // 128

LADDER = [128.0, 64.0, 32.0, 16.0, 8.0, 4.0, 2.0, 1.0]

CI_EXPB, CI_SGNB, CI_NHALF, CI_ONE, CI_PEXP = 0, 1, 2, 3, 4
CONSTS = np.zeros(8, np.float32)
CONSTS[CI_EXPB] = -10.0
CONSTS[CI_SGNB] = -128.0
CONSTS[CI_NHALF] = -0.5
CONSTS[CI_ONE] = 1.0
CONSTS[CI_PEXP] = -0.5


class Ctr:
    def __init__(self):
        self.n = {}
        self.m = {}

    def inc(self, name, k=1):
        self.n[name] = self.n.get(name, 0) + k
        return self.n[name]

    def get(self, name):
        return self.n.get(name, 0)


def build_nc():
    nc = bass.Bass("TRN2", target_bir_lowering=False)

    x_in = nc.dram_tensor("x", [BPC, L, D], F32, kind="ExternalInput").ap()
    wqkv = nc.dram_tensor("wqkv", [D, 3 * D], F32, kind="ExternalInput").ap()
    bqkv = nc.dram_tensor("bqkv", [3 * D], F32, kind="ExternalInput").ap()
    outw = nc.dram_tensor("outw", [D, D], F32, kind="ExternalInput").ap()
    outb = nc.dram_tensor("outb", [D], F32, kind="ExternalInput").ap()
    wst = nc.dram_tensor("wst", [H, DH, DH], F32, kind="ExternalInput").ap()
    lam = nc.dram_tensor("lam", [H, DH, DH], F32, kind="ExternalInput").ap()
    gam = nc.dram_tensor("gam", [H, DH, DH], F32, kind="ExternalInput").ap()
    gw1 = nc.dram_tensor("gw1", [D, G1], F32, kind="ExternalInput").ap()
    gb1 = nc.dram_tensor("gb1", [G1], F32, kind="ExternalInput").ap()
    gw2 = nc.dram_tensor("gw2", [G1, H], F32, kind="ExternalInput").ap()
    gb2 = nc.dram_tensor("gb2", [H], F32, kind="ExternalInput").ap()
    wgate = nc.dram_tensor("wgate", [FI, D], F32, kind="ExternalInput").ap()
    wup = nc.dram_tensor("wup", [FI, D], F32, kind="ExternalInput").ap()
    wdown = nc.dram_tensor("wdown", [D, FI], F32, kind="ExternalInput").ap()
    maskc = nc.dram_tensor("maskc", [128, 512], BF16, kind="ExternalInput").ap()
    bqkv_r = nc.dram_tensor("bqkv_r", [128, 24], F32, kind="ExternalInput").ap()
    gb1_r = nc.dram_tensor("gb1_r", [128, 4], F32, kind="ExternalInput").ap()
    lamT = nc.dram_tensor("lamT", [64, 1024], F32, kind="ExternalInput").ap()
    gamT = nc.dram_tensor("gamT", [64, 1024], F32, kind="ExternalInput").ap()
    wstT = nc.dram_tensor("wstT", [64, 1024], F32, kind="ExternalInput").ap()
    ident_d = nc.dram_tensor("identc", [128, 128], BF16, kind="ExternalInput").ap()
    identf_d = nc.dram_tensor("identf", [128, 128], F32, kind="ExternalInput").ap()
    io_d = nc.dram_tensor("ioc", [128, 128], BF16, kind="ExternalInput").ap()
    cvec = nc.dram_tensor("cvec", [1, 8], F32, kind="ExternalInput").ap()
    out = nc.dram_tensor("out", [BPC, L, D], F32, kind="ExternalOutput").ap()

    _K = "ExternalOutput" if KDBG else "Internal"
    q_d = nc.dram_tensor("q_d", [BPC, H, DH, L], BF16, kind=_K).ap()
    k_d = nc.dram_tensor("k_d", [BPC, H, DH, L], BF16, kind=_K).ap()
    v_d = nc.dram_tensor("v_d", [BPC, H, DH, L], BF16, kind=_K).ap()
    vtm_d = nc.dram_tensor("vtm_d", [BPC, H, L, DH], BF16, kind=_K).ap()
    al_d = nc.dram_tensor("al_d", [H, T], BF16, kind=_K).ap()
    den_d = nc.dram_tensor("den_d", [BPC, H, L], F32, kind=_K).ap()
    x1_d = nc.dram_tensor("x1_d", [T, D], F32, kind=_K).ap()
    cc_d = nc.dram_tensor("cc_d", [2, 2048], BF16, kind=_K).ap()
    g2_d = nc.dram_tensor("g2_d", [T, FI], BF16, kind=_K).ap()
    wqg_d = nc.dram_tensor("wqg_d", [D, FI], BF16, kind=_K).ap()
    wqu_d = nc.dram_tensor("wqu_d", [D, FI], BF16, kind=_K).ap()
    wqd_d = nc.dram_tensor("wqd_d", [FI, D], BF16, kind=_K).ap()
    m_d = nc.dram_tensor("m_d", [1, 4], F32, kind=_K).ap()

    x_tok = x_in.rearrange("b l d -> (b l) d")
    out_tok = out.rearrange("b l d -> (b l) d")

    with ExitStack() as es:
        e = es.enter_context

        bigs = [e(nc.sbuf_tensor(f"big{i}", [128, 4096], BF16)) for i in range(8)]
        gbuf = [e(nc.sbuf_tensor(f"gwk{i}", [128, 4096], BF16)) for i in range(4)]
        far = e(nc.sbuf_tensor("far", [128, 8192], F32))
        dbuf = e(nc.sbuf_tensor("dbuf", [128, 4096], BF16))
        lnar = e(nc.sbuf_tensor("lnar", [128, 3072], F32))
        outw_sb = e(nc.sbuf_tensor("outw_sb", [128, 8192], BF16))
        mask_sb = e(nc.sbuf_tensor("mask_sb", [128, 512], BF16))
        ident = e(nc.sbuf_tensor("ident", [128, 128], BF16))
        identF = e(nc.sbuf_tensor("identF", [128, 128], F32))
        iot = e(nc.sbuf_tensor("iot", [128, 128], BF16))
        sel = [e(nc.sbuf_tensor(f"sel{i}", [128, 128], BF16)) for i in range(2)]
        r_all = e(nc.sbuf_tensor("r_all", [128, 1024], F32))
        g_all = e(nc.sbuf_tensor("g_all", [128, 1024], F32))
        wt2 = e(nc.sbuf_tensor("wt2", [128, 1024], BF16))
        wblk = e(nc.sbuf_tensor("wblk", [128, 128], BF16))
        outb_bc = e(nc.sbuf_tensor("outb_bc", [128, 1024], BF16))
        k12 = e(nc.sbuf_tensor("k12", [128, 2048], BF16))
        q12 = e(nc.sbuf_tensor("q12", [128, 2048], BF16))
        ysm = [e(nc.sbuf_tensor(f"ysm{i}", [64, 2048], BF16)) for i in range(2)]
        prb = [e(nc.sbuf_tensor(f"prb{i}", [128, 512], BF16)) for i in range(2)]
        v65 = [e(nc.sbuf_tensor(f"v65{i}", [128, 65], BF16)) for i in range(2)]
        al_bc = e(nc.sbuf_tensor("al_bc", [65, 2048], BF16))
        c2_bc = e(nc.sbuf_tensor("c2_bc", [65, 2048], BF16))
        cbc = e(nc.sbuf_tensor("cbc", [128, 8], F32))
        qb_col = e(nc.sbuf_tensor("qb_col", [128, 24], F32))
        b1c = e(nc.sbuf_tensor("b1c", [128, 4], F32))
        b2c = e(nc.sbuf_tensor("b2c", [16, 1], F32))
        mcols = e(nc.sbuf_tensor("mcols", [128, 8], F32))
        a16 = e(nc.sbuf_tensor("a16", [16, 512], BF16))
        amax2 = e(nc.sbuf_tensor("amax2", [128, 32], F32))
        oscg = e(nc.sbuf_tensor("oscg", [128, 32], F32))
        oscu = e(nc.sbuf_tensor("oscu", [128, 32], F32))
        colw = e(nc.sbuf_tensor("colw", [128, 16], F32))

        psA = e(nc.psum_tensor("psA", [128, 512], F32))
        psB = e(nc.psum_tensor("psB", [128, 512], F32))
        psY = e(nc.psum_tensor("psY", [128, 2048], F32))
        psAV = psY[:, 0:512]
        psTF = [e(nc.psum_tensor("psTF0", [128, 512], F32)),
                e(nc.psum_tensor("psTF1", [128, 512], F32))]

        SEM = {}
        SYNC_POOL = [f"sl{i}" for i in range(12)]
        GP_POOL = [f"sg{i}" for i in range(4)]
        for s in SYNC_POOL + GP_POOL + ["sV", "sS", "sT", "sGE", "sO"]:
            SEM[s] = e(nc.semaphore(s))
        POOLS = {"sync": SYNC_POOL, "gpsimd": GP_POOL}

        blk = e(nc.Block())

        def program(eng, role):
            C = Ctr()
            M = C.m

            def dma(q, outap, inap, sem=None, transpose=False):
                if sem == "sO":
                    v = C.inc("sO", 16)
                    if role == q:
                        eng.dma_start(out=outap, in_=inap).then_inc(SEM["sO"], 16)
                    return v
                pool = POOLS[q]
                idx = C.inc(f"_rr_{q}") % len(pool)
                sm = pool[idx]
                prev = C.get(sm)
                v = C.inc(sm, 16)
                if role == q:
                    if prev > 0:
                        eng.wait_ge(SEM[sm], prev)
                    if transpose:
                        eng.dma_start_transpose(out=outap, in_=inap).then_inc(SEM[sm], 16)
                    else:
                        eng.dma_start(out=outap, in_=inap).then_inc(SEM[sm], 16)
                return (sm, v)

            def snapd():
                return tuple((sm, C.get(sm))
                             for sm in SYNC_POOL + GP_POOL if C.get(sm) > 0)

            def op(r, sem, fn):
                v = C.inc(sem, 1)
                if role == r:
                    fn(eng).then_inc(SEM[sem], 1)
                    if r in ("vector", "gpsimd"):
                        eng.wait_ge(SEM[sem], v)
                return v

            def opq(r, fn):
                if r in ("vector", "gpsimd"):
                    sem = "sV" if r == "vector" else "sGE"
                    v = C.inc(sem, 1)
                    if role == r:
                        fn(eng).then_inc(SEM[sem], 1)
                        eng.wait_ge(SEM[sem], v)
                else:
                    if role == r:
                        fn(eng)

            def w(r, sem, v=None):
                if role != r:
                    return
                if v is None:
                    mk = sem
                    if not mk:
                        return
                    if isinstance(mk, tuple) and mk and isinstance(mk[0], tuple):
                        for sm, vv in mk:
                            eng.wait_ge(SEM[sm], vv)
                    elif isinstance(mk, tuple):
                        eng.wait_ge(SEM[mk[0]], mk[1])
                else:
                    if v and v > 0:
                        eng.wait_ge(SEM[sem], v)

            def fin():
                if role == "sync":
                    for sm in SYNC_POOL:
                        if C.get(sm) > 0:
                            eng.wait_ge(SEM[sm], C.get(sm))
                    if C.get("sO") > 0:
                        eng.wait_ge(SEM["sO"], C.get("sO"))
                if role == "gpsimd":
                    for sm in GP_POOL:
                        if C.get(sm) > 0:
                            eng.wait_ge(SEM[sm], C.get(sm))

            tp_ev = {0: 0, 1: 0}

            dma("sync", mask_sb[:], maskc)
            dma("sync", ident[:], ident_d)
            dma("sync", identF[:], identf_d)
            dma("sync", iot[:], io_d)
            dma("sync", cbc[:], cvec.broadcast_to([128, 8]))
            dma("sync", qb_col[:], bqkv_r)
            dma("sync", b1c[:], gb1_r)
            dma("sync", b2c[:], gb2.unsqueeze(1))
            M["outb"] = dma("gpsimd", outb_bc[:], outb.unsqueeze(0).broadcast_to([128, 1024]))
            dma("sync", r_all[0:64, :], lamT)
            dma("sync", r_all[64:128, :], lamT)
            dma("sync", g_all[0:64, :], gamT)
            dma("sync", g_all[64:128, :], gamT)
            dma("sync", lnar[0:64, 0:1024], wstT)
            dma("sync", lnar[64:128, 0:1024], wstT)
            M["p0"] = snapd()
            w("scalar", M["p0"])
            vsg = op("scalar", "sS", lambda E: E.activation(
                out=r_all[:], in_=r_all[:], func=Act.Sigmoid))
            w("vector", "sS", vsg)
            opq("vector", lambda E: E.tensor_scalar(
                out=r_all[:], in0=r_all[:], scalar1=-1.0, scalar2=1.0,
                op0=Alu.mult, op1=Alu.add))
            opq("vector", lambda E: E.tensor_scalar_mul(
                out=g_all[:], in0=g_all[:], scalar1=0.125))
            w("vector", M["p0"])
            M["rg"] = op("vector", "sV", lambda E: E.tensor_copy(
                out=wt2[:], in_=lnar[:, 0:1024]))
            if KPH <= 0:
                fin(); return

            wtile = far[:, 0:4096]
            p1b = gbuf[1]
            n1b = gbuf[2]
            wqbF = far[:, 4096:8192]
            acc_col = colw[:, 0:1]
            tmp_col = colw[:, 1:2]

            KWS = int(os.environ.get("KWS", "3"))
            def wq_phase(widx, w_ap, wq_ap_d, nrows, ncols):
                nt = nrows // 128
                nel = nrows * ncols
                opq("vector", lambda E: E.memset(acc_col, 0.0))
                last_rd = 0
                for t in range(nt):
                    if t >= 1:
                        w("sync", "sV", M[f"wqa{widx}_{t-1}"])
                    lv = dma("sync", wtile[:, 0:ncols], w_ap[128 * t:128 * (t + 1), :])
                    w("vector", lv)
                    opq("vector", lambda E: E.reduce_sum(
                        out=tmp_col, in_=wtile[:, 0:ncols], axis=AxX,
                        apply_absolute_value=True))
                    last_rd = op("vector", "sV", lambda E: E.tensor_add(
                        out=acc_col, in0=acc_col, in1=tmp_col))
                    M[f"wqa{widx}_{t}"] = last_rd
                    if KDBG and widx == 0:
                        w("sync", "sV", last_rd)
                        snp = dma("sync", den_d[1, 1 + t // 16, (t % 16) * 128:
                                  (t % 16) * 128 + 128], tmp_col)
                        w("vector", snp)
                if KWS <= 1:
                    return
                w("tensor", "sV", last_rd)
                w("tensor", M["p0"])
                if widx > 0:
                    w("tensor", "sV", M[f"wqm{widx-1}"])
                tv = op("tensor", "sT", lambda E: E.matmul(
                    out=psA[0:1, 0:1], lhsT=acc_col, rhs=cbc[:, CI_ONE:CI_ONE + 1],
                    start=True, stop=True))
                w("vector", "sT", tv)
                mv = op("vector", "sV", lambda E: E.tensor_scalar(
                    out=tmp_col[0:1, :], in0=psA[0:1, 0:1],
                    scalar1=1.0 / nel, scalar2=1e-5, op0=Alu.mult, op1=Alu.max))
                M[f"wqm{widx}"] = mv
                w("sync", "sV", mv)
                stm = dma("sync", m_d[0:1, widx:widx + 1], tmp_col[0:1, :])
                w("sync", stm)
                lv = dma("sync", mcols[:, widx:widx + 1],
                         m_d[0:1, widx:widx + 1].broadcast_to([128, 1]))
                w("vector", lv)
                iv = op("vector", "sV", lambda E: E.reciprocal(
                    out=mcols[:, 4 + widx:5 + widx], in_=mcols[:, widx:widx + 1]))
                if KDBG and widx == 0:
                    dma("sync", den_d[0, 0, 0:128], acc_col)
                    dma("sync", den_d[0, 0, 128:256], cbc[:, CI_ONE:CI_ONE + 1])
                if KWS <= 2:
                    return
                nblk = ncols // 128
                KWT = int(os.environ.get("KWT", "99"))
                for t in range(nt):
                    if t >= KWT:
                        continue
                    if t >= 1:
                        w("sync", "sV", M[f"wqv{widx}_{t-1}"])
                    lv = dma("sync", wtile[:, 0:ncols], w_ap[128 * t:128 * (t + 1), :])
                    w("vector", lv)
                    w("vector", "sV", iv)
                    if t >= 1:
                        w("vector", "sT", M[f"wqt{widx}_{t-1}"])
                    opq("vector", lambda E: E.tensor_scalar(
                        out=p1b[:, 0:ncols], in0=wtile[:, 0:ncols],
                        scalar1=mcols[:, 4 + widx:5 + widx], scalar2=0.5,
                        op0=Alu.mult, op1=Alu.is_gt))
                    opq("vector", lambda E: E.tensor_scalar(
                        out=n1b[:, 0:ncols], in0=wtile[:, 0:ncols],
                        scalar1=mcols[:, 4 + widx:5 + widx], scalar2=-0.5,
                        op0=Alu.mult, op1=Alu.is_lt))
                    vv = op("vector", "sV", lambda E: E.tensor_sub(
                        out=wqbF[:, 0:ncols], in0=p1b[:, 0:ncols], in1=n1b[:, 0:ncols]))
                    M[f"wqv{widx}_{t}"] = vv
                    w("tensor", "sV", vv)
                    for cb in range(nblk):
                        pp = cb % 2
                        w("tensor", "sS", tp_ev[pp])
                        tv = op("tensor", "sT", lambda E, c=cb, p=pp: E.transpose(
                            out=psTF[p][:, 0:128],
                            in_=wqbF[:, 128 * c:128 * c + 128], identity=identF[:]))
                        w("scalar", "sT", tv)
                        if cb == 0:
                            prev = (M.get(f"wqs{widx}_{t-1}") if t >= 1
                                    else M.get(f"wqs{widx-1}_{(FI if widx <= 2 else D)//128 - 1}")
                                    if widx >= 1 else None)
                            w("scalar", prev)
                        tp_ev[pp] = op(
                            "scalar", "sS", lambda E, c=cb, p=pp: E.copy(
                                out=dbuf[:, 128 * c:128 * c + 128],
                                in_=psTF[p][:, 0:128]))
                    M[f"wqt{widx}_{t}"] = C.get("sT")
                    M[f"wqe{widx}_{t}"] = C.get("sS")
                    w("sync", "sS", M[f"wqe{widx}_{t}"])
                    if os.environ.get("KWX", "0") != "1":
                        for cb in range(nblk):
                            dma("sync", wq_ap_d[128 * cb:128 * cb + 128,
                                                128 * t:128 * (t + 1)],
                                dbuf[:, 128 * cb:128 * cb + 128])
                    M[f"wqs{widx}_{t}"] = snapd()
                return

            def wq_phase_guarded(widx, w_ap, wq_ap_d, nrows, ncols):
                wq_phase(widx, w_ap, wq_ap_d, nrows, ncols)

            KW = int(os.environ.get("KW", "3"))
            if KW >= 1:
                wq_phase_guarded(0, wgate, wqg_d, FI, D)
            if KW >= 2:
                wq_phase_guarded(1, wup, wqu_d, FI, D)
            if KW >= 3:
                wq_phase_guarded(2, wdown, wqd_d, D, FI)
            if KW < 3:
                fin(); return
            M["p1_S"] = C.get("sS")
            M["p1_T"] = C.get("sT")
            M["p1_V"] = C.get("sV")
            if KPH <= 1:
                fin(); return

            xt_f = lnar[:, 0:1024]
            for t in range(NTT):
                if t >= 1:
                    w("sync", "sT", M[f"xft{t-1}"])
                else:
                    w("sync", "sV", M["rg"])
                lv = dma("sync", xt_f, x_tok[128 * t:128 * (t + 1), :])
                w("tensor", lv)
                for c in range(8):
                    pp = c % 2
                    w("tensor", "sS", tp_ev[pp])
                    tv = op("tensor", "sT", lambda E, cc=c, p=pp: E.transpose(
                        out=psTF[p][:, 0:128],
                        in_=xt_f[:, 128 * cc:128 * cc + 128], identity=identF[:]))
                    w("scalar", "sT", tv)
                    tp_ev[pp] = op("scalar", "sS", lambda E, cc=c, tt=t, p=pp: E.copy(
                        out=bigs[cc][:, 128 * tt:128 * tt + 128],
                        in_=psTF[p][:, 0:128]))
                M[f"xft{t}"] = C.get("sT")
            M["xfm"] = C.get("sS")
            if KPH <= 2:
                fin(); return

            wqv_f32 = [lnar[:, 1024:2048], lnar[:, 2048:3072]]
            wqv_b16 = [gbuf[1][:, 0:1024], gbuf[1][:, 1024:2048]]
            qkv_sb = [gbuf[2][:, 0:512], gbuf[2][:, 512:1024]]
            vtm_sb = [gbuf[2][:, 1024:1152], gbuf[2][:, 1152:1280]]
            ev_bank = {0: 0, 1: 0}
            st_mark = {0: 0, 1: 0}
            vt_mark = {0: 0, 1: 0}
            vstage_use = M[f"xft{NTT-1}"]
            for f in range(24):
                par = f % 2
                if f >= 2:
                    w("sync", "sV", M[f"qkvw{f-2}"])
                lv = dma("sync",
                         wqv_f32[par].rearrange("p (c o) -> p c o", c=8),
                         wqkv[:, 128 * f:128 * (f + 1)]
                         .rearrange("(c p) o -> p c o", p=128))
                w("vector", lv)
                cv = op("vector", "sV", lambda E, p=par: E.tensor_copy(
                    out=wqv_b16[p][:], in_=wqv_f32[p][:]))
                M[f"qkvw{f}"] = cv
                w("tensor", "sV", cv)
                if f == 0:
                    w("tensor", "sS", M["xfm"])
                    w("tensor", "sV", M["wqm2"])
                for tt in range(8):
                    bk = tt % 2
                    w("tensor", "sS", ev_bank[bk])
                    for kc in range(8):
                        op("tensor", "sT", lambda E, k=kc, b2=bk, p=par, t8=tt: E.matmul(
                            out=(psA if b2 == 0 else psB)[:],
                            lhsT=wqv_b16[p][:, 128 * k:128 * (k + 1)],
                            rhs=bigs[k][:, 512 * t8:512 * (t8 + 1)],
                            start=(k == 0), stop=(k == 7)))
                    tv = C.get("sT")
                    w("scalar", "sT", tv)
                    w("scalar", st_mark[bk])
                    sv = op("scalar", "sS", lambda E, b2=bk, ff=f: E.activation(
                        out=qkv_sb[b2][:], in_=(psA if b2 == 0 else psB)[:],
                        func=Act.Identity, bias=qb_col[:, ff:ff + 1], scale=1.0))
                    ev_bank[bk] = sv
                    w("sync", "sS", sv)
                    bidx = tt // 4
                    ts0 = (tt % 4) * 512
                    dst = (q_d if f < 8 else (k_d if f < 16 else v_d))
                    hh = 2 * (f % 8)
                    st_mark[bk] = dma(
                        "sync", dst[bidx, hh:hh + 2, :, ts0:ts0 + 512]
                        .rearrange("h d t -> (h d) t"), qkv_sb[bk][:])
                    if f >= 16:
                        w("scalar", "sT", vstage_use)
                        sv2 = op("scalar", "sS", lambda E, b2=bk, ff=f: E.activation(
                            out=lnar[:, 0:512], in_=(psA if b2 == 0 else psB)[:],
                            func=Act.Identity, bias=qb_col[:, ff:ff + 1], scale=1.0))
                        ev_bank[bk] = sv2
                        w("tensor", "sS", sv2)
                        for c4 in range(4):
                            sl = c4 % 2
                            w("tensor", "sS", tp_ev[sl])
                            tv = op("tensor", "sT", lambda E, c=c4, s=sl: E.transpose(
                                out=psTF[s][:, 0:128],
                                in_=lnar[:, 128 * c:128 * c + 128],
                                identity=identF[:]))
                            w("scalar", "sT", tv)
                            w("scalar", vt_mark[sl])
                            sv = op("scalar", "sS", lambda E, s=sl: E.copy(
                                out=vtm_sb[s][:], in_=psTF[s][:, 0:128]))
                            tp_ev[sl] = sv
                            w("sync", "sS", sv)
                            tb = ts0 + 128 * c4
                            vt_mark[sl] = (
                                dma("sync", vtm_d[bidx, hh, tb:tb + 128, :],
                                    vtm_sb[sl][:, 0:64]),
                                dma("sync", vtm_d[bidx, hh + 1, tb:tb + 128, :],
                                    vtm_sb[sl][:, 64:128]))
                        vstage_use = C.get("sT")
            M["qkv_L"] = snapd()
            M["qkv_T"] = C.get("sT")
            M["qkv_S"] = C.get("sS")
            if KPH <= 3:
                fin(); return

            for c in range(4):
                if c >= 1:
                    w("sync", "sV", M[f"gw1c{c-1}"])
                lv = dma("sync",
                         lnar[:, 1024:2048].rearrange("p (k o) -> p k o", k=8),
                         gw1[:, 128 * c:128 * (c + 1)]
                         .rearrange("(k p) o -> p k o", p=128))
                w("vector", lv)
                M[f"gw1c{c}"] = op("vector", "sV", lambda E, cc=c: E.tensor_copy(
                    out=gbuf[3][:, 1024 * cc:1024 * (cc + 1)], in_=lnar[:, 1024:2048]))
            w("sync", "sV", M["gw1c3"])
            lv = dma("sync",
                     lnar[:, 1024:1088].rearrange("p (c h) -> p c h", c=4),
                     gw2.rearrange("(c p) h -> p c h", p=128))
            w("vector", lv)
            M["gw2"] = op("vector", "sV", lambda E: E.tensor_copy(
                out=k12[:, 0:64], in_=lnar[:, 1024:1088]))
            gtile = gbuf[1][:, 2048:2560]
            av_ev = 0
            gt_use = 0
            al_st = None
            for tt in range(8):
                for c in range(4):
                    bk = c % 2
                    w("tensor", "sV", M[f"gw1c{c}"])
                    w("tensor", "sV", M["gw2"])
                    w("tensor", "sS", ev_bank[bk])
                    for kc in range(8):
                        op("tensor", "sT", lambda E, k=kc, b2=bk, cc=c, t8=tt: E.matmul(
                            out=(psA if b2 == 0 else psB)[:],
                            lhsT=gbuf[3][:, 1024 * cc + 128 * k:1024 * cc + 128 * (k + 1)],
                            rhs=bigs[k][:, 512 * t8:512 * (t8 + 1)],
                            start=(k == 0), stop=(k == 7)))
                    tv = C.get("sT")
                    w("scalar", "sT", tv)
                    w("scalar", "sT", gt_use)
                    sv = op("scalar", "sS", lambda E, b2=bk, cc=c: E.activation(
                        out=gtile[:], in_=(psA if b2 == 0 else psB)[:],
                        func=Act.Gelu, bias=b1c[:, cc:cc + 1], scale=1.0))
                    ev_bank[bk] = sv
                    w("tensor", "sS", sv)
                    if c == 0:
                        w("tensor", "sS", av_ev)
                    op("tensor", "sT", lambda E, cc=c: E.matmul(
                        out=psAV[0:16, :], lhsT=k12[:, 16 * cc:16 * (cc + 1)],
                        rhs=gtile[:], start=(cc == 0), stop=(cc == 3)))
                    gt_use = C.get("sT")
                w("scalar", "sT", gt_use)
                w("scalar", al_st)
                av_ev = op("scalar", "sS", lambda E: E.activation(
                    out=a16[:, 0:512], in_=psAV[0:16, :], func=Act.Sigmoid,
                    bias=b2c[:], scale=1.0))
                w("sync", "sS", av_ev)
                al_st = dma("sync", al_d[:, 512 * tt:512 * (tt + 1)], a16[:, 0:512])
            M["alpha_L"] = snapd()
            M["p3_psAB"] = max(ev_bank[0], ev_bank[1])
            M["p3_psAV"] = av_ev
            M["p3_gtuse"] = gt_use
            if KPH <= 4:
                fin(); return

            last_kq_use = M["p3_gtuse"]
            prb_use = {0: 0, 1: 0}
            v65_use = {0: 0, 1: 0}
            sc_ev = {0: M["p3_psAB"], 1: M["p3_psAB"]}
            av_done = 0
            first_av = True
            blend_done = 0
            den_st_last = 0
            u_mark = {0: 0, 1: 0}
            red_mark = {0: 0, 1: 0}
            w("sync", M["qkv_L"])
            w("vector", M["qkv_L"])
            KH = int(os.environ.get("KH", str(H)))
            for h in range(KH):
                w("sync", "sT", last_kq_use)
                kqm = []
                for b in range(BPC):
                    kqm.append(dma("sync", k12[64 * b:64 * b + 64, :], k_d[b, h, :, :]))
                    kqm.append(dma("sync", q12[64 * b:64 * b + 64, :], q_d[b, h, :, :]))
                M[f"h{h}kq"] = tuple(kqm)
                w("vector", lv)
                opq("vector", lambda E: E.memset(wblk[:], 0.0))
                opq("vector", lambda E, hh=h: E.tensor_copy(
                    out=wblk[0:64, 0:64], in_=wt2[0:64, 64 * hh:64 * hh + 64]))
                wbv = op("vector", "sV", lambda E, hh=h: E.tensor_copy(
                    out=wblk[64:128, 64:128], in_=wt2[64:128, 64 * hh:64 * hh + 64]))
                if h == 0:
                    opq("vector", lambda E: E.memset(v65[0][:, 64:65], 1.0))
                    opq("vector", lambda E: E.memset(v65[1][:, 64:65], 1.0))

                for b in range(BPC):
                    for qt in range(NQT):
                        av_mm = 0
                        for kc in range(4 * qt + 4):
                            pp = kc % 2
                            w("tensor", "sS", sc_ev[pp])
                            w("tensor", M[f"h{h}kq"])
                            op("tensor", "sT", lambda E, k=kc, q=qt, bb=b, p=pp: E.matmul(
                                out=(psA if p == 0 else psB)[:],
                                lhsT=k12[64 * bb:64 * bb + 64, 128 * k:128 * (k + 1)],
                                rhs=q12[64 * bb:64 * bb + 64, 512 * q:512 * (q + 1)],
                                start=True, stop=True))
                            tv = C.get("sT")
                            w("scalar", "sT", tv)
                            w("scalar", "sT", prb_use[pp])
                            sv = op("scalar", "sS", lambda E, p=pp: E.activation(
                                out=prb[p][:], in_=(psA if p == 0 else psB)[:],
                                func=Act.Exp, bias=cbc[:, CI_EXPB:CI_EXPB + 1],
                                scale=0.125))
                            sc_ev[pp] = sv
                            mask_v = 0
                            if kc >= 4 * qt:
                                r = 128 * kc - 512 * qt
                                w("vector", "sS", sv)
                                mask_v = op("vector", "sV", lambda E, p=pp, rr=r: E.scalar_tensor_tensor(
                                    out=prb[p][:], in0=mask_sb[:],
                                    scalar=float(rr), in1=prb[p][:],
                                    op0=Alu.is_ge, op1=Alu.mult))
                            w("sync", "sT", v65_use[pp])
                            lv = dma("sync", v65[pp][:, 0:64],
                                     vtm_d[b, h, 128 * kc:128 * (kc + 1), :])
                            w("tensor", lv)
                            if mask_v:
                                w("tensor", "sV", mask_v)
                            else:
                                w("tensor", "sS", sv)
                            if kc == 0:
                                if first_av:
                                    w("tensor", "sS", M["p3_psAV"])
                                    first_av = False
                                w("tensor", "sV", av_done)
                            op("tensor", "sT", lambda E, p=pp, k=kc, q=qt: E.matmul(
                                out=psAV[0:65, :], lhsT=v65[p][:], rhs=prb[p][:],
                                start=(k == 0), stop=(k == 4 * q + 3)))
                            av_mm = C.get("sT")
                            prb_use[pp] = av_mm
                            v65_use[pp] = av_mm
                        w("vector", "sT", av_mm)
                        if qt == 0:
                            w("vector", den_st_last)
                        opq("vector", lambda E, bb=b, q=qt: E.tensor_copy(
                            out=ysm[bb][:, 512 * q:512 * (q + 1)], in_=psAV[0:64, :]))
                        av_done = op("vector", "sV", lambda E, q=qt: E.tensor_copy(
                            out=lnar[64:65, 512 * q:512 * (q + 1)], in_=psAV[64:65, :]))
                    w("sync", "sV", av_done)
                    den_st_last = dma("sync", den_d[b, h, :], lnar[64:65, 0:2048])
                    M[f"h{h}den{b}"] = den_st_last

                w("tensor", "sV", wbv)
                w("tensor", "sV", blend_done)
                w("tensor", "sV", av_done)
                for qt4 in range(4):
                    op("tensor", "sT", lambda E, q=qt4: E.matmul(
                        out=psY[:, 512 * q:512 * (q + 1)], lhsT=wblk[:],
                        rhs=q12[:, 512 * q:512 * (q + 1)], start=True, stop=False))
                u_f = [far[:, 0:2048], far[:, 2048:4096]]
                z_f = [far[:, 4096:6144], far[:, 6144:8192]]
                vrep = [gbuf[1][:, 0:2048], gbuf[1][:, 2048:4096]]
                wbf = [gbuf[2][:, 0:2048], gbuf[2][:, 2048:4096]]
                for i in range(DH):
                    par = i % 2
                    qname = "sync" if par == 0 else "gpsimd"
                    if h == 0 and i < 2:
                        w(qname, "sT", M["p3_gtuse"])
                    w(qname, "sV", u_mark[par])
                    vmks = tuple(
                        dma(qname, vrep[par][64 * b:64 * b + 64, :],
                            v_d[b, h, i, :].unsqueeze(0)
                            .broadcast_to([64, 2048])) for b in range(BPC))
                    w("vector", "sT", red_mark[par])
                    opq("vector", lambda E, ii=i, p=par: E.tensor_single_scalar(
                        out=sel[p][:], in_=iot[:], scalar=float(ii), op=Alu.is_equal))
                    w("vector", vmks)
                    u_mark[par] = op("vector", "sV", lambda E, p=par: E.tensor_mul(
                        out=u_f[p][:], in0=vrep[p][:], in1=k12[:]))
                    opq("vector", lambda E, p=par, hh=h, ii=i: E.tensor_tensor_scan(
                        out=z_f[p][:],
                        data0=r_all[:, 64 * hh + ii:64 * hh + ii + 1]
                        .broadcast_to([128, 2048]),
                        data1=u_f[p][:], initial=0.0, op0=Alu.mult, op1=Alu.add))
                    wv = op("vector", "sV", lambda E, p=par, hh=h, ii=i: E.scalar_tensor_tensor(
                        out=wbf[p][:], in0=z_f[p][:],
                        scalar=g_all[:, 64 * hh + ii:64 * hh + ii + 1],
                        in1=q12[:], op0=Alu.mult, op1=Alu.mult))
                    w("tensor", "sV", wv)
                    for qt4 in range(4):
                        op("tensor", "sT", lambda E, q=qt4, p=par, ii=i: E.matmul(
                            out=psY[:, 512 * q:512 * (q + 1)], lhsT=sel[p][:],
                            rhs=wbf[p][:, 512 * q:512 * (q + 1)],
                            start=False, stop=(ii == DH - 1)))
                    red_mark[par] = C.get("sT")
                    last_kq_use = red_mark[par]
                M[f"h{h}scan"] = C.get("sT")

                for b in range(BPC):
                    w("sync", M["alpha_L"])
                    w("sync", M[f"h{h}den{b}"])
                    if blend_done:
                        w("sync", "sV", blend_done)
                    lv2 = dma("sync", al_bc[64:65, :],
                              al_d[h:h + 1, 2048 * b:2048 * (b + 1)])
                    lv2b = dma("sync", lnar[64:65, 0:2048],
                               den_d[b:b + 1, h, :])
                    w("vector", lv2)
                    w("vector", lv2b)
                    opq("vector", lambda E: E.tensor_scalar(
                        out=c2_bc[64:65, :], in0=al_bc[64:65, :],
                        scalar1=-1.0, scalar2=1.0, op0=Alu.mult, op1=Alu.add))
                    opq("vector", lambda E: E.reciprocal(
                        out=lnar[64:65, 0:2048], in_=lnar[64:65, 0:2048]))
                    cro = op("vector", "sV", lambda E: E.tensor_mul(
                        out=al_bc[64:65, :], in0=al_bc[64:65, :],
                        in1=lnar[64:65, 0:2048]))
                    w("sync", "sV", cro)
                    bmk1 = dma("sync", cc_d[0:1, :], al_bc[64:65, :])
                    bmk = dma("sync", cc_d[1:2, :], c2_bc[64:65, :])
                    w("sync", bmk1)
                    w("sync", bmk)
                    lv3a = dma("sync", al_bc[0:64, :],
                               cc_d[0:1, :].broadcast_to([64, 2048]))
                    lv3 = dma("sync", c2_bc[0:64, :],
                              cc_d[1:2, :].broadcast_to([64, 2048]))
                    w("vector", lv3a)
                    w("vector", lv3)
                    w("vector", "sT", M[f"h{h}scan"])
                    opq("vector", lambda E, bb=b: E.tensor_mul(
                        out=al_bc[0:64, :], in0=ysm[bb][:], in1=al_bc[0:64, :]))
                    opq("vector", lambda E, bb=b: E.scalar_tensor_tensor(
                        out=c2_bc[0:64, :], in0=psY[64 * bb:64 * bb + 64, :],
                        scalar=1.0, in1=c2_bc[0:64, :], op0=Alu.mult, op1=Alu.mult))
                    rows0 = (h % 2) * 64
                    blend_done = op("vector", "sV", lambda E, bb=b, r0=rows0, hh=h: E.tensor_add(
                        out=bigs[hh // 2][r0:r0 + 64, 2048 * bb:2048 * (bb + 1)],
                        in0=al_bc[0:64, :], in1=c2_bc[0:64, :]))
            M["p4_V"] = C.get("sV")
            M["p4_psAB"] = max(sc_ev[0], sc_ev[1])
            if KPH <= 5:
                if KDBG:
                    w("sync", "sV", M["p4_V"])
                    for c in range(8):
                        dma("sync", g2_d[128 * c:128 * (c + 1), :], bigs[c][:])
                fin(); return

            w("sync", den_st_last)
            for kk in range(8):
                if kk >= 1:
                    w("sync", "sV", M[f"ow{kk-1}"])
                lv = dma("sync", lnar[:, 0:1024], outw[128 * kk:128 * (kk + 1), :])
                w("vector", lv)
                w("vector", "sV", M["p4_V"])
                M[f"ow{kk}"] = op("vector", "sV", lambda E, k=kk: E.tensor_copy(
                    out=outw_sb[:, 1024 * k:1024 * (k + 1)], in_=lnar[:, 0:1024]))
            xres = lnar[:, 0:1024]
            lnb = lnar[:, 1024:2048]
            d2b = lnar[:, 2048:3072]
            u0b = far[:, 0:1024]
            uLb = far[:, 1024:2048]
            xqF = far[:, 2048:3072]
            s_col = colw[:, 2:3]
            am_col = colw[:, 3:4]
            mn_col = colw[:, 4:5]
            vr_col = colw[:, 5:6]
            rs_col = colw[:, 6:7]
            res_done = 0
            x1st = None
            for t in range(NTT):
                w("tensor", "sV", M["ow7"])
                w("tensor", "sS", M["p4_psAB"])
                w("tensor", "sV", res_done)
                for half in range(2):
                    for kk in range(8):
                        op("tensor", "sT", lambda E, k=kk, hf=half, tt=t: E.matmul(
                            out=(psA if hf == 0 else psB)[:],
                            lhsT=bigs[k][:, 128 * tt:128 * (tt + 1)],
                            rhs=outw_sb[:, 1024 * k + 512 * hf:1024 * k + 512 * hf + 512],
                            start=(k == 0), stop=(k == 7)))
                mmv = C.get("sT")
                M[f"p5mm{t}"] = mmv
                if t == 0:
                    w("sync", "sV", M["ow7"])
                lv = dma("sync", xres, x_tok[128 * t:128 * (t + 1), :])
                w("vector", "sT", mmv)
                w("vector", lv)
                w("vector", x1st)
                for half in range(2):
                    opq("vector", lambda E, hf=half: E.scalar_tensor_tensor(
                        out=lnb[:, 512 * hf:512 * hf + 512],
                        in0=(psA if hf == 0 else psB)[:], scalar=1.0,
                        in1=xres[:, 512 * hf:512 * hf + 512],
                        op0=Alu.mult, op1=Alu.add))
                if t == 0:
                    w("vector", M["outb"])
                res_done = op("vector", "sV", lambda E: E.tensor_add(
                    out=lnb[:], in0=lnb[:], in1=outb_bc[:]))
                opq("vector", lambda E: E.reduce_sum(out=mn_col, in_=lnb[:], axis=AxX))
                opq("vector", lambda E: E.tensor_scalar_mul(
                    out=mn_col, in0=mn_col, scalar1=1.0 / D))
                opq("vector", lambda E: E.tensor_single_scalar(
                    out=d2b, in_=lnb[:], scalar=mn_col, op=Alu.subtract))
                opq("vector", lambda E: E.scalar_tensor_tensor(
                    out=lnb[:], in0=d2b, scalar=1.0, in1=d2b,
                    op0=Alu.mult, op1=Alu.mult, accum_out=vr_col))
                vv = op("vector", "sV", lambda E: E.tensor_scalar(
                    out=vr_col, in0=vr_col, scalar1=1.0 / D, scalar2=1e-5,
                    op0=Alu.mult, op1=Alu.add))
                w("gpsimd", "sV", vv)
                gv = op("gpsimd", "sGE", lambda E: E.tensor_tensor(
                    out=rs_col, in0=vr_col, in1=cbc[:, CI_PEXP:CI_PEXP + 1],
                    op=Alu.pow))
                w("vector", "sGE", gv)
                x1v = op("vector", "sV", lambda E: E.tensor_scalar_mul(
                    out=lnb[:], in0=d2b, scalar1=rs_col))
                w("sync", "sV", x1v)
                x1st = dma("sync", x1_d[128 * t:128 * (t + 1), :], lnb[:])
                opq("vector", lambda E: E.reduce_max(
                    out=am_col, in_=lnb[:], axis=AxX, apply_absolute_value=True))
                opq("vector", lambda E: E.tensor_scalar_max(
                    out=am_col, in0=am_col, scalar1=1e-5))
                opq("vector", lambda E: E.reciprocal(out=s_col, in_=am_col))
                opq("vector", lambda E: E.tensor_scalar_mul(
                    out=s_col, in0=s_col, scalar1=127.0))
                opq("vector", lambda E, tt=t: E.scalar_tensor_tensor(
                    out=oscg[:, tt:tt + 1], in0=am_col, scalar=1.0 / 127.0,
                    in1=mcols[:, 0:1], op0=Alu.mult, op1=Alu.mult))
                opq("vector", lambda E, tt=t: E.scalar_tensor_tensor(
                    out=oscu[:, tt:tt + 1], in0=am_col, scalar=1.0 / 127.0,
                    in1=mcols[:, 1:2], op0=Alu.mult, op1=Alu.mult))
                opq("vector", lambda E: E.tensor_scalar(
                    out=u0b, in0=lnb[:], scalar1=s_col, scalar2=128.5,
                    op0=Alu.mult, op1=Alu.add))
                lastv = op("vector", "sV", lambda E: E.tensor_copy(out=uLb, in_=u0b))
                for li, bb in enumerate(LADDER):
                    w("scalar", "sV", lastv)
                    if t == 0 and li == 0:
                        w("scalar", M["wqs2_7"])
                    sv = op("scalar", "sS", lambda E: E.activation(
                        out=dbuf[:, 0:1024], in_=uLb, func=Act.Sign,
                        bias=cbc[:, CI_SGNB:CI_SGNB + 1], scale=1.0))
                    w("vector", "sS", sv)
                    lastv = op("vector", "sV", lambda E, b2=bb: E.scalar_tensor_tensor(
                        out=uLb, in0=dbuf[:, 0:1024], scalar=-b2 / 2.0, in1=uLb,
                        op0=Alu.mult, op1=Alu.add))
                w("vector", "sT", M.get(f"p5tp{t-1}", 0))
                xqv = op("vector", "sV", lambda E: E.scalar_tensor_tensor(
                    out=xqF, in0=uLb, scalar=-1.0, in1=u0b,
                    op0=Alu.mult, op1=Alu.add))
                w("tensor", "sV", xqv)
                for c in range(8):
                    pp = c % 2
                    w("tensor", "sS", tp_ev[pp])
                    tv = op("tensor", "sT", lambda E, cc=c, p=pp: E.transpose(
                        out=psTF[p][:, 0:128],
                        in_=xqF[:, 128 * cc:128 * cc + 128],
                        identity=identF[:]))
                    w("scalar", "sT", tv)
                    tp_ev[pp] = op("scalar", "sS", lambda E, cc=c, tt=t, p=pp: E.activation(
                        out=bigs[cc][:, 128 * tt:128 * tt + 128],
                        in_=psTF[p][:, 0:128], func=Act.Identity,
                        bias=cbc[:, CI_NHALF:CI_NHALF + 1], scale=1.0))
                M[f"p5tp{t}"] = C.get("sT")
                M[f"p5q{t}"] = C.get("sS")
            M["p5_S"] = C.get("sS")
            M["p5_V"] = C.get("sV")
            if KPH <= 6:
                fin(); return

            wgv = [gbuf[1][:, 0:2048], gbuf[1][:, 2048:4096]]
            wuv = [gbuf[2][:, 0:2048], gbuf[2][:, 2048:4096]]
            h1t = gbuf[3][:, 0:256]
            h2t = gbuf[3][:, 256:512]
            g2t = [gbuf[3][:, 512:768], gbuf[3][:, 768:1024]]
            opq("vector", lambda E: E.memset(amax2[:], 0.0))
            evA = M["p4_psAB"]
            evB = M["p4_psAB"]
            h1_use = 0
            wblk_use = {0: 0, 1: 0}
            g2st = {0: 0, 1: 0}
            for nb in range(FI // 256):
                par = nb % 2
                if nb == 0:
                    w("sync", "sT", M[f"h{H-1}scan"])
                    w("sync", "sV", M["p4_V"])
                w("sync", "sT", wblk_use[par])
                lg = dma("sync",
                         wgv[par].rearrange("p (c o) -> p c o", c=8),
                         wqg_d[:, 256 * nb:256 * (nb + 1)]
                         .rearrange("(c p) o -> p c o", p=128))
                lu = dma("sync",
                         wuv[par].rearrange("p (c o) -> p c o", c=8),
                         wqu_d[:, 256 * nb:256 * (nb + 1)]
                         .rearrange("(c p) o -> p c o", p=128))
                for tt in range(NTT):
                    w("tensor", lg)
                    w("tensor", lu)
                    if nb == 0 and tt == 0:
                        w("tensor", "sS", M["p5_S"])
                        w("tensor", "sV", res_done)
                    w("tensor", "sS", evA)
                    w("tensor", "sV", evB)
                    for kk in range(8):
                        op("tensor", "sT", lambda E, k=kk, p=par, t2=tt: E.matmul(
                            out=psA[:, 0:256],
                            lhsT=bigs[k][:, 128 * t2:128 * (t2 + 1)],
                            rhs=wgv[p][:, 256 * k:256 * (k + 1)],
                            start=(k == 0), stop=(k == 7)))
                        op("tensor", "sT", lambda E, k=kk, p=par, t2=tt: E.matmul(
                            out=psB[:, 0:256],
                            lhsT=bigs[k][:, 128 * t2:128 * (t2 + 1)],
                            rhs=wuv[p][:, 256 * k:256 * (k + 1)],
                            start=(k == 0), stop=(k == 7)))
                    mmv = C.get("sT")
                    wblk_use[par] = mmv
                    w("scalar", "sT", mmv)
                    if nb == 0 and tt == 0:
                        w("scalar", "sV", M["p5_V"])
                    w("scalar", "sV", h1_use)
                    evA = op("scalar", "sS", lambda E, t2=tt: E.activation(
                        out=h1t[:], in_=psA[:, 0:256], func=Act.Silu,
                        scale=oscg[:, t2:t2 + 1]))
                    w("vector", "sS", evA)
                    w("vector", "sT", mmv)
                    w("vector", g2st[tt % 2])
                    opq("vector", lambda E, t2=tt: E.tensor_scalar_mul(
                        out=h2t[:], in0=psB[:, 0:256], scalar1=oscu[:, t2:t2 + 1]))
                    gv = op("vector", "sV", lambda E, p2=tt % 2: E.tensor_mul(
                        out=g2t[p2][:], in0=h1t[:], in1=h2t[:]))
                    evB = gv
                    h1_use = gv
                    opq("vector", lambda E, p2=tt % 2: E.reduce_max(
                        out=tmp_col, in_=g2t[p2][:], axis=AxX,
                        apply_absolute_value=True))
                    av = op("vector", "sV", lambda E, t2=tt: E.tensor_max(
                        out=amax2[:, t2:t2 + 1], in0=amax2[:, t2:t2 + 1],
                        in1=tmp_col))
                    w("sync", "sV", av)
                    g2st[tt % 2] = dma(
                        "sync", g2_d[128 * tt:128 * (tt + 1), 256 * nb:256 * (nb + 1)],
                        g2t[tt % 2][:])
            M["p6_T"] = C.get("sT")
            M["p6_V"] = C.get("sV")
            M["p6_L"] = snapd()
            if KPH <= 7:
                fin(); return

            w("sync", "sT", M["p6_T"])
            w("sync", M["p6_L"])
            for kk in range(32):
                lv = dma("sync", bigs[kk // 4][:, 1024 * (kk % 4):1024 * (kk % 4) + 1024],
                         wqd_d[128 * kk:128 * (kk + 1), :])
            M["wdT"] = snapd()
            g2l = [gbuf[1][:, 0:4096], gbuf[2][:, 0:4096]]
            gqb = gbuf[3][:, 0:4096]
            u0w = far[:, 0:4096]
            uLw = far[:, 4096:8192]
            s2_col = colw[:, 8:9]
            oscd = colw[:, 9:10]
            u0_done = {0: 0, 1: 0}
            gq_used = 0
            res7 = 0
            psT7 = M["p5_S"]
            for t in range(NTT):
                par = t % 2
                w("sync", "sV", u0_done[par])
                lv = dma("sync", g2l[par][:], g2_d[128 * t:128 * (t + 1), :])
                w("vector", lv)
                if t == 0:
                    w("vector", "sV", M["p6_V"])
                opq("vector", lambda E, t2=t: E.tensor_scalar_max(
                    out=am_col, in0=amax2[:, t2:t2 + 1], scalar1=1e-5))
                opq("vector", lambda E: E.reciprocal(out=s2_col, in_=am_col))
                opq("vector", lambda E: E.tensor_scalar_mul(
                    out=s2_col, in0=s2_col, scalar1=127.0))
                opq("vector", lambda E: E.scalar_tensor_tensor(
                    out=oscd, in0=am_col, scalar=1.0 / 127.0,
                    in1=mcols[:, 2:3], op0=Alu.mult, op1=Alu.mult))
                w("vector", "sT", M.get(f"p7tp{t-1}", 0))
                u0v = op("vector", "sV", lambda E, p=par: E.tensor_scalar(
                    out=u0w, in0=g2l[p][:], scalar1=s2_col, scalar2=128.5,
                    op0=Alu.mult, op1=Alu.add))
                u0_done[par] = u0v
                lastv = op("vector", "sV", lambda E: E.tensor_copy(out=uLw, in_=u0w))
                for bb in LADDER:
                    w("scalar", "sV", lastv)
                    if t == 0 and bb == LADDER[0]:
                        w("scalar", M["wdT"])
                    sv = op("scalar", "sS", lambda E: E.activation(
                        out=dbuf[:], in_=uLw, func=Act.Sign,
                        bias=cbc[:, CI_SGNB:CI_SGNB + 1], scale=1.0))
                    w("vector", "sS", sv)
                    lastv = op("vector", "sV", lambda E, b2=bb: E.scalar_tensor_tensor(
                        out=uLw, in0=dbuf[:], scalar=-b2 / 2.0, in1=uLw,
                        op0=Alu.mult, op1=Alu.add))
                xqv = op("vector", "sV", lambda E: E.scalar_tensor_tensor(
                    out=u0w, in0=uLw, scalar=-1.0, in1=u0w,
                    op0=Alu.mult, op1=Alu.add))
                w("tensor", "sV", xqv)
                for c in range(32):
                    pp = c % 2
                    w("tensor", "sS", tp_ev[pp])
                    tv = op("tensor", "sT", lambda E, cc=c, p=pp: E.transpose(
                        out=psTF[p][:, 0:128],
                        in_=u0w[:, 128 * cc:128 * cc + 128], identity=identF[:]))
                    w("scalar", "sT", tv)
                    if c == 0:
                        w("scalar", "sT", gq_used)
                        if t == 0:
                            w("scalar", "sV", M["p6_V"])
                            w("scalar", M["p6_L"])
                    tp_ev[pp] = op("scalar", "sS", lambda E, cc=c, p=pp: E.activation(
                        out=gqb[:, 128 * cc:128 * cc + 128],
                        in_=psTF[p][:, 0:128],
                        func=Act.Identity, bias=cbc[:, CI_NHALF:CI_NHALF + 1],
                        scale=1.0))
                M[f"p7tp{t}"] = C.get("sT")
                psT7 = C.get("sS")
                w("tensor", "sS", psT7)
                w("tensor", M["wdT"])
                w("tensor", "sV", res7)
                for kk in range(32):
                    op("tensor", "sT", lambda E, k=kk: E.matmul(
                        out=psA[:], lhsT=gqb[:, 128 * k:128 * (k + 1)],
                        rhs=bigs[k // 4][:, 1024 * (k % 4):1024 * (k % 4) + 512],
                        start=(k == 0), stop=(k == 31)))
                    op("tensor", "sT", lambda E, k=kk: E.matmul(
                        out=psB[:], lhsT=gqb[:, 128 * k:128 * (k + 1)],
                        rhs=bigs[k // 4][:, 1024 * (k % 4) + 512:1024 * (k % 4) + 1024],
                        start=(k == 0), stop=(k == 31)))
                gq_used = C.get("sT")
                lv = dma("sync", xres, x1_d[128 * t:128 * (t + 1), :])
                w("vector", "sT", gq_used)
                w("vector", lv)
                if t >= 1:
                    w("vector", "sO", 16 * t)
                for half in range(2):
                    opq("vector", lambda E, hf=half: E.tensor_scalar_mul(
                        out=d2b[:, 512 * hf:512 * hf + 512],
                        in0=(psA if hf == 0 else psB)[:], scalar1=oscd))
                res7 = op("vector", "sV", lambda E: E.tensor_add(
                    out=lnb[:], in0=d2b, in1=xres))
                opq("vector", lambda E: E.reduce_sum(out=mn_col, in_=lnb[:], axis=AxX))
                opq("vector", lambda E: E.tensor_scalar_mul(
                    out=mn_col, in0=mn_col, scalar1=1.0 / D))
                opq("vector", lambda E: E.tensor_single_scalar(
                    out=d2b, in_=lnb[:], scalar=mn_col, op=Alu.subtract))
                opq("vector", lambda E: E.scalar_tensor_tensor(
                    out=lnb[:], in0=d2b, scalar=1.0, in1=d2b,
                    op0=Alu.mult, op1=Alu.mult, accum_out=vr_col))
                vv = op("vector", "sV", lambda E: E.tensor_scalar(
                    out=vr_col, in0=vr_col, scalar1=1.0 / D, scalar2=1e-5,
                    op0=Alu.mult, op1=Alu.add))
                w("gpsimd", "sV", vv)
                gv = op("gpsimd", "sGE", lambda E: E.tensor_tensor(
                    out=rs_col, in0=vr_col, in1=cbc[:, CI_PEXP:CI_PEXP + 1],
                    op=Alu.pow))
                w("vector", "sGE", gv)
                ov = op("vector", "sV", lambda E: E.tensor_scalar_mul(
                    out=lnb[:], in0=d2b, scalar1=rs_col))
                w("sync", "sV", ov)
                dma("sync", out_tok[128 * t:128 * (t + 1), :], lnb[:], sem="sO")
            w("sync", "sO", C.get("sO"))

        blk.sync(lambda E: program(E, "sync"))
        blk.vector(lambda E: program(E, "vector"))
        blk.scalar(lambda E: program(E, "scalar"))
        blk.tensor(lambda E: program(E, "tensor"))
        blk.gpsimd(lambda E: program(E, "gpsimd"))
    return nc


def _host_consts():
    kk = np.arange(128)[:, None]
    qq = np.arange(512)[None, :]
    maskbig = (qq - kk).astype(ml_dtypes.bfloat16)
    ident = np.eye(128, dtype=ml_dtypes.bfloat16)
    p = np.arange(128)[:, None]
    m = np.arange(128)[None, :]
    iot = (m - 64 * (p // 64)).astype(ml_dtypes.bfloat16)
    return maskbig, ident, iot


def build_shared(L0_attn_Wqkv_w, L0_attn_Wqkv_b, L0_attn_out_w,
                 L0_attn_out_b, L0_attn_W_static, L0_attn_Lambda_raw,
                 L0_attn_Gamma, L0_attn_gate_w1, L0_attn_gate_b1,
                 L0_attn_gate_w2, L0_attn_gate_b2, L0_ffn_w_gate,
                 L0_ffn_w_up, L0_ffn_w_down):
    maskbig, ident, iot = _host_consts()
    shared = {
        "wqkv": np.ascontiguousarray(L0_attn_Wqkv_w, dtype=np.float32),
        "bqkv": np.ascontiguousarray(L0_attn_Wqkv_b, dtype=np.float32),
        "outw": np.ascontiguousarray(L0_attn_out_w, dtype=np.float32),
        "outb": np.ascontiguousarray(L0_attn_out_b, dtype=np.float32),
        "wst": np.ascontiguousarray(L0_attn_W_static, dtype=np.float32),
        "lam": np.ascontiguousarray(L0_attn_Lambda_raw, dtype=np.float32),
        "gam": np.ascontiguousarray(L0_attn_Gamma, dtype=np.float32),
        "gw1": np.ascontiguousarray(L0_attn_gate_w1, dtype=np.float32),
        "gb1": np.ascontiguousarray(L0_attn_gate_b1, dtype=np.float32),
        "gw2": np.ascontiguousarray(L0_attn_gate_w2, dtype=np.float32),
        "gb2": np.ascontiguousarray(L0_attn_gate_b2, dtype=np.float32),
        "wgate": np.ascontiguousarray(L0_ffn_w_gate, dtype=np.float32),
        "wup": np.ascontiguousarray(L0_ffn_w_up, dtype=np.float32),
        "wdown": np.ascontiguousarray(L0_ffn_w_down, dtype=np.float32),
        "maskc": maskbig, "identc": ident, "ioc": iot,
        "identf": np.eye(128, dtype=np.float32),
        "cvec": CONSTS.reshape(1, 8).copy(),
        "bqkv_r": np.ascontiguousarray(
            np.asarray(L0_attn_Wqkv_b, dtype=np.float32).reshape(24, 128).T),
        "gb1_r": np.ascontiguousarray(
            np.asarray(L0_attn_gate_b1, dtype=np.float32).reshape(4, 128).T),
        "lamT": np.ascontiguousarray(
            np.asarray(L0_attn_Lambda_raw, dtype=np.float32)
            .transpose(2, 0, 1).reshape(64, 1024)),
        "gamT": np.ascontiguousarray(
            np.asarray(L0_attn_Gamma, dtype=np.float32)
            .transpose(2, 0, 1).reshape(64, 1024)),
        "wstT": np.ascontiguousarray(
            np.asarray(L0_attn_W_static, dtype=np.float32)
            .transpose(2, 0, 1).reshape(64, 1024)),
    }
    return shared


def kernel(L0_attn_x, L0_attn_Wqkv_w, L0_attn_Wqkv_b, L0_attn_out_w,
           L0_attn_out_b, L0_attn_W_static, L0_attn_Lambda_raw, L0_attn_Gamma,
           L0_attn_gate_w1, L0_attn_gate_b1, L0_attn_gate_w2, L0_attn_gate_b2,
           L0_ffn_w_gate, L0_ffn_w_up, L0_ffn_w_down, _n_cores=NCORES):
    shared = build_shared(
        L0_attn_Wqkv_w, L0_attn_Wqkv_b, L0_attn_out_w, L0_attn_out_b,
        L0_attn_W_static, L0_attn_Lambda_raw, L0_attn_Gamma, L0_attn_gate_w1,
        L0_attn_gate_b1, L0_attn_gate_w2, L0_attn_gate_b2, L0_ffn_w_gate,
        L0_ffn_w_up, L0_ffn_w_down)
    nc = build_nc()
    x = np.ascontiguousarray(np.asarray(L0_attn_x, dtype=np.float32))
    in_maps = []
    for c in range(_n_cores):
        m = dict(shared)
        m["x"] = np.ascontiguousarray(x[BPC * c:BPC * (c + 1)])
        in_maps.append(m)
    res = run_bass_kernel_spmd(nc, in_maps, core_ids=list(range(_n_cores)))
    return np.concatenate([r["out"] for r in res.results], axis=0)


if __name__ == "__main__":
    nc = build_nc()
    print("build ok; instructions:", nc.n_instructions())


# version 02
# speedup vs baseline: 1.0026x; 1.0026x over previous
import os
import numpy as np
import ml_dtypes
import concourse.bass as bass
import concourse.mybir as mybir
from concourse.bass_utils import run_bass_kernel_spmd
from contextlib import ExitStack

F32 = mybir.dt.float32
BF16 = mybir.dt.bfloat16
Alu = mybir.AluOpType
Act = mybir.ActivationFunctionType
AxX = mybir.AxisListType.X

KPH = int(os.environ.get("KPH", "8"))
KDBG = os.environ.get("KDBG", "0") == "1"

NCORES = 8
BPC = 2
L = 2048
D = 1024
H = 16
DH = 64
T = BPC * L
NTT = T // 128
G1 = 512
FI = 4096
NQT = L // 512
NKC = L // 128

LADDER = [128.0, 64.0, 32.0, 16.0, 8.0, 4.0, 2.0, 1.0]

CI_EXPB, CI_SGNB, CI_NHALF, CI_ONE, CI_PEXP = 0, 1, 2, 3, 4
CONSTS = np.zeros(8, np.float32)
CONSTS[CI_EXPB] = -10.0
CONSTS[CI_SGNB] = -128.0
CONSTS[CI_NHALF] = -0.5
CONSTS[CI_ONE] = 1.0
CONSTS[CI_PEXP] = -0.5


class Ctr:
    def __init__(self):
        self.n = {}
        self.m = {}

    def inc(self, name, k=1):
        self.n[name] = self.n.get(name, 0) + k
        return self.n[name]

    def get(self, name):
        return self.n.get(name, 0)


def build_nc():
    nc = bass.Bass("TRN2", target_bir_lowering=False)

    x_in = nc.dram_tensor("x", [BPC, L, D], F32, kind="ExternalInput").ap()
    wqkv = nc.dram_tensor("wqkv", [D, 3 * D], F32, kind="ExternalInput").ap()
    bqkv = nc.dram_tensor("bqkv", [3 * D], F32, kind="ExternalInput").ap()
    outw = nc.dram_tensor("outw", [D, D], F32, kind="ExternalInput").ap()
    outb = nc.dram_tensor("outb", [D], F32, kind="ExternalInput").ap()
    wst = nc.dram_tensor("wst", [H, DH, DH], F32, kind="ExternalInput").ap()
    lam = nc.dram_tensor("lam", [H, DH, DH], F32, kind="ExternalInput").ap()
    gam = nc.dram_tensor("gam", [H, DH, DH], F32, kind="ExternalInput").ap()
    gw1 = nc.dram_tensor("gw1", [D, G1], F32, kind="ExternalInput").ap()
    gb1 = nc.dram_tensor("gb1", [G1], F32, kind="ExternalInput").ap()
    gw2 = nc.dram_tensor("gw2", [G1, H], F32, kind="ExternalInput").ap()
    gb2 = nc.dram_tensor("gb2", [H], F32, kind="ExternalInput").ap()
    wgate = nc.dram_tensor("wgate", [FI, D], F32, kind="ExternalInput").ap()
    wup = nc.dram_tensor("wup", [FI, D], F32, kind="ExternalInput").ap()
    wdown = nc.dram_tensor("wdown", [D, FI], F32, kind="ExternalInput").ap()
    maskc = nc.dram_tensor("maskc", [128, 2048], BF16, kind="ExternalInput").ap()
    bqkv_r = nc.dram_tensor("bqkv_r", [128, 24], F32, kind="ExternalInput").ap()
    gb1_r = nc.dram_tensor("gb1_r", [128, 4], F32, kind="ExternalInput").ap()
    lamT = nc.dram_tensor("lamT", [64, 1024], F32, kind="ExternalInput").ap()
    gamT = nc.dram_tensor("gamT", [64, 1024], F32, kind="ExternalInput").ap()
    wstT = nc.dram_tensor("wstT", [64, 1024], F32, kind="ExternalInput").ap()
    ident_d = nc.dram_tensor("identc", [128, 128], BF16, kind="ExternalInput").ap()
    identf_d = nc.dram_tensor("identf", [128, 128], F32, kind="ExternalInput").ap()
    io_d = nc.dram_tensor("ioc", [128, 128], BF16, kind="ExternalInput").ap()
    cvec = nc.dram_tensor("cvec", [1, 8], F32, kind="ExternalInput").ap()
    out = nc.dram_tensor("out", [BPC, L, D], F32, kind="ExternalOutput").ap()

    _K = "ExternalOutput" if KDBG else "Internal"
    q_d = nc.dram_tensor("q_d", [BPC, H, DH, L], BF16, kind=_K).ap()
    k_d = nc.dram_tensor("k_d", [BPC, H, DH, L], BF16, kind=_K).ap()
    v_d = nc.dram_tensor("v_d", [BPC, H, DH, L], BF16, kind=_K).ap()
    vtm_d = nc.dram_tensor("vtm_d", [BPC, H, L, DH], BF16, kind=_K).ap()
    al_d = nc.dram_tensor("al_d", [H, T], BF16, kind=_K).ap()
    den_d = nc.dram_tensor("den_d", [BPC, H, L], F32, kind=_K).ap()
    x1_d = nc.dram_tensor("x1_d", [T, D], F32, kind=_K).ap()
    cc_d = nc.dram_tensor("cc_d", [2, 2048], BF16, kind=_K).ap()
    g2_d = nc.dram_tensor("g2_d", [T, FI], BF16, kind=_K).ap()
    wqg_d = nc.dram_tensor("wqg_d", [D, FI], BF16, kind=_K).ap()
    wqu_d = nc.dram_tensor("wqu_d", [D, FI], BF16, kind=_K).ap()
    wqd_d = nc.dram_tensor("wqd_d", [FI, D], BF16, kind=_K).ap()
    m_d = nc.dram_tensor("m_d", [1, 4], F32, kind=_K).ap()

    x_tok = x_in.rearrange("b l d -> (b l) d")
    out_tok = out.rearrange("b l d -> (b l) d")

    with ExitStack() as es:
        e = es.enter_context

        bigs = [e(nc.sbuf_tensor(f"big{i}", [128, 4096], BF16)) for i in range(8)]
        gbuf = [e(nc.sbuf_tensor(f"gwk{i}", [128, 4096], BF16)) for i in range(4)]
        far = e(nc.sbuf_tensor("far", [128, 8192], F32))
        dbuf = e(nc.sbuf_tensor("dbuf", [128, 4096], BF16))
        lnar = e(nc.sbuf_tensor("lnar", [128, 3072], F32))
        outw_sb = e(nc.sbuf_tensor("outw_sb", [128, 8192], BF16))
        ident = e(nc.sbuf_tensor("ident", [128, 128], BF16))
        identF = e(nc.sbuf_tensor("identF", [128, 128], F32))
        iot = e(nc.sbuf_tensor("iot", [128, 128], BF16))
        sel = [e(nc.sbuf_tensor(f"sel{i}", [128, 128], BF16)) for i in range(2)]
        r_all = e(nc.sbuf_tensor("r_all", [128, 1024], F32))
        g_all = e(nc.sbuf_tensor("g_all", [128, 1024], F32))
        wt2 = e(nc.sbuf_tensor("wt2", [128, 1024], BF16))
        wblk = e(nc.sbuf_tensor("wblk", [128, 128], BF16))
        outb_bc = e(nc.sbuf_tensor("outb_bc", [128, 1024], BF16))
        k12 = e(nc.sbuf_tensor("k12", [128, 2048], BF16))
        q12 = e(nc.sbuf_tensor("q12", [128, 2048], BF16))
        ysm = [e(nc.sbuf_tensor(f"ysm{i}", [64, 2048], BF16)) for i in range(2)]
        prb = [e(nc.sbuf_tensor(f"prb{i}", [128, 512], BF16)) for i in range(2)]
        v65 = [e(nc.sbuf_tensor(f"v65{i}", [128, 65], BF16)) for i in range(2)]
        al_bc = e(nc.sbuf_tensor("al_bc", [65, 2048], BF16))
        c2_bc = e(nc.sbuf_tensor("c2_bc", [65, 2048], BF16))
        cbc = e(nc.sbuf_tensor("cbc", [128, 8], F32))
        qb_col = e(nc.sbuf_tensor("qb_col", [128, 24], F32))
        b1c = e(nc.sbuf_tensor("b1c", [128, 4], F32))
        b2c = e(nc.sbuf_tensor("b2c", [16, 1], F32))
        mcols = e(nc.sbuf_tensor("mcols", [128, 8], F32))
        a16 = e(nc.sbuf_tensor("a16", [16, 512], BF16))
        amax2 = e(nc.sbuf_tensor("amax2", [128, 32], F32))
        oscg = e(nc.sbuf_tensor("oscg", [128, 32], F32))
        oscu = e(nc.sbuf_tensor("oscu", [128, 32], F32))
        colw = e(nc.sbuf_tensor("colw", [128, 16], F32))

        psA = e(nc.psum_tensor("psA", [128, 512], F32))
        psB = e(nc.psum_tensor("psB", [128, 512], F32))
        psY = e(nc.psum_tensor("psY", [128, 2048], F32))
        psAV = psY[:, 0:512]
        psTF = [e(nc.psum_tensor("psTF0", [128, 512], F32)),
                e(nc.psum_tensor("psTF1", [128, 512], F32))]

        SEM = {}
        SYNC_POOL = [f"sl{i}" for i in range(12)]
        GP_POOL = [f"sg{i}" for i in range(4)]
        for s in SYNC_POOL + GP_POOL + ["sV", "sS", "sT", "sGE", "sO"]:
            SEM[s] = e(nc.semaphore(s))
        POOLS = {"sync": SYNC_POOL, "gpsimd": GP_POOL}

        blk = e(nc.Block())

        def program(eng, role):
            C = Ctr()
            M = C.m

            def dma(q, outap, inap, sem=None, transpose=False):
                if sem == "sO":
                    v = C.inc("sO", 16)
                    if role == q:
                        eng.dma_start(out=outap, in_=inap).then_inc(SEM["sO"], 16)
                    return v
                pool = POOLS[q]
                idx = C.inc(f"_rr_{q}") % len(pool)
                sm = pool[idx]
                prev = C.get(sm)
                v = C.inc(sm, 16)
                if role == q:
                    if prev > 0:
                        eng.wait_ge(SEM[sm], prev)
                    if transpose:
                        eng.dma_start_transpose(out=outap, in_=inap).then_inc(SEM[sm], 16)
                    else:
                        eng.dma_start(out=outap, in_=inap).then_inc(SEM[sm], 16)
                return (sm, v)

            def snapd():
                return tuple((sm, C.get(sm))
                             for sm in SYNC_POOL + GP_POOL if C.get(sm) > 0)

            def op(r, sem, fn):
                v = C.inc(sem, 1)
                if role == r:
                    fn(eng).then_inc(SEM[sem], 1)
                    if r in ("vector", "gpsimd"):
                        eng.wait_ge(SEM[sem], v)
                return v

            def opq(r, fn):
                if r in ("vector", "gpsimd"):
                    sem = "sV" if r == "vector" else "sGE"
                    v = C.inc(sem, 1)
                    if role == r:
                        fn(eng).then_inc(SEM[sem], 1)
                        eng.wait_ge(SEM[sem], v)
                else:
                    if role == r:
                        fn(eng)

            def w(r, sem, v=None):
                if role != r:
                    return
                if v is None:
                    mk = sem
                    if not mk:
                        return
                    if isinstance(mk, tuple) and mk and isinstance(mk[0], tuple):
                        for sm, vv in mk:
                            eng.wait_ge(SEM[sm], vv)
                    elif isinstance(mk, tuple):
                        eng.wait_ge(SEM[mk[0]], mk[1])
                else:
                    if v and v > 0:
                        eng.wait_ge(SEM[sem], v)

            def fin():
                if role == "sync":
                    for sm in SYNC_POOL:
                        if C.get(sm) > 0:
                            eng.wait_ge(SEM[sm], C.get(sm))
                    if C.get("sO") > 0:
                        eng.wait_ge(SEM["sO"], C.get("sO"))
                if role == "gpsimd":
                    for sm in GP_POOL:
                        if C.get(sm) > 0:
                            eng.wait_ge(SEM[sm], C.get(sm))

            tp_ev = {0: 0, 1: 0}

            dma("sync", gbuf[0][:, 0:2048], maskc)
            dma("sync", ident[:], ident_d)
            dma("sync", identF[:], identf_d)
            dma("sync", iot[:], io_d)
            dma("sync", cbc[:], cvec.broadcast_to([128, 8]))
            dma("sync", qb_col[:], bqkv_r)
            dma("sync", b1c[:], gb1_r)
            dma("sync", b2c[:], gb2.unsqueeze(1))
            M["outb"] = dma("gpsimd", outb_bc[:], outb.unsqueeze(0).broadcast_to([128, 1024]))
            dma("sync", r_all[0:64, :], lamT)
            dma("sync", r_all[64:128, :], lamT)
            dma("sync", g_all[0:64, :], gamT)
            dma("sync", g_all[64:128, :], gamT)
            dma("sync", lnar[0:64, 0:1024], wstT)
            dma("sync", lnar[64:128, 0:1024], wstT)
            M["p0"] = snapd()
            w("scalar", M["p0"])
            vsg = op("scalar", "sS", lambda E: E.activation(
                out=r_all[:], in_=r_all[:], func=Act.Sigmoid))
            w("vector", "sS", vsg)
            opq("vector", lambda E: E.tensor_scalar(
                out=r_all[:], in0=r_all[:], scalar1=-1.0, scalar2=1.0,
                op0=Alu.mult, op1=Alu.add))
            opq("vector", lambda E: E.tensor_scalar_mul(
                out=g_all[:], in0=g_all[:], scalar1=0.125))
            w("vector", M["p0"])
            M["rg"] = op("vector", "sV", lambda E: E.tensor_copy(
                out=wt2[:], in_=lnar[:, 0:1024]))
            if KPH <= 0:
                fin(); return

            wtile = far[:, 0:4096]
            p1b = gbuf[1]
            n1b = gbuf[2]
            wqbF = far[:, 4096:8192]
            acc_col = colw[:, 0:1]
            tmp_col = colw[:, 1:2]

            KWS = int(os.environ.get("KWS", "3"))
            def wq_phase(widx, w_ap, wq_ap_d, nrows, ncols):
                nt = nrows // 128
                nel = nrows * ncols
                opq("vector", lambda E: E.memset(acc_col, 0.0))
                last_rd = 0
                for t in range(nt):
                    if t >= 1:
                        w("sync", "sV", M[f"wqa{widx}_{t-1}"])
                    lv = dma("sync", wtile[:, 0:ncols], w_ap[128 * t:128 * (t + 1), :])
                    w("vector", lv)
                    opq("vector", lambda E: E.reduce_sum(
                        out=tmp_col, in_=wtile[:, 0:ncols], axis=AxX,
                        apply_absolute_value=True))
                    last_rd = op("vector", "sV", lambda E: E.tensor_add(
                        out=acc_col, in0=acc_col, in1=tmp_col))
                    M[f"wqa{widx}_{t}"] = last_rd
                    if KDBG and widx == 0:
                        w("sync", "sV", last_rd)
                        snp = dma("sync", den_d[1, 1 + t // 16, (t % 16) * 128:
                                  (t % 16) * 128 + 128], tmp_col)
                        w("vector", snp)
                if KWS <= 1:
                    return
                w("tensor", "sV", last_rd)
                w("tensor", M["p0"])
                if widx > 0:
                    w("tensor", "sV", M[f"wqm{widx-1}"])
                tv = op("tensor", "sT", lambda E: E.matmul(
                    out=psA[0:1, 0:1], lhsT=acc_col, rhs=cbc[:, CI_ONE:CI_ONE + 1],
                    start=True, stop=True))
                w("vector", "sT", tv)
                mv = op("vector", "sV", lambda E: E.tensor_scalar(
                    out=tmp_col[0:1, :], in0=psA[0:1, 0:1],
                    scalar1=1.0 / nel, scalar2=1e-5, op0=Alu.mult, op1=Alu.max))
                M[f"wqm{widx}"] = mv
                w("sync", "sV", mv)
                stm = dma("sync", m_d[0:1, widx:widx + 1], tmp_col[0:1, :])
                w("sync", stm)
                lv = dma("sync", mcols[:, widx:widx + 1],
                         m_d[0:1, widx:widx + 1].broadcast_to([128, 1]))
                w("vector", lv)
                iv = op("vector", "sV", lambda E: E.reciprocal(
                    out=mcols[:, 4 + widx:5 + widx], in_=mcols[:, widx:widx + 1]))
                if KDBG and widx == 0:
                    dma("sync", den_d[0, 0, 0:128], acc_col)
                    dma("sync", den_d[0, 0, 128:256], cbc[:, CI_ONE:CI_ONE + 1])
                if KWS <= 2:
                    return
                nblk = ncols // 128
                KWT = int(os.environ.get("KWT", "99"))
                for t in range(nt):
                    if t >= KWT:
                        continue
                    if t >= 1:
                        w("sync", "sV", M[f"wqv{widx}_{t-1}"])
                    lv = dma("sync", wtile[:, 0:ncols], w_ap[128 * t:128 * (t + 1), :])
                    w("vector", lv)
                    w("vector", "sV", iv)
                    if t >= 1:
                        w("vector", "sT", M[f"wqt{widx}_{t-1}"])
                    opq("vector", lambda E: E.tensor_scalar(
                        out=p1b[:, 0:ncols], in0=wtile[:, 0:ncols],
                        scalar1=mcols[:, 4 + widx:5 + widx], scalar2=0.5,
                        op0=Alu.mult, op1=Alu.is_gt))
                    opq("vector", lambda E: E.tensor_scalar(
                        out=n1b[:, 0:ncols], in0=wtile[:, 0:ncols],
                        scalar1=mcols[:, 4 + widx:5 + widx], scalar2=-0.5,
                        op0=Alu.mult, op1=Alu.is_lt))
                    vv = op("vector", "sV", lambda E: E.tensor_sub(
                        out=wqbF[:, 0:ncols], in0=p1b[:, 0:ncols], in1=n1b[:, 0:ncols]))
                    M[f"wqv{widx}_{t}"] = vv
                    w("tensor", "sV", vv)
                    for cb in range(nblk):
                        pp = cb % 2
                        w("tensor", "sS", tp_ev[pp])
                        tv = op("tensor", "sT", lambda E, c=cb, p=pp: E.transpose(
                            out=psTF[p][:, 0:128],
                            in_=wqbF[:, 128 * c:128 * c + 128], identity=identF[:]))
                        w("scalar", "sT", tv)
                        if cb == 0:
                            prev = (M.get(f"wqs{widx}_{t-1}") if t >= 1
                                    else M.get(f"wqs{widx-1}_{(FI if widx <= 2 else D)//128 - 1}")
                                    if widx >= 1 else None)
                            w("scalar", prev)
                        tp_ev[pp] = op(
                            "scalar", "sS", lambda E, c=cb, p=pp: E.copy(
                                out=dbuf[:, 128 * c:128 * c + 128],
                                in_=psTF[p][:, 0:128]))
                    M[f"wqt{widx}_{t}"] = C.get("sT")
                    M[f"wqe{widx}_{t}"] = C.get("sS")
                    w("sync", "sS", M[f"wqe{widx}_{t}"])
                    if os.environ.get("KWX", "0") != "1":
                        for cb in range(nblk):
                            dma("sync", wq_ap_d[128 * cb:128 * cb + 128,
                                                128 * t:128 * (t + 1)],
                                dbuf[:, 128 * cb:128 * cb + 128])
                    M[f"wqs{widx}_{t}"] = snapd()
                return

            def wq_phase_guarded(widx, w_ap, wq_ap_d, nrows, ncols):
                wq_phase(widx, w_ap, wq_ap_d, nrows, ncols)

            KW = int(os.environ.get("KW", "3"))
            if KW >= 1:
                wq_phase_guarded(0, wgate, wqg_d, FI, D)
            if KW >= 2:
                wq_phase_guarded(1, wup, wqu_d, FI, D)
            if KW >= 3:
                wq_phase_guarded(2, wdown, wqd_d, D, FI)
            if KW < 3:
                fin(); return
            M["p1_S"] = C.get("sS")
            M["p1_T"] = C.get("sT")
            M["p1_V"] = C.get("sV")
            if KPH <= 1:
                fin(); return

            xt_f = lnar[:, 0:1024]
            for t in range(NTT):
                if t >= 1:
                    w("sync", "sT", M[f"xft{t-1}"])
                else:
                    w("sync", "sV", M["rg"])
                lv = dma("sync", xt_f, x_tok[128 * t:128 * (t + 1), :])
                w("tensor", lv)
                for c in range(8):
                    pp = c % 2
                    w("tensor", "sS", tp_ev[pp])
                    tv = op("tensor", "sT", lambda E, cc=c, p=pp: E.transpose(
                        out=psTF[p][:, 0:128],
                        in_=xt_f[:, 128 * cc:128 * cc + 128], identity=identF[:]))
                    w("scalar", "sT", tv)
                    tp_ev[pp] = op("scalar", "sS", lambda E, cc=c, tt=t, p=pp: E.copy(
                        out=bigs[cc][:, 128 * tt:128 * tt + 128],
                        in_=psTF[p][:, 0:128]))
                M[f"xft{t}"] = C.get("sT")
            M["xfm"] = C.get("sS")
            if KPH <= 2:
                fin(); return

            wqv_f32 = [lnar[:, 1024:2048], lnar[:, 2048:3072]]
            wqv_b16 = [gbuf[1][:, 0:1024], gbuf[1][:, 1024:2048]]
            qkv_sb = [gbuf[2][:, 0:512], gbuf[2][:, 512:1024]]
            vtm_sb = [gbuf[2][:, 1024:1152], gbuf[2][:, 1152:1280]]
            ev_bank = {0: 0, 1: 0}
            st_mark = {0: 0, 1: 0}
            vt_mark = {0: 0, 1: 0}
            vstage_use = M[f"xft{NTT-1}"]
            for f in range(24):
                par = f % 2
                if f >= 2:
                    w("sync", "sV", M[f"qkvw{f-2}"])
                lv = dma("sync",
                         wqv_f32[par].rearrange("p (c o) -> p c o", c=8),
                         wqkv[:, 128 * f:128 * (f + 1)]
                         .rearrange("(c p) o -> p c o", p=128))
                w("vector", lv)
                cv = op("vector", "sV", lambda E, p=par: E.tensor_copy(
                    out=wqv_b16[p][:], in_=wqv_f32[p][:]))
                M[f"qkvw{f}"] = cv
                w("tensor", "sV", cv)
                if f == 0:
                    w("tensor", "sS", M["xfm"])
                    w("tensor", "sV", M["wqm2"])
                for tt in range(8):
                    bk = tt % 2
                    w("tensor", "sS", ev_bank[bk])
                    for kc in range(8):
                        op("tensor", "sT", lambda E, k=kc, b2=bk, p=par, t8=tt: E.matmul(
                            out=(psA if b2 == 0 else psB)[:],
                            lhsT=wqv_b16[p][:, 128 * k:128 * (k + 1)],
                            rhs=bigs[k][:, 512 * t8:512 * (t8 + 1)],
                            start=(k == 0), stop=(k == 7)))
                    tv = C.get("sT")
                    w("scalar", "sT", tv)
                    w("scalar", st_mark[bk])
                    sv = op("scalar", "sS", lambda E, b2=bk, ff=f: E.activation(
                        out=qkv_sb[b2][:], in_=(psA if b2 == 0 else psB)[:],
                        func=Act.Identity, bias=qb_col[:, ff:ff + 1], scale=1.0))
                    ev_bank[bk] = sv
                    w("sync", "sS", sv)
                    bidx = tt // 4
                    ts0 = (tt % 4) * 512
                    dst = (q_d if f < 8 else (k_d if f < 16 else v_d))
                    hh = 2 * (f % 8)
                    st_mark[bk] = dma(
                        "sync", dst[bidx, hh:hh + 2, :, ts0:ts0 + 512]
                        .rearrange("h d t -> (h d) t"), qkv_sb[bk][:])
                    if f >= 16:
                        w("scalar", "sT", vstage_use)
                        sv2 = op("scalar", "sS", lambda E, b2=bk, ff=f: E.activation(
                            out=lnar[:, 0:512], in_=(psA if b2 == 0 else psB)[:],
                            func=Act.Identity, bias=qb_col[:, ff:ff + 1], scale=1.0))
                        ev_bank[bk] = sv2
                        w("tensor", "sS", sv2)
                        for c4 in range(4):
                            sl = c4 % 2
                            w("tensor", "sS", tp_ev[sl])
                            tv = op("tensor", "sT", lambda E, c=c4, s=sl: E.transpose(
                                out=psTF[s][:, 0:128],
                                in_=lnar[:, 128 * c:128 * c + 128],
                                identity=identF[:]))
                            w("scalar", "sT", tv)
                            w("scalar", vt_mark[sl])
                            sv = op("scalar", "sS", lambda E, s=sl: E.copy(
                                out=vtm_sb[s][:], in_=psTF[s][:, 0:128]))
                            tp_ev[sl] = sv
                            w("sync", "sS", sv)
                            tb = ts0 + 128 * c4
                            vt_mark[sl] = (
                                dma("sync", vtm_d[bidx, hh, tb:tb + 128, :],
                                    vtm_sb[sl][:, 0:64]),
                                dma("sync", vtm_d[bidx, hh + 1, tb:tb + 128, :],
                                    vtm_sb[sl][:, 64:128]))
                        vstage_use = C.get("sT")
            M["qkv_L"] = snapd()
            M["qkv_T"] = C.get("sT")
            M["qkv_S"] = C.get("sS")
            if KPH <= 3:
                fin(); return

            for c in range(4):
                if c >= 1:
                    w("sync", "sV", M[f"gw1c{c-1}"])
                lv = dma("sync",
                         lnar[:, 1024:2048].rearrange("p (k o) -> p k o", k=8),
                         gw1[:, 128 * c:128 * (c + 1)]
                         .rearrange("(k p) o -> p k o", p=128))
                w("vector", lv)
                M[f"gw1c{c}"] = op("vector", "sV", lambda E, cc=c: E.tensor_copy(
                    out=gbuf[3][:, 1024 * cc:1024 * (cc + 1)], in_=lnar[:, 1024:2048]))
            w("sync", "sV", M["gw1c3"])
            lv = dma("sync",
                     lnar[:, 1024:1088].rearrange("p (c h) -> p c h", c=4),
                     gw2.rearrange("(c p) h -> p c h", p=128))
            w("vector", lv)
            M["gw2"] = op("vector", "sV", lambda E: E.tensor_copy(
                out=k12[:, 0:64], in_=lnar[:, 1024:1088]))
            gtile = gbuf[1][:, 2048:2560]
            av_ev = 0
            gt_use = 0
            al_st = None
            for tt in range(8):
                for c in range(4):
                    bk = c % 2
                    w("tensor", "sV", M[f"gw1c{c}"])
                    w("tensor", "sV", M["gw2"])
                    w("tensor", "sS", ev_bank[bk])
                    for kc in range(8):
                        op("tensor", "sT", lambda E, k=kc, b2=bk, cc=c, t8=tt: E.matmul(
                            out=(psA if b2 == 0 else psB)[:],
                            lhsT=gbuf[3][:, 1024 * cc + 128 * k:1024 * cc + 128 * (k + 1)],
                            rhs=bigs[k][:, 512 * t8:512 * (t8 + 1)],
                            start=(k == 0), stop=(k == 7)))
                    tv = C.get("sT")
                    w("scalar", "sT", tv)
                    w("scalar", "sT", gt_use)
                    sv = op("scalar", "sS", lambda E, b2=bk, cc=c: E.activation(
                        out=gtile[:], in_=(psA if b2 == 0 else psB)[:],
                        func=Act.Gelu, bias=b1c[:, cc:cc + 1], scale=1.0))
                    ev_bank[bk] = sv
                    w("tensor", "sS", sv)
                    if c == 0:
                        w("tensor", "sS", av_ev)
                    op("tensor", "sT", lambda E, cc=c: E.matmul(
                        out=psAV[0:16, :], lhsT=k12[:, 16 * cc:16 * (cc + 1)],
                        rhs=gtile[:], start=(cc == 0), stop=(cc == 3)))
                    gt_use = C.get("sT")
                w("scalar", "sT", gt_use)
                w("scalar", al_st)
                av_ev = op("scalar", "sS", lambda E: E.activation(
                    out=a16[:, 0:512], in_=psAV[0:16, :], func=Act.Sigmoid,
                    bias=b2c[:], scale=1.0))
                w("sync", "sS", av_ev)
                al_st = dma("sync", al_d[:, 512 * tt:512 * (tt + 1)], a16[:, 0:512])
            M["alpha_L"] = snapd()
            M["p3_psAB"] = max(ev_bank[0], ev_bank[1])
            M["p3_psAV"] = av_ev
            M["p3_gtuse"] = gt_use
            if KPH <= 4:
                fin(); return

            last_kq_use = M["p3_gtuse"]
            prb_use = {0: 0, 1: 0}
            v65_use = {0: 0, 1: 0}
            sc_ev = {0: M["p3_psAB"], 1: M["p3_psAB"]}
            av_done = 0
            first_av = True
            blend_done = 0
            den_st_last = 0
            u_mark = {0: 0, 1: 0}
            red_mark = {0: 0, 1: 0}
            w("sync", M["qkv_L"])
            w("vector", M["qkv_L"])
            KH = int(os.environ.get("KH", str(H)))
            for h in range(KH):
                w("sync", "sT", last_kq_use)
                kqm = []
                for b in range(BPC):
                    kqm.append(dma("sync", k12[64 * b:64 * b + 64, :], k_d[b, h, :, :]))
                    kqm.append(dma("sync", q12[64 * b:64 * b + 64, :], q_d[b, h, :, :]))
                M[f"h{h}kq"] = tuple(kqm)
                w("vector", lv)
                opq("vector", lambda E: E.memset(wblk[:], 0.0))
                opq("vector", lambda E, hh=h: E.tensor_copy(
                    out=wblk[0:64, 0:64], in_=wt2[0:64, 64 * hh:64 * hh + 64]))
                wbv = op("vector", "sV", lambda E, hh=h: E.tensor_copy(
                    out=wblk[64:128, 64:128], in_=wt2[64:128, 64 * hh:64 * hh + 64]))
                if h == 0:
                    opq("vector", lambda E: E.memset(v65[0][:, 64:65], 1.0))
                    opq("vector", lambda E: E.memset(v65[1][:, 64:65], 1.0))

                for b in range(BPC):
                    for qt in range(NQT):
                        av_mm = 0
                        for kc in range(4 * qt + 4):
                            pp = kc % 2
                            w("tensor", "sS", sc_ev[pp])
                            w("tensor", M[f"h{h}kq"])
                            op("tensor", "sT", lambda E, k=kc, q=qt, bb=b, p=pp: E.matmul(
                                out=(psA if p == 0 else psB)[:],
                                lhsT=k12[64 * bb:64 * bb + 64, 128 * k:128 * (k + 1)],
                                rhs=q12[64 * bb:64 * bb + 64, 512 * q:512 * (q + 1)],
                                start=True, stop=True))
                            tv = C.get("sT")
                            w("scalar", "sT", tv)
                            w("scalar", "sT", prb_use[pp])
                            sv = op("scalar", "sS", lambda E, p=pp: E.activation(
                                out=prb[p][:], in_=(psA if p == 0 else psB)[:],
                                func=Act.Exp, bias=cbc[:, CI_EXPB:CI_EXPB + 1],
                                scale=0.125))
                            sc_ev[pp] = sv
                            mask_v = 0
                            if kc >= 4 * qt:
                                r = 128 * kc - 512 * qt
                                w("vector", "sS", sv)
                                mask_v = op("vector", "sV", lambda E, p=pp, rr=r: E.tensor_mul(
                                    out=prb[p][:], in0=prb[p][:],
                                    in1=gbuf[0][:, 512 * (rr // 128):512 * (rr // 128) + 512]))
                            w("sync", "sT", v65_use[pp])
                            lv = dma("sync", v65[pp][:, 0:64],
                                     vtm_d[b, h, 128 * kc:128 * (kc + 1), :])
                            w("tensor", lv)
                            if mask_v:
                                w("tensor", "sV", mask_v)
                            else:
                                w("tensor", "sS", sv)
                            if kc == 0:
                                if first_av:
                                    w("tensor", "sS", M["p3_psAV"])
                                    first_av = False
                                w("tensor", "sV", av_done)
                            op("tensor", "sT", lambda E, p=pp, k=kc, q=qt: E.matmul(
                                out=psAV[0:65, :], lhsT=v65[p][:], rhs=prb[p][:],
                                start=(k == 0), stop=(k == 4 * q + 3)))
                            av_mm = C.get("sT")
                            prb_use[pp] = av_mm
                            v65_use[pp] = av_mm
                        w("vector", "sT", av_mm)
                        if qt == 0:
                            w("vector", den_st_last)
                        opq("vector", lambda E, bb=b, q=qt: E.tensor_copy(
                            out=ysm[bb][:, 512 * q:512 * (q + 1)], in_=psAV[0:64, :]))
                        av_done = op("vector", "sV", lambda E, q=qt: E.tensor_copy(
                            out=lnar[64:65, 512 * q:512 * (q + 1)], in_=psAV[64:65, :]))
                    w("sync", "sV", av_done)
                    den_st_last = dma("sync", den_d[b, h, :], lnar[64:65, 0:2048])
                    M[f"h{h}den{b}"] = den_st_last

                w("tensor", "sV", wbv)
                w("tensor", "sV", blend_done)
                w("tensor", "sV", av_done)
                for qt4 in range(4):
                    op("tensor", "sT", lambda E, q=qt4: E.matmul(
                        out=psY[:, 512 * q:512 * (q + 1)], lhsT=wblk[:],
                        rhs=q12[:, 512 * q:512 * (q + 1)], start=True, stop=False))
                u_f = [far[:, 0:2048], far[:, 2048:4096]]
                z_f = [far[:, 4096:6144], far[:, 6144:8192]]
                vrep = [gbuf[1][:, 0:2048], gbuf[1][:, 2048:4096]]
                wbf = [gbuf[2][:, 0:2048], gbuf[2][:, 2048:4096]]
                for i in range(DH):
                    par = i % 2
                    qname = "sync" if par == 0 else "gpsimd"
                    if h == 0 and i < 2:
                        w(qname, "sT", M["p3_gtuse"])
                    w(qname, "sV", u_mark[par])
                    vmks = tuple(
                        dma(qname, vrep[par][64 * b:64 * b + 64, :],
                            v_d[b, h, i, :].unsqueeze(0)
                            .broadcast_to([64, 2048])) for b in range(BPC))
                    w("vector", "sT", red_mark[par])
                    opq("vector", lambda E, ii=i, p=par: E.tensor_single_scalar(
                        out=sel[p][:], in_=iot[:], scalar=float(ii), op=Alu.is_equal))
                    w("vector", vmks)
                    u_mark[par] = op("vector", "sV", lambda E, p=par: E.tensor_mul(
                        out=u_f[p][:], in0=vrep[p][:], in1=k12[:]))
                    opq("vector", lambda E, p=par, hh=h, ii=i: E.tensor_tensor_scan(
                        out=z_f[p][:],
                        data0=r_all[:, 64 * hh + ii:64 * hh + ii + 1]
                        .broadcast_to([128, 2048]),
                        data1=u_f[p][:], initial=0.0, op0=Alu.mult, op1=Alu.add))
                    wv = op("vector", "sV", lambda E, p=par, hh=h, ii=i: E.scalar_tensor_tensor(
                        out=wbf[p][:], in0=z_f[p][:],
                        scalar=g_all[:, 64 * hh + ii:64 * hh + ii + 1],
                        in1=q12[:], op0=Alu.mult, op1=Alu.mult))
                    w("tensor", "sV", wv)
                    for qt4 in range(4):
                        op("tensor", "sT", lambda E, q=qt4, p=par, ii=i: E.matmul(
                            out=psY[:, 512 * q:512 * (q + 1)], lhsT=sel[p][:],
                            rhs=wbf[p][:, 512 * q:512 * (q + 1)],
                            start=False, stop=(ii == DH - 1)))
                    red_mark[par] = C.get("sT")
                    last_kq_use = red_mark[par]
                M[f"h{h}scan"] = C.get("sT")

                for b in range(BPC):
                    w("sync", M["alpha_L"])
                    w("sync", M[f"h{h}den{b}"])
                    if blend_done:
                        w("sync", "sV", blend_done)
                    lv2 = dma("sync", al_bc[64:65, :],
                              al_d[h:h + 1, 2048 * b:2048 * (b + 1)])
                    lv2b = dma("sync", lnar[64:65, 0:2048],
                               den_d[b:b + 1, h, :])
                    w("vector", lv2)
                    w("vector", lv2b)
                    opq("vector", lambda E: E.tensor_scalar(
                        out=c2_bc[64:65, :], in0=al_bc[64:65, :],
                        scalar1=-1.0, scalar2=1.0, op0=Alu.mult, op1=Alu.add))
                    opq("vector", lambda E: E.reciprocal(
                        out=lnar[64:65, 0:2048], in_=lnar[64:65, 0:2048]))
                    cro = op("vector", "sV", lambda E: E.tensor_mul(
                        out=al_bc[64:65, :], in0=al_bc[64:65, :],
                        in1=lnar[64:65, 0:2048]))
                    w("sync", "sV", cro)
                    bmk1 = dma("sync", cc_d[0:1, :], al_bc[64:65, :])
                    bmk = dma("sync", cc_d[1:2, :], c2_bc[64:65, :])
                    w("sync", bmk1)
                    w("sync", bmk)
                    lv3a = dma("sync", al_bc[0:64, :],
                               cc_d[0:1, :].broadcast_to([64, 2048]))
                    lv3 = dma("sync", c2_bc[0:64, :],
                              cc_d[1:2, :].broadcast_to([64, 2048]))
                    w("vector", lv3a)
                    w("vector", lv3)
                    w("vector", "sT", M[f"h{h}scan"])
                    opq("vector", lambda E, bb=b: E.tensor_mul(
                        out=al_bc[0:64, :], in0=ysm[bb][:], in1=al_bc[0:64, :]))
                    opq("vector", lambda E, bb=b: E.scalar_tensor_tensor(
                        out=c2_bc[0:64, :], in0=psY[64 * bb:64 * bb + 64, :],
                        scalar=1.0, in1=c2_bc[0:64, :], op0=Alu.mult, op1=Alu.mult))
                    rows0 = (h % 2) * 64
                    blend_done = op("vector", "sV", lambda E, bb=b, r0=rows0, hh=h: E.tensor_add(
                        out=bigs[hh // 2][r0:r0 + 64, 2048 * bb:2048 * (bb + 1)],
                        in0=al_bc[0:64, :], in1=c2_bc[0:64, :]))
            M["p4_V"] = C.get("sV")
            M["p4_psAB"] = max(sc_ev[0], sc_ev[1])
            if KPH <= 5:
                if KDBG:
                    w("sync", "sV", M["p4_V"])
                    for c in range(8):
                        dma("sync", g2_d[128 * c:128 * (c + 1), :], bigs[c][:])
                fin(); return

            w("sync", den_st_last)
            for kk in range(8):
                if kk >= 1:
                    w("sync", "sV", M[f"ow{kk-1}"])
                lv = dma("sync", lnar[:, 0:1024], outw[128 * kk:128 * (kk + 1), :])
                w("vector", lv)
                w("vector", "sV", M["p4_V"])
                M[f"ow{kk}"] = op("vector", "sV", lambda E, k=kk: E.tensor_copy(
                    out=outw_sb[:, 1024 * k:1024 * (k + 1)], in_=lnar[:, 0:1024]))
            xres = lnar[:, 0:1024]
            lnb = lnar[:, 1024:2048]
            d2b = lnar[:, 2048:3072]
            u0b = far[:, 0:1024]
            uLb = far[:, 1024:2048]
            xqF = far[:, 2048:3072]
            s_col = colw[:, 2:3]
            am_col = colw[:, 3:4]
            mn_col = colw[:, 4:5]
            vr_col = colw[:, 5:6]
            rs_col = colw[:, 6:7]
            res_done = 0
            x1st = None
            for t in range(NTT):
                w("tensor", "sV", M["ow7"])
                w("tensor", "sS", M["p4_psAB"])
                w("tensor", "sV", res_done)
                for half in range(2):
                    for kk in range(8):
                        op("tensor", "sT", lambda E, k=kk, hf=half, tt=t: E.matmul(
                            out=(psA if hf == 0 else psB)[:],
                            lhsT=bigs[k][:, 128 * tt:128 * (tt + 1)],
                            rhs=outw_sb[:, 1024 * k + 512 * hf:1024 * k + 512 * hf + 512],
                            start=(k == 0), stop=(k == 7)))
                mmv = C.get("sT")
                M[f"p5mm{t}"] = mmv
                if t == 0:
                    w("sync", "sV", M["ow7"])
                lv = dma("sync", xres, x_tok[128 * t:128 * (t + 1), :])
                w("vector", "sT", mmv)
                w("vector", lv)
                w("vector", x1st)
                for half in range(2):
                    opq("vector", lambda E, hf=half: E.scalar_tensor_tensor(
                        out=lnb[:, 512 * hf:512 * hf + 512],
                        in0=(psA if hf == 0 else psB)[:], scalar=1.0,
                        in1=xres[:, 512 * hf:512 * hf + 512],
                        op0=Alu.mult, op1=Alu.add))
                if t == 0:
                    w("vector", M["outb"])
                res_done = op("vector", "sV", lambda E: E.tensor_add(
                    out=lnb[:], in0=lnb[:], in1=outb_bc[:]))
                opq("vector", lambda E: E.reduce_sum(out=mn_col, in_=lnb[:], axis=AxX))
                opq("vector", lambda E: E.tensor_scalar_mul(
                    out=mn_col, in0=mn_col, scalar1=1.0 / D))
                opq("vector", lambda E: E.tensor_single_scalar(
                    out=d2b, in_=lnb[:], scalar=mn_col, op=Alu.subtract))
                opq("vector", lambda E: E.scalar_tensor_tensor(
                    out=lnb[:], in0=d2b, scalar=1.0, in1=d2b,
                    op0=Alu.mult, op1=Alu.mult, accum_out=vr_col))
                vv = op("vector", "sV", lambda E: E.tensor_scalar(
                    out=vr_col, in0=vr_col, scalar1=1.0 / D, scalar2=1e-5,
                    op0=Alu.mult, op1=Alu.add))
                w("gpsimd", "sV", vv)
                gv = op("gpsimd", "sGE", lambda E: E.tensor_tensor(
                    out=rs_col, in0=vr_col, in1=cbc[:, CI_PEXP:CI_PEXP + 1],
                    op=Alu.pow))
                w("vector", "sGE", gv)
                x1v = op("vector", "sV", lambda E: E.tensor_scalar_mul(
                    out=lnb[:], in0=d2b, scalar1=rs_col))
                w("sync", "sV", x1v)
                x1st = dma("sync", x1_d[128 * t:128 * (t + 1), :], lnb[:])
                opq("vector", lambda E: E.reduce_max(
                    out=am_col, in_=lnb[:], axis=AxX, apply_absolute_value=True))
                opq("vector", lambda E: E.tensor_scalar_max(
                    out=am_col, in0=am_col, scalar1=1e-5))
                opq("vector", lambda E: E.reciprocal(out=s_col, in_=am_col))
                opq("vector", lambda E: E.tensor_scalar_mul(
                    out=s_col, in0=s_col, scalar1=127.0))
                opq("vector", lambda E, tt=t: E.scalar_tensor_tensor(
                    out=oscg[:, tt:tt + 1], in0=am_col, scalar=1.0 / 127.0,
                    in1=mcols[:, 0:1], op0=Alu.mult, op1=Alu.mult))
                opq("vector", lambda E, tt=t: E.scalar_tensor_tensor(
                    out=oscu[:, tt:tt + 1], in0=am_col, scalar=1.0 / 127.0,
                    in1=mcols[:, 1:2], op0=Alu.mult, op1=Alu.mult))
                opq("vector", lambda E: E.tensor_scalar(
                    out=u0b, in0=lnb[:], scalar1=s_col, scalar2=128.5,
                    op0=Alu.mult, op1=Alu.add))
                lastv = op("vector", "sV", lambda E: E.tensor_copy(out=uLb, in_=u0b))
                for li, bb in enumerate(LADDER):
                    w("scalar", "sV", lastv)
                    if t == 0 and li == 0:
                        w("scalar", M["wqs2_7"])
                    sv = op("scalar", "sS", lambda E: E.activation(
                        out=dbuf[:, 0:1024], in_=uLb, func=Act.Sign,
                        bias=cbc[:, CI_SGNB:CI_SGNB + 1], scale=1.0))
                    w("vector", "sS", sv)
                    lastv = op("vector", "sV", lambda E, b2=bb: E.scalar_tensor_tensor(
                        out=uLb, in0=dbuf[:, 0:1024], scalar=-b2 / 2.0, in1=uLb,
                        op0=Alu.mult, op1=Alu.add))
                w("vector", "sT", M.get(f"p5tp{t-1}", 0))
                xqv = op("vector", "sV", lambda E: E.scalar_tensor_tensor(
                    out=xqF, in0=uLb, scalar=-1.0, in1=u0b,
                    op0=Alu.mult, op1=Alu.add))
                w("tensor", "sV", xqv)
                for c in range(8):
                    pp = c % 2
                    w("tensor", "sS", tp_ev[pp])
                    tv = op("tensor", "sT", lambda E, cc=c, p=pp: E.transpose(
                        out=psTF[p][:, 0:128],
                        in_=xqF[:, 128 * cc:128 * cc + 128],
                        identity=identF[:]))
                    w("scalar", "sT", tv)
                    tp_ev[pp] = op("scalar", "sS", lambda E, cc=c, tt=t, p=pp: E.activation(
                        out=bigs[cc][:, 128 * tt:128 * tt + 128],
                        in_=psTF[p][:, 0:128], func=Act.Identity,
                        bias=cbc[:, CI_NHALF:CI_NHALF + 1], scale=1.0))
                M[f"p5tp{t}"] = C.get("sT")
                M[f"p5q{t}"] = C.get("sS")
            M["p5_S"] = C.get("sS")
            M["p5_V"] = C.get("sV")
            if KPH <= 6:
                fin(); return

            wgv = [gbuf[1][:, 0:2048], gbuf[1][:, 2048:4096]]
            wuv = [gbuf[2][:, 0:2048], gbuf[2][:, 2048:4096]]
            h1t = gbuf[3][:, 0:256]
            h2t = gbuf[3][:, 256:512]
            g2t = [gbuf[3][:, 512:768], gbuf[3][:, 768:1024]]
            opq("vector", lambda E: E.memset(amax2[:], 0.0))
            evA = M["p4_psAB"]
            evB = M["p4_psAB"]
            h1_use = 0
            wblk_use = {0: 0, 1: 0}
            g2st = {0: 0, 1: 0}
            for nb in range(FI // 256):
                par = nb % 2
                if nb == 0:
                    w("sync", "sT", M[f"h{H-1}scan"])
                    w("sync", "sV", M["p4_V"])
                w("sync", "sT", wblk_use[par])
                lg = dma("sync",
                         wgv[par].rearrange("p (c o) -> p c o", c=8),
                         wqg_d[:, 256 * nb:256 * (nb + 1)]
                         .rearrange("(c p) o -> p c o", p=128))
                lu = dma("sync",
                         wuv[par].rearrange("p (c o) -> p c o", c=8),
                         wqu_d[:, 256 * nb:256 * (nb + 1)]
                         .rearrange("(c p) o -> p c o", p=128))
                for tt in range(NTT):
                    w("tensor", lg)
                    w("tensor", lu)
                    if nb == 0 and tt == 0:
                        w("tensor", "sS", M["p5_S"])
                        w("tensor", "sV", res_done)
                    w("tensor", "sS", evA)
                    w("tensor", "sV", evB)
                    for kk in range(8):
                        op("tensor", "sT", lambda E, k=kk, p=par, t2=tt: E.matmul(
                            out=psA[:, 0:256],
                            lhsT=bigs[k][:, 128 * t2:128 * (t2 + 1)],
                            rhs=wgv[p][:, 256 * k:256 * (k + 1)],
                            start=(k == 0), stop=(k == 7)))
                        op("tensor", "sT", lambda E, k=kk, p=par, t2=tt: E.matmul(
                            out=psB[:, 0:256],
                            lhsT=bigs[k][:, 128 * t2:128 * (t2 + 1)],
                            rhs=wuv[p][:, 256 * k:256 * (k + 1)],
                            start=(k == 0), stop=(k == 7)))
                    mmv = C.get("sT")
                    wblk_use[par] = mmv
                    w("scalar", "sT", mmv)
                    if nb == 0 and tt == 0:
                        w("scalar", "sV", M["p5_V"])
                    w("scalar", "sV", h1_use)
                    evA = op("scalar", "sS", lambda E, t2=tt: E.activation(
                        out=h1t[:], in_=psA[:, 0:256], func=Act.Silu,
                        scale=oscg[:, t2:t2 + 1]))
                    w("vector", "sS", evA)
                    w("vector", "sT", mmv)
                    w("vector", g2st[tt % 2])
                    opq("vector", lambda E, t2=tt: E.tensor_scalar_mul(
                        out=h2t[:], in0=psB[:, 0:256], scalar1=oscu[:, t2:t2 + 1]))
                    gv = op("vector", "sV", lambda E, p2=tt % 2: E.tensor_mul(
                        out=g2t[p2][:], in0=h1t[:], in1=h2t[:]))
                    evB = gv
                    h1_use = gv
                    opq("vector", lambda E, p2=tt % 2: E.reduce_max(
                        out=tmp_col, in_=g2t[p2][:], axis=AxX,
                        apply_absolute_value=True))
                    av = op("vector", "sV", lambda E, t2=tt: E.tensor_max(
                        out=amax2[:, t2:t2 + 1], in0=amax2[:, t2:t2 + 1],
                        in1=tmp_col))
                    w("sync", "sV", av)
                    g2st[tt % 2] = dma(
                        "sync", g2_d[128 * tt:128 * (tt + 1), 256 * nb:256 * (nb + 1)],
                        g2t[tt % 2][:])
            M["p6_T"] = C.get("sT")
            M["p6_V"] = C.get("sV")
            M["p6_L"] = snapd()
            if KPH <= 7:
                fin(); return

            w("sync", "sT", M["p6_T"])
            w("sync", M["p6_L"])
            for kk in range(32):
                lv = dma("sync", bigs[kk // 4][:, 1024 * (kk % 4):1024 * (kk % 4) + 1024],
                         wqd_d[128 * kk:128 * (kk + 1), :])
            M["wdT"] = snapd()
            g2l = [gbuf[1][:, 0:4096], gbuf[2][:, 0:4096]]
            gqb = gbuf[3][:, 0:4096]
            u0w = far[:, 0:4096]
            uLw = far[:, 4096:8192]
            s2_col = colw[:, 8:9]
            oscd = colw[:, 9:10]
            u0_done = {0: 0, 1: 0}
            gq_used = 0
            res7 = 0
            psT7 = M["p5_S"]
            for t in range(NTT):
                par = t % 2
                w("sync", "sV", u0_done[par])
                lv = dma("sync", g2l[par][:], g2_d[128 * t:128 * (t + 1), :])
                w("vector", lv)
                if t == 0:
                    w("vector", "sV", M["p6_V"])
                opq("vector", lambda E, t2=t: E.tensor_scalar_max(
                    out=am_col, in0=amax2[:, t2:t2 + 1], scalar1=1e-5))
                opq("vector", lambda E: E.reciprocal(out=s2_col, in_=am_col))
                opq("vector", lambda E: E.tensor_scalar_mul(
                    out=s2_col, in0=s2_col, scalar1=127.0))
                opq("vector", lambda E: E.scalar_tensor_tensor(
                    out=oscd, in0=am_col, scalar=1.0 / 127.0,
                    in1=mcols[:, 2:3], op0=Alu.mult, op1=Alu.mult))
                w("vector", "sT", M.get(f"p7tp{t-1}", 0))
                u0v = op("vector", "sV", lambda E, p=par: E.tensor_scalar(
                    out=u0w, in0=g2l[p][:], scalar1=s2_col, scalar2=128.5,
                    op0=Alu.mult, op1=Alu.add))
                u0_done[par] = u0v
                lastv = op("vector", "sV", lambda E: E.tensor_copy(out=uLw, in_=u0w))
                for bb in LADDER:
                    w("scalar", "sV", lastv)
                    if t == 0 and bb == LADDER[0]:
                        w("scalar", M["wdT"])
                    sv = op("scalar", "sS", lambda E: E.activation(
                        out=dbuf[:], in_=uLw, func=Act.Sign,
                        bias=cbc[:, CI_SGNB:CI_SGNB + 1], scale=1.0))
                    w("vector", "sS", sv)
                    lastv = op("vector", "sV", lambda E, b2=bb: E.scalar_tensor_tensor(
                        out=uLw, in0=dbuf[:], scalar=-b2 / 2.0, in1=uLw,
                        op0=Alu.mult, op1=Alu.add))
                xqv = op("vector", "sV", lambda E: E.scalar_tensor_tensor(
                    out=u0w, in0=uLw, scalar=-1.0, in1=u0w,
                    op0=Alu.mult, op1=Alu.add))
                w("tensor", "sV", xqv)
                for c in range(32):
                    pp = c % 2
                    w("tensor", "sS", tp_ev[pp])
                    tv = op("tensor", "sT", lambda E, cc=c, p=pp: E.transpose(
                        out=psTF[p][:, 0:128],
                        in_=u0w[:, 128 * cc:128 * cc + 128], identity=identF[:]))
                    w("scalar", "sT", tv)
                    if c == 0:
                        w("scalar", "sT", gq_used)
                        if t == 0:
                            w("scalar", "sV", M["p6_V"])
                            w("scalar", M["p6_L"])
                    tp_ev[pp] = op("scalar", "sS", lambda E, cc=c, p=pp: E.activation(
                        out=gqb[:, 128 * cc:128 * cc + 128],
                        in_=psTF[p][:, 0:128],
                        func=Act.Identity, bias=cbc[:, CI_NHALF:CI_NHALF + 1],
                        scale=1.0))
                M[f"p7tp{t}"] = C.get("sT")
                psT7 = C.get("sS")
                w("tensor", "sS", psT7)
                w("tensor", M["wdT"])
                w("tensor", "sV", res7)
                for kk in range(32):
                    op("tensor", "sT", lambda E, k=kk: E.matmul(
                        out=psA[:], lhsT=gqb[:, 128 * k:128 * (k + 1)],
                        rhs=bigs[k // 4][:, 1024 * (k % 4):1024 * (k % 4) + 512],
                        start=(k == 0), stop=(k == 31)))
                    op("tensor", "sT", lambda E, k=kk: E.matmul(
                        out=psB[:], lhsT=gqb[:, 128 * k:128 * (k + 1)],
                        rhs=bigs[k // 4][:, 1024 * (k % 4) + 512:1024 * (k % 4) + 1024],
                        start=(k == 0), stop=(k == 31)))
                gq_used = C.get("sT")
                lv = dma("sync", xres, x1_d[128 * t:128 * (t + 1), :])
                w("vector", "sT", gq_used)
                w("vector", lv)
                if t >= 1:
                    w("vector", "sO", 16 * t)
                for half in range(2):
                    opq("vector", lambda E, hf=half: E.tensor_scalar_mul(
                        out=d2b[:, 512 * hf:512 * hf + 512],
                        in0=(psA if hf == 0 else psB)[:], scalar1=oscd))
                res7 = op("vector", "sV", lambda E: E.tensor_add(
                    out=lnb[:], in0=d2b, in1=xres))
                opq("vector", lambda E: E.reduce_sum(out=mn_col, in_=lnb[:], axis=AxX))
                opq("vector", lambda E: E.tensor_scalar_mul(
                    out=mn_col, in0=mn_col, scalar1=1.0 / D))
                opq("vector", lambda E: E.tensor_single_scalar(
                    out=d2b, in_=lnb[:], scalar=mn_col, op=Alu.subtract))
                opq("vector", lambda E: E.scalar_tensor_tensor(
                    out=lnb[:], in0=d2b, scalar=1.0, in1=d2b,
                    op0=Alu.mult, op1=Alu.mult, accum_out=vr_col))
                vv = op("vector", "sV", lambda E: E.tensor_scalar(
                    out=vr_col, in0=vr_col, scalar1=1.0 / D, scalar2=1e-5,
                    op0=Alu.mult, op1=Alu.add))
                w("gpsimd", "sV", vv)
                gv = op("gpsimd", "sGE", lambda E: E.tensor_tensor(
                    out=rs_col, in0=vr_col, in1=cbc[:, CI_PEXP:CI_PEXP + 1],
                    op=Alu.pow))
                w("vector", "sGE", gv)
                ov = op("vector", "sV", lambda E: E.tensor_scalar_mul(
                    out=lnb[:], in0=d2b, scalar1=rs_col))
                w("sync", "sV", ov)
                dma("sync", out_tok[128 * t:128 * (t + 1), :], lnb[:], sem="sO")
            w("sync", "sO", C.get("sO"))

        blk.sync(lambda E: program(E, "sync"))
        blk.vector(lambda E: program(E, "vector"))
        blk.scalar(lambda E: program(E, "scalar"))
        blk.tensor(lambda E: program(E, "tensor"))
        blk.gpsimd(lambda E: program(E, "gpsimd"))
    return nc


def _host_consts():
    kk = np.arange(128)[:, None]
    qq = np.arange(512)[None, :]
    maskbig = np.concatenate(
        [(kk <= qq - r).astype(ml_dtypes.bfloat16) for r in (0, 128, 256, 384)],
        axis=1)
    ident = np.eye(128, dtype=ml_dtypes.bfloat16)
    p = np.arange(128)[:, None]
    m = np.arange(128)[None, :]
    iot = (m - 64 * (p // 64)).astype(ml_dtypes.bfloat16)
    return maskbig, ident, iot


def build_shared(L0_attn_Wqkv_w, L0_attn_Wqkv_b, L0_attn_out_w,
                 L0_attn_out_b, L0_attn_W_static, L0_attn_Lambda_raw,
                 L0_attn_Gamma, L0_attn_gate_w1, L0_attn_gate_b1,
                 L0_attn_gate_w2, L0_attn_gate_b2, L0_ffn_w_gate,
                 L0_ffn_w_up, L0_ffn_w_down):
    maskbig, ident, iot = _host_consts()
    shared = {
        "wqkv": np.ascontiguousarray(L0_attn_Wqkv_w, dtype=np.float32),
        "bqkv": np.ascontiguousarray(L0_attn_Wqkv_b, dtype=np.float32),
        "outw": np.ascontiguousarray(L0_attn_out_w, dtype=np.float32),
        "outb": np.ascontiguousarray(L0_attn_out_b, dtype=np.float32),
        "wst": np.ascontiguousarray(L0_attn_W_static, dtype=np.float32),
        "lam": np.ascontiguousarray(L0_attn_Lambda_raw, dtype=np.float32),
        "gam": np.ascontiguousarray(L0_attn_Gamma, dtype=np.float32),
        "gw1": np.ascontiguousarray(L0_attn_gate_w1, dtype=np.float32),
        "gb1": np.ascontiguousarray(L0_attn_gate_b1, dtype=np.float32),
        "gw2": np.ascontiguousarray(L0_attn_gate_w2, dtype=np.float32),
        "gb2": np.ascontiguousarray(L0_attn_gate_b2, dtype=np.float32),
        "wgate": np.ascontiguousarray(L0_ffn_w_gate, dtype=np.float32),
        "wup": np.ascontiguousarray(L0_ffn_w_up, dtype=np.float32),
        "wdown": np.ascontiguousarray(L0_ffn_w_down, dtype=np.float32),
        "maskc": maskbig, "identc": ident, "ioc": iot,
        "identf": np.eye(128, dtype=np.float32),
        "cvec": CONSTS.reshape(1, 8).copy(),
        "bqkv_r": np.ascontiguousarray(
            np.asarray(L0_attn_Wqkv_b, dtype=np.float32).reshape(24, 128).T),
        "gb1_r": np.ascontiguousarray(
            np.asarray(L0_attn_gate_b1, dtype=np.float32).reshape(4, 128).T),
        "lamT": np.ascontiguousarray(
            np.asarray(L0_attn_Lambda_raw, dtype=np.float32)
            .transpose(2, 0, 1).reshape(64, 1024)),
        "gamT": np.ascontiguousarray(
            np.asarray(L0_attn_Gamma, dtype=np.float32)
            .transpose(2, 0, 1).reshape(64, 1024)),
        "wstT": np.ascontiguousarray(
            np.asarray(L0_attn_W_static, dtype=np.float32)
            .transpose(2, 0, 1).reshape(64, 1024)),
    }
    return shared


def kernel(L0_attn_x, L0_attn_Wqkv_w, L0_attn_Wqkv_b, L0_attn_out_w,
           L0_attn_out_b, L0_attn_W_static, L0_attn_Lambda_raw, L0_attn_Gamma,
           L0_attn_gate_w1, L0_attn_gate_b1, L0_attn_gate_w2, L0_attn_gate_b2,
           L0_ffn_w_gate, L0_ffn_w_up, L0_ffn_w_down, _n_cores=NCORES):
    shared = build_shared(
        L0_attn_Wqkv_w, L0_attn_Wqkv_b, L0_attn_out_w, L0_attn_out_b,
        L0_attn_W_static, L0_attn_Lambda_raw, L0_attn_Gamma, L0_attn_gate_w1,
        L0_attn_gate_b1, L0_attn_gate_w2, L0_attn_gate_b2, L0_ffn_w_gate,
        L0_ffn_w_up, L0_ffn_w_down)
    nc = build_nc()
    x = np.ascontiguousarray(np.asarray(L0_attn_x, dtype=np.float32))
    in_maps = []
    for c in range(_n_cores):
        m = dict(shared)
        m["x"] = np.ascontiguousarray(x[BPC * c:BPC * (c + 1)])
        in_maps.append(m)
    res = run_bass_kernel_spmd(nc, in_maps, core_ids=list(range(_n_cores)))
    return np.concatenate([r["out"] for r in res.results], axis=0)


if __name__ == "__main__":
    nc = build_nc()
    print("build ok; instructions:", nc.n_instructions())


# version 03
# speedup vs baseline: 1.0724x; 1.0697x over previous
import os
import numpy as np
import ml_dtypes
import concourse.bass as bass
import concourse.mybir as mybir
from concourse.bass_utils import run_bass_kernel_spmd
from contextlib import ExitStack

F32 = mybir.dt.float32
BF16 = mybir.dt.bfloat16
Alu = mybir.AluOpType
Act = mybir.ActivationFunctionType
AxX = mybir.AxisListType.X

KPH = int(os.environ.get("KPH", "8"))
KDBG = os.environ.get("KDBG", "0") == "1"

NCORES = 8
BPC = 2
L = 2048
D = 1024
H = 16
DH = 64
T = BPC * L
NTT = T // 128
G1 = 512
FI = 4096
NQT = L // 512
NKC = L // 128

LADDER = [128.0, 64.0, 32.0, 16.0, 8.0, 4.0, 2.0, 1.0]

CI_EXPB, CI_SGNB, CI_NHALF, CI_ONE, CI_PEXP = 0, 1, 2, 3, 4
CONSTS = np.zeros(8, np.float32)
CONSTS[CI_EXPB] = -10.0
CONSTS[CI_SGNB] = -128.0
CONSTS[CI_NHALF] = -0.5
CONSTS[CI_ONE] = 1.0
CONSTS[CI_PEXP] = -0.5


class Ctr:
    def __init__(self):
        self.n = {}
        self.m = {}

    def inc(self, name, k=1):
        self.n[name] = self.n.get(name, 0) + k
        return self.n[name]

    def get(self, name):
        return self.n.get(name, 0)


def build_nc():
    nc = bass.Bass("TRN2", target_bir_lowering=False)

    x_in = nc.dram_tensor("x", [BPC, L, D], F32, kind="ExternalInput").ap()
    wqkv = nc.dram_tensor("wqkv", [D, 3 * D], F32, kind="ExternalInput").ap()
    bqkv = nc.dram_tensor("bqkv", [3 * D], F32, kind="ExternalInput").ap()
    outw = nc.dram_tensor("outw", [D, D], F32, kind="ExternalInput").ap()
    outb = nc.dram_tensor("outb", [D], F32, kind="ExternalInput").ap()
    wst = nc.dram_tensor("wst", [H, DH, DH], F32, kind="ExternalInput").ap()
    lam = nc.dram_tensor("lam", [H, DH, DH], F32, kind="ExternalInput").ap()
    gam = nc.dram_tensor("gam", [H, DH, DH], F32, kind="ExternalInput").ap()
    gw1 = nc.dram_tensor("gw1", [D, G1], F32, kind="ExternalInput").ap()
    gb1 = nc.dram_tensor("gb1", [G1], F32, kind="ExternalInput").ap()
    gw2 = nc.dram_tensor("gw2", [G1, H], F32, kind="ExternalInput").ap()
    gb2 = nc.dram_tensor("gb2", [H], F32, kind="ExternalInput").ap()
    wgate = nc.dram_tensor("wgate", [FI, D], F32, kind="ExternalInput").ap()
    wup = nc.dram_tensor("wup", [FI, D], F32, kind="ExternalInput").ap()
    wdown = nc.dram_tensor("wdown", [D, FI], F32, kind="ExternalInput").ap()
    maskc = nc.dram_tensor("maskc", [128, 2048], BF16, kind="ExternalInput").ap()
    bqkv_r = nc.dram_tensor("bqkv_r", [128, 24], F32, kind="ExternalInput").ap()
    gb1_r = nc.dram_tensor("gb1_r", [128, 4], F32, kind="ExternalInput").ap()
    lamT = nc.dram_tensor("lamT", [64, 1024], F32, kind="ExternalInput").ap()
    gamT = nc.dram_tensor("gamT", [64, 1024], F32, kind="ExternalInput").ap()
    wstT = nc.dram_tensor("wstT", [64, 1024], F32, kind="ExternalInput").ap()
    ident_d = nc.dram_tensor("identc", [128, 128], BF16, kind="ExternalInput").ap()
    identf_d = nc.dram_tensor("identf", [128, 128], F32, kind="ExternalInput").ap()
    io_d = nc.dram_tensor("ioc", [128, 128], BF16, kind="ExternalInput").ap()
    cvec = nc.dram_tensor("cvec", [1, 8], F32, kind="ExternalInput").ap()
    out = nc.dram_tensor("out", [BPC, L, D], F32, kind="ExternalOutput").ap()

    _K = "ExternalOutput" if KDBG else "Internal"
    q_d = nc.dram_tensor("q_d", [BPC, H, DH, L], BF16, kind=_K).ap()
    k_d = nc.dram_tensor("k_d", [BPC, H, DH, L], BF16, kind=_K).ap()
    v_d = nc.dram_tensor("v_d", [BPC, H, DH, L], BF16, kind=_K).ap()
    vtm_d = nc.dram_tensor("vtm_d", [BPC, H, L, DH], BF16, kind=_K).ap()
    al_d = nc.dram_tensor("al_d", [H, T], BF16, kind=_K).ap()
    den_d = nc.dram_tensor("den_d", [BPC, H, L], F32, kind=_K).ap()
    x1_d = nc.dram_tensor("x1_d", [T, D], F32, kind=_K).ap()
    cc_d = nc.dram_tensor("cc_d", [4, 2048], BF16, kind=_K).ap()
    g2_d = nc.dram_tensor("g2_d", [T, FI], BF16, kind=_K).ap()
    wqg_d = nc.dram_tensor("wqg_d", [D, FI], BF16, kind=_K).ap()
    wqu_d = nc.dram_tensor("wqu_d", [D, FI], BF16, kind=_K).ap()
    wqd_d = nc.dram_tensor("wqd_d", [FI, D], BF16, kind=_K).ap()
    m_d = nc.dram_tensor("m_d", [1, 4], F32, kind=_K).ap()

    x_tok = x_in.rearrange("b l d -> (b l) d")
    out_tok = out.rearrange("b l d -> (b l) d")

    with ExitStack() as es:
        e = es.enter_context

        bigs = [e(nc.sbuf_tensor(f"big{i}", [128, 4096], BF16)) for i in range(8)]
        gbuf = [e(nc.sbuf_tensor(f"gwk{i}", [128, 4096], BF16)) for i in range(4)]
        far = e(nc.sbuf_tensor("far", [128, 8192], F32))
        dbuf = e(nc.sbuf_tensor("dbuf", [128, 4096], BF16))
        lnar = e(nc.sbuf_tensor("lnar", [128, 3072], F32))
        outw_sb = e(nc.sbuf_tensor("outw_sb", [128, 8192], BF16))
        ident = e(nc.sbuf_tensor("ident", [128, 128], BF16))
        identF = e(nc.sbuf_tensor("identF", [128, 128], F32))
        iot = e(nc.sbuf_tensor("iot", [128, 128], BF16))
        sel = [e(nc.sbuf_tensor(f"sel{i}", [128, 128], BF16)) for i in range(2)]
        r_all = e(nc.sbuf_tensor("r_all", [128, 1024], F32))
        g_all = e(nc.sbuf_tensor("g_all", [128, 1024], F32))
        wt2 = e(nc.sbuf_tensor("wt2", [128, 1024], BF16))
        wblk = e(nc.sbuf_tensor("wblk", [128, 128], BF16))
        outb_bc = e(nc.sbuf_tensor("outb_bc", [128, 1024], BF16))
        k12 = e(nc.sbuf_tensor("k12", [128, 2048], BF16))
        q12 = e(nc.sbuf_tensor("q12", [128, 2048], BF16))
        ysm = [e(nc.sbuf_tensor(f"ysm{i}", [64, 2048], BF16)) for i in range(2)]
        prb = [e(nc.sbuf_tensor(f"prb{i}", [128, 512], BF16)) for i in range(2)]
        v65 = [e(nc.sbuf_tensor(f"v65{i}", [128, 65], BF16)) for i in range(2)]
        cbc = e(nc.sbuf_tensor("cbc", [128, 8], F32))
        qb_col = e(nc.sbuf_tensor("qb_col", [128, 24], F32))
        b1c = e(nc.sbuf_tensor("b1c", [128, 4], F32))
        b2c = e(nc.sbuf_tensor("b2c", [16, 1], F32))
        mcols = e(nc.sbuf_tensor("mcols", [128, 8], F32))
        a16 = e(nc.sbuf_tensor("a16", [16, 512], BF16))
        amax2 = e(nc.sbuf_tensor("amax2", [128, 32], F32))
        oscg = e(nc.sbuf_tensor("oscg", [128, 32], F32))
        oscu = e(nc.sbuf_tensor("oscu", [128, 32], F32))
        colw = e(nc.sbuf_tensor("colw", [128, 16], F32))

        psA = e(nc.psum_tensor("psA", [128, 512], F32))
        psB = e(nc.psum_tensor("psB", [128, 512], F32))
        psY = e(nc.psum_tensor("psY", [128, 2048], F32))
        psAV = psY[:, 0:512]
        psTF = [e(nc.psum_tensor("psTF0", [128, 512], F32)),
                e(nc.psum_tensor("psTF1", [128, 512], F32))]
        psAV4 = psTF[0]

        SEM = {}
        SYNC_POOL = [f"sl{i}" for i in range(12)]
        GP_POOL = [f"sg{i}" for i in range(4)]
        for s in SYNC_POOL + GP_POOL + ["sV", "sS", "sT", "sGE", "sO"]:
            SEM[s] = e(nc.semaphore(s))
        POOLS = {"sync": SYNC_POOL, "gpsimd": GP_POOL}

        blk = e(nc.Block())

        def program(eng, role):
            C = Ctr()
            M = C.m

            def dma(q, outap, inap, sem=None, transpose=False):
                if sem == "sO":
                    v = C.inc("sO", 16)
                    if role == q:
                        eng.dma_start(out=outap, in_=inap).then_inc(SEM["sO"], 16)
                    return v
                pool = POOLS[q]
                idx = C.inc(f"_rr_{q}") % len(pool)
                sm = pool[idx]
                prev = C.get(sm)
                v = C.inc(sm, 16)
                if role == q:
                    if prev > 0:
                        eng.wait_ge(SEM[sm], prev)
                    if transpose:
                        eng.dma_start_transpose(out=outap, in_=inap).then_inc(SEM[sm], 16)
                    else:
                        eng.dma_start(out=outap, in_=inap).then_inc(SEM[sm], 16)
                return (sm, v)

            def snapd():
                return tuple((sm, C.get(sm))
                             for sm in SYNC_POOL + GP_POOL if C.get(sm) > 0)

            def op(r, sem, fn):
                v = C.inc(sem, 1)
                if role == r:
                    fn(eng).then_inc(SEM[sem], 1)
                    if r in ("vector", "gpsimd"):
                        eng.wait_ge(SEM[sem], v)
                return v

            def opn(r, sem, fn):
                v = C.inc(sem, 1)
                if role == r:
                    fn(eng).then_inc(SEM[sem], 1)
                return v

            def opq(r, fn):
                if r in ("vector", "gpsimd"):
                    sem = "sV" if r == "vector" else "sGE"
                    v = C.inc(sem, 1)
                    if role == r:
                        fn(eng).then_inc(SEM[sem], 1)
                        eng.wait_ge(SEM[sem], v)
                else:
                    if role == r:
                        fn(eng)

            def w(r, sem, v=None):
                if role != r:
                    return
                if v is None:
                    mk = sem
                    if not mk:
                        return
                    if isinstance(mk, tuple) and mk and isinstance(mk[0], tuple):
                        for sm, vv in mk:
                            eng.wait_ge(SEM[sm], vv)
                    elif isinstance(mk, tuple):
                        eng.wait_ge(SEM[mk[0]], mk[1])
                else:
                    if v and v > 0:
                        eng.wait_ge(SEM[sem], v)

            def fin():
                if role == "sync":
                    for sm in SYNC_POOL:
                        if C.get(sm) > 0:
                            eng.wait_ge(SEM[sm], C.get(sm))
                    if C.get("sO") > 0:
                        eng.wait_ge(SEM["sO"], C.get("sO"))
                if role == "gpsimd":
                    for sm in GP_POOL:
                        if C.get(sm) > 0:
                            eng.wait_ge(SEM[sm], C.get(sm))

            tp_ev = {0: 0, 1: 0}

            dma("sync", gbuf[0][:, 0:2048], maskc)
            dma("sync", ident[:], ident_d)
            dma("sync", identF[:], identf_d)
            dma("sync", iot[:], io_d)
            dma("sync", cbc[:], cvec.broadcast_to([128, 8]))
            dma("sync", qb_col[:], bqkv_r)
            dma("sync", b1c[:], gb1_r)
            dma("sync", b2c[:], gb2.unsqueeze(1))
            M["outb"] = dma("gpsimd", outb_bc[:], outb.unsqueeze(0).broadcast_to([128, 1024]))
            dma("sync", r_all[0:64, :], lamT)
            dma("sync", r_all[64:128, :], lamT)
            dma("sync", g_all[0:64, :], gamT)
            dma("sync", g_all[64:128, :], gamT)
            dma("sync", lnar[0:64, 0:1024], wstT)
            dma("sync", lnar[64:128, 0:1024], wstT)
            M["p0"] = snapd()
            w("scalar", M["p0"])
            vsg = op("scalar", "sS", lambda E: E.activation(
                out=r_all[:], in_=r_all[:], func=Act.Sigmoid))
            w("vector", "sS", vsg)
            opq("vector", lambda E: E.tensor_scalar(
                out=r_all[:], in0=r_all[:], scalar1=-1.0, scalar2=1.0,
                op0=Alu.mult, op1=Alu.add))
            opq("vector", lambda E: E.tensor_scalar_mul(
                out=g_all[:], in0=g_all[:], scalar1=0.125))
            w("vector", M["p0"])
            M["rg"] = op("vector", "sV", lambda E: E.tensor_copy(
                out=wt2[:], in_=lnar[:, 0:1024]))
            if KPH <= 0:
                fin(); return

            wtile = far[:, 0:4096]
            p1b = gbuf[1]
            n1b = gbuf[2]
            wqbF = far[:, 4096:8192]
            acc_col = colw[:, 0:1]
            tmp_col = colw[:, 1:2]

            KWS = int(os.environ.get("KWS", "3"))
            def wq_phase(widx, w_ap, wq_ap_d, nrows, ncols):
                nt = nrows // 128
                nel = nrows * ncols
                opq("vector", lambda E: E.memset(acc_col, 0.0))
                last_rd = 0
                for t in range(nt):
                    if t >= 1:
                        w("sync", "sV", M[f"wqa{widx}_{t-1}"])
                    lv = dma("sync", wtile[:, 0:ncols], w_ap[128 * t:128 * (t + 1), :])
                    w("vector", lv)
                    opq("vector", lambda E: E.reduce_sum(
                        out=tmp_col, in_=wtile[:, 0:ncols], axis=AxX,
                        apply_absolute_value=True))
                    last_rd = op("vector", "sV", lambda E: E.tensor_add(
                        out=acc_col, in0=acc_col, in1=tmp_col))
                    M[f"wqa{widx}_{t}"] = last_rd
                    if KDBG and widx == 0:
                        w("sync", "sV", last_rd)
                        snp = dma("sync", den_d[1, 1 + t // 16, (t % 16) * 128:
                                  (t % 16) * 128 + 128], tmp_col)
                        w("vector", snp)
                if KWS <= 1:
                    return
                w("tensor", "sV", last_rd)
                w("tensor", M["p0"])
                if widx > 0:
                    w("tensor", "sV", M[f"wqm{widx-1}"])
                tv = op("tensor", "sT", lambda E: E.matmul(
                    out=psA[0:1, 0:1], lhsT=acc_col, rhs=cbc[:, CI_ONE:CI_ONE + 1],
                    start=True, stop=True))
                w("vector", "sT", tv)
                mv = op("vector", "sV", lambda E: E.tensor_scalar(
                    out=tmp_col[0:1, :], in0=psA[0:1, 0:1],
                    scalar1=1.0 / nel, scalar2=1e-5, op0=Alu.mult, op1=Alu.max))
                M[f"wqm{widx}"] = mv
                w("sync", "sV", mv)
                stm = dma("sync", m_d[0:1, widx:widx + 1], tmp_col[0:1, :])
                w("sync", stm)
                lv = dma("sync", mcols[:, widx:widx + 1],
                         m_d[0:1, widx:widx + 1].broadcast_to([128, 1]))
                w("vector", lv)
                iv = op("vector", "sV", lambda E: E.reciprocal(
                    out=mcols[:, 4 + widx:5 + widx], in_=mcols[:, widx:widx + 1]))
                if KDBG and widx == 0:
                    dma("sync", den_d[0, 0, 0:128], acc_col)
                    dma("sync", den_d[0, 0, 128:256], cbc[:, CI_ONE:CI_ONE + 1])
                if KWS <= 2:
                    return
                nblk = ncols // 128
                KWT = int(os.environ.get("KWT", "99"))
                for t in range(nt):
                    if t >= KWT:
                        continue
                    if t >= 1:
                        w("sync", "sV", M[f"wqv{widx}_{t-1}"])
                    lv = dma("sync", wtile[:, 0:ncols], w_ap[128 * t:128 * (t + 1), :])
                    w("vector", lv)
                    w("vector", "sV", iv)
                    if t >= 1:
                        w("vector", "sT", M[f"wqt{widx}_{t-1}"])
                    opq("vector", lambda E: E.tensor_scalar(
                        out=p1b[:, 0:ncols], in0=wtile[:, 0:ncols],
                        scalar1=mcols[:, 4 + widx:5 + widx], scalar2=0.5,
                        op0=Alu.mult, op1=Alu.is_gt))
                    opq("vector", lambda E: E.tensor_scalar(
                        out=n1b[:, 0:ncols], in0=wtile[:, 0:ncols],
                        scalar1=mcols[:, 4 + widx:5 + widx], scalar2=-0.5,
                        op0=Alu.mult, op1=Alu.is_lt))
                    vv = op("vector", "sV", lambda E: E.tensor_sub(
                        out=wqbF[:, 0:ncols], in0=p1b[:, 0:ncols], in1=n1b[:, 0:ncols]))
                    M[f"wqv{widx}_{t}"] = vv
                    w("tensor", "sV", vv)
                    for cb in range(nblk):
                        pp = cb % 2
                        w("tensor", "sS", tp_ev[pp])
                        tv = op("tensor", "sT", lambda E, c=cb, p=pp: E.transpose(
                            out=psTF[p][:, 0:128],
                            in_=wqbF[:, 128 * c:128 * c + 128], identity=identF[:]))
                        w("scalar", "sT", tv)
                        if cb == 0:
                            prev = (M.get(f"wqs{widx}_{t-1}") if t >= 1
                                    else M.get(f"wqs{widx-1}_{(FI if widx <= 2 else D)//128 - 1}")
                                    if widx >= 1 else None)
                            w("scalar", prev)
                        tp_ev[pp] = op(
                            "scalar", "sS", lambda E, c=cb, p=pp: E.copy(
                                out=dbuf[:, 128 * c:128 * c + 128],
                                in_=psTF[p][:, 0:128]))
                    M[f"wqt{widx}_{t}"] = C.get("sT")
                    M[f"wqe{widx}_{t}"] = C.get("sS")
                    w("sync", "sS", M[f"wqe{widx}_{t}"])
                    if os.environ.get("KWX", "0") != "1":
                        for cb in range(nblk):
                            dma("sync", wq_ap_d[128 * cb:128 * cb + 128,
                                                128 * t:128 * (t + 1)],
                                dbuf[:, 128 * cb:128 * cb + 128])
                    M[f"wqs{widx}_{t}"] = snapd()
                return

            def wq_phase_guarded(widx, w_ap, wq_ap_d, nrows, ncols):
                wq_phase(widx, w_ap, wq_ap_d, nrows, ncols)

            KW = int(os.environ.get("KW", "3"))
            if KW >= 1:
                wq_phase_guarded(0, wgate, wqg_d, FI, D)
            if KW >= 2:
                wq_phase_guarded(1, wup, wqu_d, FI, D)
            if KW >= 3:
                wq_phase_guarded(2, wdown, wqd_d, D, FI)
            if KW < 3:
                fin(); return
            M["p1_S"] = C.get("sS")
            M["p1_T"] = C.get("sT")
            M["p1_V"] = C.get("sV")
            if KPH <= 1:
                fin(); return

            xt_f = lnar[:, 0:1024]
            for t in range(NTT):
                if t >= 1:
                    w("sync", "sT", M[f"xft{t-1}"])
                else:
                    w("sync", "sV", M["rg"])
                lv = dma("sync", xt_f, x_tok[128 * t:128 * (t + 1), :])
                w("tensor", lv)
                for c in range(8):
                    pp = c % 2
                    w("tensor", "sS", tp_ev[pp])
                    tv = op("tensor", "sT", lambda E, cc=c, p=pp: E.transpose(
                        out=psTF[p][:, 0:128],
                        in_=xt_f[:, 128 * cc:128 * cc + 128], identity=identF[:]))
                    w("scalar", "sT", tv)
                    tp_ev[pp] = op("scalar", "sS", lambda E, cc=c, tt=t, p=pp: E.copy(
                        out=bigs[cc][:, 128 * tt:128 * tt + 128],
                        in_=psTF[p][:, 0:128]))
                M[f"xft{t}"] = C.get("sT")
            M["xfm"] = C.get("sS")
            if KPH <= 2:
                fin(); return

            wqv_f32 = [lnar[:, 1024:2048], lnar[:, 2048:3072]]
            wqv_b16 = [gbuf[1][:, 0:1024], gbuf[1][:, 1024:2048]]
            qkv_sb = [gbuf[2][:, 0:512], gbuf[2][:, 512:1024]]
            vtm_sb = [gbuf[2][:, 1024:1152], gbuf[2][:, 1152:1280]]
            ev_bank = {0: 0, 1: 0}
            st_mark = {0: 0, 1: 0}
            vt_mark = {0: 0, 1: 0}
            vstage_use = M[f"xft{NTT-1}"]
            for f in range(24):
                par = f % 2
                if f >= 2:
                    w("sync", "sV", M[f"qkvw{f-2}"])
                lv = dma("sync",
                         wqv_f32[par].rearrange("p (c o) -> p c o", c=8),
                         wqkv[:, 128 * f:128 * (f + 1)]
                         .rearrange("(c p) o -> p c o", p=128))
                w("vector", lv)
                cv = op("vector", "sV", lambda E, p=par: E.tensor_copy(
                    out=wqv_b16[p][:], in_=wqv_f32[p][:]))
                M[f"qkvw{f}"] = cv
                w("tensor", "sV", cv)
                if f == 0:
                    w("tensor", "sS", M["xfm"])
                    w("tensor", "sV", M["wqm2"])
                for tt in range(8):
                    bk = tt % 2
                    w("tensor", "sS", ev_bank[bk])
                    for kc in range(8):
                        op("tensor", "sT", lambda E, k=kc, b2=bk, p=par, t8=tt: E.matmul(
                            out=(psA if b2 == 0 else psB)[:],
                            lhsT=wqv_b16[p][:, 128 * k:128 * (k + 1)],
                            rhs=bigs[k][:, 512 * t8:512 * (t8 + 1)],
                            start=(k == 0), stop=(k == 7)))
                    tv = C.get("sT")
                    w("scalar", "sT", tv)
                    w("scalar", st_mark[bk])
                    sv = op("scalar", "sS", lambda E, b2=bk, ff=f: E.activation(
                        out=qkv_sb[b2][:], in_=(psA if b2 == 0 else psB)[:],
                        func=Act.Identity, bias=qb_col[:, ff:ff + 1], scale=1.0))
                    ev_bank[bk] = sv
                    w("sync", "sS", sv)
                    bidx = tt // 4
                    ts0 = (tt % 4) * 512
                    dst = (q_d if f < 8 else (k_d if f < 16 else v_d))
                    hh = 2 * (f % 8)
                    st_mark[bk] = dma(
                        "sync", dst[bidx, hh:hh + 2, :, ts0:ts0 + 512]
                        .rearrange("h d t -> (h d) t"), qkv_sb[bk][:])
                    if f >= 16:
                        w("scalar", "sT", vstage_use)
                        sv2 = op("scalar", "sS", lambda E, b2=bk, ff=f: E.activation(
                            out=lnar[:, 0:512], in_=(psA if b2 == 0 else psB)[:],
                            func=Act.Identity, bias=qb_col[:, ff:ff + 1], scale=1.0))
                        ev_bank[bk] = sv2
                        w("tensor", "sS", sv2)
                        for c4 in range(4):
                            sl = c4 % 2
                            w("tensor", "sS", tp_ev[sl])
                            tv = op("tensor", "sT", lambda E, c=c4, s=sl: E.transpose(
                                out=psTF[s][:, 0:128],
                                in_=lnar[:, 128 * c:128 * c + 128],
                                identity=identF[:]))
                            w("scalar", "sT", tv)
                            w("scalar", vt_mark[sl])
                            sv = op("scalar", "sS", lambda E, s=sl: E.copy(
                                out=vtm_sb[s][:], in_=psTF[s][:, 0:128]))
                            tp_ev[sl] = sv
                            w("sync", "sS", sv)
                            tb = ts0 + 128 * c4
                            vt_mark[sl] = (
                                dma("sync", vtm_d[bidx, hh, tb:tb + 128, :],
                                    vtm_sb[sl][:, 0:64]),
                                dma("sync", vtm_d[bidx, hh + 1, tb:tb + 128, :],
                                    vtm_sb[sl][:, 64:128]))
                        vstage_use = C.get("sT")
            M["qkv_L"] = snapd()
            M["qkv_T"] = C.get("sT")
            M["qkv_S"] = C.get("sS")
            if KPH <= 3:
                fin(); return

            for c in range(4):
                if c >= 1:
                    w("sync", "sV", M[f"gw1c{c-1}"])
                lv = dma("sync",
                         lnar[:, 1024:2048].rearrange("p (k o) -> p k o", k=8),
                         gw1[:, 128 * c:128 * (c + 1)]
                         .rearrange("(k p) o -> p k o", p=128))
                w("vector", lv)
                M[f"gw1c{c}"] = op("vector", "sV", lambda E, cc=c: E.tensor_copy(
                    out=gbuf[3][:, 1024 * cc:1024 * (cc + 1)], in_=lnar[:, 1024:2048]))
            w("sync", "sV", M["gw1c3"])
            lv = dma("sync",
                     lnar[:, 1024:1088].rearrange("p (c h) -> p c h", c=4),
                     gw2.rearrange("(c p) h -> p c h", p=128))
            w("vector", lv)
            M["gw2"] = op("vector", "sV", lambda E: E.tensor_copy(
                out=k12[:, 0:64], in_=lnar[:, 1024:1088]))
            gtile = gbuf[1][:, 2048:2560]
            av_ev = 0
            gt_use = 0
            al_st = None
            for tt in range(8):
                for c in range(4):
                    bk = c % 2
                    w("tensor", "sV", M[f"gw1c{c}"])
                    w("tensor", "sV", M["gw2"])
                    w("tensor", "sS", ev_bank[bk])
                    for kc in range(8):
                        op("tensor", "sT", lambda E, k=kc, b2=bk, cc=c, t8=tt: E.matmul(
                            out=(psA if b2 == 0 else psB)[:],
                            lhsT=gbuf[3][:, 1024 * cc + 128 * k:1024 * cc + 128 * (k + 1)],
                            rhs=bigs[k][:, 512 * t8:512 * (t8 + 1)],
                            start=(k == 0), stop=(k == 7)))
                    tv = C.get("sT")
                    w("scalar", "sT", tv)
                    w("scalar", "sT", gt_use)
                    sv = op("scalar", "sS", lambda E, b2=bk, cc=c: E.activation(
                        out=gtile[:], in_=(psA if b2 == 0 else psB)[:],
                        func=Act.Gelu, bias=b1c[:, cc:cc + 1], scale=1.0))
                    ev_bank[bk] = sv
                    w("tensor", "sS", sv)
                    if c == 0:
                        w("tensor", "sS", av_ev)
                    op("tensor", "sT", lambda E, cc=c: E.matmul(
                        out=psAV[0:16, :], lhsT=k12[:, 16 * cc:16 * (cc + 1)],
                        rhs=gtile[:], start=(cc == 0), stop=(cc == 3)))
                    gt_use = C.get("sT")
                w("scalar", "sT", gt_use)
                w("scalar", al_st)
                av_ev = op("scalar", "sS", lambda E: E.activation(
                    out=a16[:, 0:512], in_=psAV[0:16, :], func=Act.Sigmoid,
                    bias=b2c[:], scale=1.0))
                w("sync", "sS", av_ev)
                al_st = dma("sync", al_d[:, 512 * tt:512 * (tt + 1)], a16[:, 0:512])
            M["alpha_L"] = snapd()
            M["p3_psAB"] = max(ev_bank[0], ev_bank[1])
            M["p3_psAV"] = av_ev
            M["p3_gtuse"] = gt_use
            if KPH <= 4:
                fin(); return

            last_kq_use = M["p3_gtuse"]
            prb_use = {0: 0, 1: 0}
            v65_use = {0: 0, 1: 0}
            sc_ev = {0: M["p3_psAB"], 1: M["p3_psAB"]}
            ST = {"av": 0, "first": True}
            blend_done = 0
            u_mark = {0: 0, 1: 0}
            scan_use = {0: 0, 1: 0}
            sel_mark = {0: 0, 1: 0}
            red_mark = {0: 0, 1: 0}
            wv_mark = {0: 0, 1: 0}
            wblk_use = 0
            w("sync", M["qkv_L"])
            w("vector", M["qkv_L"])
            w("sync", M["alpha_L"])
            KH = int(os.environ.get("KH", str(H)))
            u_f = [far[:, 0:2048], far[:, 2048:4096]]
            z_f = [far[:, 4096:6144], far[:, 6144:8192]]
            vrep = [gbuf[1][:, 0:2048], gbuf[1][:, 2048:4096]]
            wbf = [gbuf[2][:, 0:2048], gbuf[2][:, 2048:4096]]

            def emit_attn(h, step, alm):
                b, qt, kc = step
                pp = kc % 2
                w("tensor", "sS", sc_ev[pp])
                qkmm = op("tensor", "sT", lambda E, k=kc, q=qt, bb=b, p=pp: E.matmul(
                    out=(psA if p == 0 else psB)[:],
                    lhsT=k12[64 * bb:64 * bb + 64, 128 * k:128 * (k + 1)],
                    rhs=q12[64 * bb:64 * bb + 64, 512 * q:512 * (q + 1)],
                    start=True, stop=True))
                w("scalar", "sT", qkmm)
                w("scalar", "sT", prb_use[pp])
                sv = op("scalar", "sS", lambda E, p=pp: E.activation(
                    out=prb[p][:], in_=(psA if p == 0 else psB)[:],
                    func=Act.Exp, bias=cbc[:, CI_EXPB:CI_EXPB + 1],
                    scale=0.125))
                sc_ev[pp] = sv
                mask_v = 0
                if kc >= 4 * qt:
                    r = 128 * kc - 512 * qt
                    w("vector", "sS", sv)
                    mask_v = opn("vector", "sV", lambda E, p=pp, rr=r: E.tensor_mul(
                        out=prb[p][:], in0=prb[p][:],
                        in1=gbuf[0][:, 512 * (rr // 128):512 * (rr // 128) + 512]))
                w("sync", "sT", v65_use[pp])
                lv = dma("sync", v65[pp][:, 0:64],
                         vtm_d[b, h, 128 * kc:128 * (kc + 1), :])
                w("tensor", lv)
                if mask_v:
                    w("tensor", "sV", mask_v)
                else:
                    w("tensor", "sS", sv)
                if kc == 0:
                    if ST["first"]:
                        w("tensor", "sS", tp_ev[0])
                        ST["first"] = False
                    w("tensor", "sS", ST["av"])
                avm = op("tensor", "sT", lambda E, p=pp, k=kc, q=qt: E.matmul(
                    out=psAV4[0:65, :], lhsT=v65[p][:], rhs=prb[p][:],
                    start=(k == 0), stop=(k == 4 * q + 3)))
                prb_use[pp] = avm
                v65_use[pp] = avm
                if kc == 4 * qt + 3:
                    w("scalar", "sT", avm)
                    if qt == 0 and h >= 1:
                        w("scalar", "sV", M[f"h{h-1}bl{b}"])
                    op("scalar", "sS", lambda E, bb=b, q=qt: E.copy(
                        out=ysm[bb][:, 512 * q:512 * (q + 1)], in_=psAV4[0:64, :]))
                    ST["av"] = op("scalar", "sS", lambda E, bb=b, q=qt: E.copy(
                        out=lnar[64 + 32 * bb:65 + 32 * bb, 512 * q:512 * (q + 1)],
                        in_=psAV4[64:65, :]))
                    if qt == NQT - 1:
                        M[f"h{h}ysmd{b}"] = ST["av"]

            def emit_coeff(h, b, alm):
                w("vector", "sS", M[f"h{h}ysmd{b}"])
                if h >= 1:
                    w("vector", M[f"h{h-1}cst{b}"])
                opq("vector", lambda E, bb=b: E.reciprocal(
                    out=lnar[64 + 32 * bb:65 + 32 * bb, 0:2048],
                    in_=lnar[64 + 32 * bb:65 + 32 * bb, 0:2048]))
                w("vector", alm[b])
                opq("vector", lambda E, r0=64 + 32 * b: E.tensor_scalar(
                    out=outw_sb[r0:r0 + 1, 2048:4096],
                    in0=outw_sb[r0:r0 + 1, 0:2048],
                    scalar1=-1.0, scalar2=1.0, op0=Alu.mult, op1=Alu.add))
                crm = op("vector", "sV", lambda E, r0=64 + 32 * b: E.tensor_mul(
                    out=outw_sb[r0:r0 + 1, 4096:6144],
                    in0=outw_sb[r0:r0 + 1, 0:2048],
                    in1=lnar[r0:r0 + 1, 0:2048]))
                M[f"h{h}cr{b}"] = crm
                if h >= 1:
                    w("sync", M[f"h{h-1}bc{b}"])
                w("sync", "sV", crm)
                st1 = dma("sync", cc_d[2 * b:2 * b + 1, :],
                          outw_sb[64 + 32 * b:65 + 32 * b, 4096:6144])
                st2 = dma("sync", cc_d[2 * b + 1:2 * b + 2, :],
                          outw_sb[64 + 32 * b:65 + 32 * b, 2048:4096])
                M[f"h{h}cst{b}"] = (st1, st2)
                w("sync", st1)
                w("sync", st2)
                if h >= 1:
                    w("sync", "sV", M[f"h{h-1}bl{b}"])
                b1m = dma("sync", dbuf[0:64, 2048 * b:2048 * (b + 1)],
                          cc_d[2 * b:2 * b + 1, :].broadcast_to([64, 2048]))
                b2m = dma("sync", gbuf[3][0:64, 2048 * b:2048 * (b + 1)],
                          cc_d[2 * b + 1:2 * b + 2, :].broadcast_to([64, 2048]))
                M[f"h{h}bc{b}"] = (b1m, b2m)

            def emit_wterm(h, j, stop):
                pj = j % 2
                w("vector", "sV", scan_use[pj])
                w("vector", "sT", red_mark[pj])
                wv_mark[pj] = opn("vector", "sV", lambda E, p=pj, h2=h, jj=j:
                                  E.scalar_tensor_tensor(
                    out=wbf[p][:], in0=z_f[p][:],
                    scalar=g_all[:, 64 * h2 + jj:64 * h2 + jj + 1],
                    in1=q12[:], op0=Alu.mult, op1=Alu.mult))
                w("tensor", "sV", wv_mark[pj])
                w("tensor", "sGE", sel_mark[pj])
                for qt4 in range(4):
                    op("tensor", "sT", lambda E, q=qt4, p=pj, s=stop: E.matmul(
                        out=psY[:, 512 * q:512 * (q + 1)], lhsT=sel[p][:],
                        rhs=wbf[p][:, 512 * q:512 * (q + 1)],
                        start=False, stop=s))
                red_mark[pj] = C.get("sT")

            for h in range(KH):
                w("sync", "sT", last_kq_use)
                kqm = []
                for b in range(BPC):
                    kqm.append(dma("sync", k12[64 * b:64 * b + 64, :], k_d[b, h, :, :]))
                    kqm.append(dma("sync", q12[64 * b:64 * b + 64, :], q_d[b, h, :, :]))
                M[f"h{h}kq"] = tuple(kqm)
                if h == 0:
                    w("sync", "sT", M["p3_gtuse"])
                alm = {}
                for b in range(BPC):
                    if h >= 1:
                        w("sync", "sV", M[f"h{h-1}cr{b}"])
                    alm[b] = dma("sync", outw_sb[64 + 32 * b:65 + 32 * b, 0:2048],
                                 al_d[h:h + 1, 2048 * b:2048 * (b + 1)])
                w("vector", "sT", wblk_use)
                opq("vector", lambda E: E.memset(wblk[:], 0.0))
                opq("vector", lambda E, hh=h: E.tensor_copy(
                    out=wblk[0:64, 0:64], in_=wt2[0:64, 64 * hh:64 * hh + 64]))
                wbv = op("vector", "sV", lambda E, hh=h: E.tensor_copy(
                    out=wblk[64:128, 64:128], in_=wt2[64:128, 64 * hh:64 * hh + 64]))
                if h == 0:
                    opq("vector", lambda E: E.memset(v65[0][:, 64:65], 1.0))
                    opq("vector", lambda E: E.memset(v65[1][:, 64:65], 1.0))
                    w("vector", "sT", M["qkv_T"])
                    w("gpsimd", "sT", M["qkv_T"])
                w("tensor", M[f"h{h}kq"])
                w("tensor", "sV", wbv)
                w("tensor", "sV", blend_done)
                if h == 0:
                    w("tensor", "sS", M["p3_psAV"])
                for qt4 in range(4):
                    op("tensor", "sT", lambda E, q=qt4: E.matmul(
                        out=psY[:, 512 * q:512 * (q + 1)], lhsT=wblk[:],
                        rhs=q12[:, 512 * q:512 * (q + 1)], start=True, stop=False))
                wblk_use = C.get("sT")

                steps = [(b, qt, kc) for b in range(BPC) for qt in range(NQT)
                         for kc in range(4 * qt + 4)]
                for i in range(DH):
                    par = i % 2
                    if h == 0 and i < 2:
                        w("sync", "sT", M["p3_gtuse"])
                    w("sync", "sGE", u_mark[par])
                    vmks = tuple(
                        dma("sync", vrep[par][64 * b:64 * b + 64, :],
                            v_d[b, h, i, :].unsqueeze(0)
                            .broadcast_to([64, 2048])) for b in range(BPC))
                    w("gpsimd", vmks)
                    if i == 0:
                        w("gpsimd", M[f"h{h}kq"])
                    w("gpsimd", "sV", scan_use[par])
                    u_mark[par] = opn("gpsimd", "sGE", lambda E, p=par: E.tensor_mul(
                        out=u_f[p][:], in0=vrep[p][:], in1=k12[:]))
                    w("gpsimd", "sT", red_mark[par])
                    sel_mark[par] = opn(
                        "gpsimd", "sGE", lambda E, ii=i, p=par: E.tensor_single_scalar(
                            out=sel[p][:], in_=iot[:], scalar=float(ii),
                            op=Alu.is_equal))
                    if i >= 1:
                        emit_wterm(h, i - 1, stop=False)
                    w("vector", "sGE", u_mark[par])
                    w("vector", "sV", wv_mark[par])
                    scan_use[par] = opn(
                        "vector", "sV", lambda E, p=par, hh=h, ii=i: E.tensor_tensor_scan(
                            out=z_f[p][:],
                            data0=r_all[:, 64 * hh + ii:64 * hh + ii + 1]
                            .broadcast_to([128, 2048]),
                            data1=u_f[p][:], initial=0.0, op0=Alu.mult, op1=Alu.add))
                    if 2 * i < len(steps):
                        emit_attn(h, steps[2 * i], alm)
                    if 2 * i + 1 < len(steps):
                        emit_attn(h, steps[2 * i + 1], alm)
                    if i == 22:
                        emit_coeff(h, 0, alm)
                    if i == 44:
                        emit_coeff(h, 1, alm)
                emit_wterm(h, DH - 1, stop=True)
                M[f"h{h}scan"] = red_mark[(DH - 1) % 2]
                last_kq_use = red_mark[(DH - 1) % 2]

                for b in range(BPC):
                    w("vector", M[f"h{h}bc{b}"])
                    w("vector", "sT", M[f"h{h}scan"])
                    if h <= 1:
                        w("vector", "sT", M["p3_gtuse"])
                    opq("vector", lambda E, bb=b: E.tensor_mul(
                        out=dbuf[0:64, 2048 * bb:2048 * (bb + 1)], in0=ysm[bb][:],
                        in1=dbuf[0:64, 2048 * bb:2048 * (bb + 1)]))
                    opq("vector", lambda E, bb=b: E.scalar_tensor_tensor(
                        out=gbuf[3][0:64, 2048 * bb:2048 * (bb + 1)],
                        in0=psY[64 * bb:64 * bb + 64, :], scalar=1.0,
                        in1=gbuf[3][0:64, 2048 * bb:2048 * (bb + 1)],
                        op0=Alu.mult, op1=Alu.mult))
                    rows0 = (h % 2) * 64
                    M[f"h{h}bl{b}"] = op(
                        "vector", "sV", lambda E, bb=b, r0=rows0, hh=h: E.tensor_add(
                            out=bigs[hh // 2][r0:r0 + 64, 2048 * bb:2048 * (bb + 1)],
                            in0=dbuf[0:64, 2048 * bb:2048 * (bb + 1)],
                            in1=gbuf[3][0:64, 2048 * bb:2048 * (bb + 1)]))
                blend_done = M[f"h{h}bl{1}"]
            tp_ev[0] = ST["av"]
            M["p4_V"] = C.get("sV")
            M["p4_psAB"] = max(sc_ev[0], sc_ev[1])
            if KPH <= 5:
                if KDBG:
                    w("sync", "sV", M["p4_V"])
                    for c in range(8):
                        dma("sync", g2_d[128 * c:128 * (c + 1), :], bigs[c][:])
                fin(); return

            w("sync", "sV", M["p4_V"])
            for kk in range(8):
                if kk >= 1:
                    w("sync", "sV", M[f"ow{kk-1}"])
                lv = dma("sync", lnar[:, 0:1024], outw[128 * kk:128 * (kk + 1), :])
                w("vector", lv)
                w("vector", "sV", M["p4_V"])
                M[f"ow{kk}"] = op("vector", "sV", lambda E, k=kk: E.tensor_copy(
                    out=outw_sb[:, 1024 * k:1024 * (k + 1)], in_=lnar[:, 0:1024]))
            xres = lnar[:, 0:1024]
            lnb = lnar[:, 1024:2048]
            d2b = lnar[:, 2048:3072]
            u0b = far[:, 0:1024]
            uLb = far[:, 1024:2048]
            xqF = far[:, 2048:3072]
            s_col = colw[:, 2:3]
            am_col = colw[:, 3:4]
            mn_col = colw[:, 4:5]
            vr_col = colw[:, 5:6]
            rs_col = colw[:, 6:7]
            res_done = 0
            x1st = None
            for t in range(NTT):
                w("tensor", "sV", M["ow7"])
                w("tensor", "sS", M["p4_psAB"])
                w("tensor", "sV", res_done)
                for half in range(2):
                    for kk in range(8):
                        op("tensor", "sT", lambda E, k=kk, hf=half, tt=t: E.matmul(
                            out=(psA if hf == 0 else psB)[:],
                            lhsT=bigs[k][:, 128 * tt:128 * (tt + 1)],
                            rhs=outw_sb[:, 1024 * k + 512 * hf:1024 * k + 512 * hf + 512],
                            start=(k == 0), stop=(k == 7)))
                mmv = C.get("sT")
                M[f"p5mm{t}"] = mmv
                if t == 0:
                    w("sync", "sV", M["ow7"])
                lv = dma("sync", xres, x_tok[128 * t:128 * (t + 1), :])
                w("vector", "sT", mmv)
                w("vector", lv)
                w("vector", x1st)
                for half in range(2):
                    opq("vector", lambda E, hf=half: E.scalar_tensor_tensor(
                        out=lnb[:, 512 * hf:512 * hf + 512],
                        in0=(psA if hf == 0 else psB)[:], scalar=1.0,
                        in1=xres[:, 512 * hf:512 * hf + 512],
                        op0=Alu.mult, op1=Alu.add))
                if t == 0:
                    w("vector", M["outb"])
                res_done = op("vector", "sV", lambda E: E.tensor_add(
                    out=lnb[:], in0=lnb[:], in1=outb_bc[:]))
                opq("vector", lambda E: E.reduce_sum(out=mn_col, in_=lnb[:], axis=AxX))
                opq("vector", lambda E: E.tensor_scalar_mul(
                    out=mn_col, in0=mn_col, scalar1=1.0 / D))
                opq("vector", lambda E: E.tensor_single_scalar(
                    out=d2b, in_=lnb[:], scalar=mn_col, op=Alu.subtract))
                opq("vector", lambda E: E.scalar_tensor_tensor(
                    out=lnb[:], in0=d2b, scalar=1.0, in1=d2b,
                    op0=Alu.mult, op1=Alu.mult, accum_out=vr_col))
                vv = op("vector", "sV", lambda E: E.tensor_scalar(
                    out=vr_col, in0=vr_col, scalar1=1.0 / D, scalar2=1e-5,
                    op0=Alu.mult, op1=Alu.add))
                w("gpsimd", "sV", vv)
                gv = op("gpsimd", "sGE", lambda E: E.tensor_tensor(
                    out=rs_col, in0=vr_col, in1=cbc[:, CI_PEXP:CI_PEXP + 1],
                    op=Alu.pow))
                w("vector", "sGE", gv)
                x1v = op("vector", "sV", lambda E: E.tensor_scalar_mul(
                    out=lnb[:], in0=d2b, scalar1=rs_col))
                w("sync", "sV", x1v)
                x1st = dma("sync", x1_d[128 * t:128 * (t + 1), :], lnb[:])
                opq("vector", lambda E: E.reduce_max(
                    out=am_col, in_=lnb[:], axis=AxX, apply_absolute_value=True))
                opq("vector", lambda E: E.tensor_scalar_max(
                    out=am_col, in0=am_col, scalar1=1e-5))
                opq("vector", lambda E: E.reciprocal(out=s_col, in_=am_col))
                opq("vector", lambda E: E.tensor_scalar_mul(
                    out=s_col, in0=s_col, scalar1=127.0))
                opq("vector", lambda E, tt=t: E.scalar_tensor_tensor(
                    out=oscg[:, tt:tt + 1], in0=am_col, scalar=1.0 / 127.0,
                    in1=mcols[:, 0:1], op0=Alu.mult, op1=Alu.mult))
                opq("vector", lambda E, tt=t: E.scalar_tensor_tensor(
                    out=oscu[:, tt:tt + 1], in0=am_col, scalar=1.0 / 127.0,
                    in1=mcols[:, 1:2], op0=Alu.mult, op1=Alu.mult))
                opq("vector", lambda E: E.tensor_scalar(
                    out=u0b, in0=lnb[:], scalar1=s_col, scalar2=128.5,
                    op0=Alu.mult, op1=Alu.add))
                lastv = op("vector", "sV", lambda E: E.tensor_copy(out=uLb, in_=u0b))
                for li, bb in enumerate(LADDER):
                    w("scalar", "sV", lastv)
                    if t == 0 and li == 0:
                        w("scalar", M["wqs2_7"])
                    sv = op("scalar", "sS", lambda E: E.activation(
                        out=dbuf[:, 0:1024], in_=uLb, func=Act.Sign,
                        bias=cbc[:, CI_SGNB:CI_SGNB + 1], scale=1.0))
                    w("vector", "sS", sv)
                    lastv = op("vector", "sV", lambda E, b2=bb: E.scalar_tensor_tensor(
                        out=uLb, in0=dbuf[:, 0:1024], scalar=-b2 / 2.0, in1=uLb,
                        op0=Alu.mult, op1=Alu.add))
                w("vector", "sT", M.get(f"p5tp{t-1}", 0))
                xqv = op("vector", "sV", lambda E: E.scalar_tensor_tensor(
                    out=xqF, in0=uLb, scalar=-1.0, in1=u0b,
                    op0=Alu.mult, op1=Alu.add))
                w("tensor", "sV", xqv)
                for c in range(8):
                    pp = c % 2
                    w("tensor", "sS", tp_ev[pp])
                    tv = op("tensor", "sT", lambda E, cc=c, p=pp: E.transpose(
                        out=psTF[p][:, 0:128],
                        in_=xqF[:, 128 * cc:128 * cc + 128],
                        identity=identF[:]))
                    w("scalar", "sT", tv)
                    tp_ev[pp] = op("scalar", "sS", lambda E, cc=c, tt=t, p=pp: E.activation(
                        out=bigs[cc][:, 128 * tt:128 * tt + 128],
                        in_=psTF[p][:, 0:128], func=Act.Identity,
                        bias=cbc[:, CI_NHALF:CI_NHALF + 1], scale=1.0))
                M[f"p5tp{t}"] = C.get("sT")
                M[f"p5q{t}"] = C.get("sS")
            M["p5_S"] = C.get("sS")
            M["p5_V"] = C.get("sV")
            if KPH <= 6:
                fin(); return

            wgv = [gbuf[1][:, 0:2048], gbuf[1][:, 2048:4096]]
            wuv = [gbuf[2][:, 0:2048], gbuf[2][:, 2048:4096]]
            h1t = gbuf[3][:, 0:256]
            h2t = gbuf[3][:, 256:512]
            g2t = [gbuf[3][:, 512:768], gbuf[3][:, 768:1024]]
            opq("vector", lambda E: E.memset(amax2[:], 0.0))
            evA = M["p4_psAB"]
            evB = M["p4_psAB"]
            h1_use = 0
            wblk_use = {0: 0, 1: 0}
            g2st = {0: 0, 1: 0}
            for nb in range(FI // 256):
                par = nb % 2
                if nb == 0:
                    w("sync", "sT", M[f"h{H-1}scan"])
                    w("sync", "sV", M["p4_V"])
                w("sync", "sT", wblk_use[par])
                lg = dma("sync",
                         wgv[par].rearrange("p (c o) -> p c o", c=8),
                         wqg_d[:, 256 * nb:256 * (nb + 1)]
                         .rearrange("(c p) o -> p c o", p=128))
                lu = dma("sync",
                         wuv[par].rearrange("p (c o) -> p c o", c=8),
                         wqu_d[:, 256 * nb:256 * (nb + 1)]
                         .rearrange("(c p) o -> p c o", p=128))
                for tt in range(NTT):
                    w("tensor", lg)
                    w("tensor", lu)
                    if nb == 0 and tt == 0:
                        w("tensor", "sS", M["p5_S"])
                        w("tensor", "sV", res_done)
                    w("tensor", "sS", evA)
                    w("tensor", "sV", evB)
                    for kk in range(8):
                        op("tensor", "sT", lambda E, k=kk, p=par, t2=tt: E.matmul(
                            out=psA[:, 0:256],
                            lhsT=bigs[k][:, 128 * t2:128 * (t2 + 1)],
                            rhs=wgv[p][:, 256 * k:256 * (k + 1)],
                            start=(k == 0), stop=(k == 7)))
                        op("tensor", "sT", lambda E, k=kk, p=par, t2=tt: E.matmul(
                            out=psB[:, 0:256],
                            lhsT=bigs[k][:, 128 * t2:128 * (t2 + 1)],
                            rhs=wuv[p][:, 256 * k:256 * (k + 1)],
                            start=(k == 0), stop=(k == 7)))
                    mmv = C.get("sT")
                    wblk_use[par] = mmv
                    w("scalar", "sT", mmv)
                    if nb == 0 and tt == 0:
                        w("scalar", "sV", M["p5_V"])
                    w("scalar", "sV", h1_use)
                    evA = op("scalar", "sS", lambda E, t2=tt: E.activation(
                        out=h1t[:], in_=psA[:, 0:256], func=Act.Silu,
                        scale=oscg[:, t2:t2 + 1]))
                    w("vector", "sS", evA)
                    w("vector", "sT", mmv)
                    w("vector", g2st[tt % 2])
                    opq("vector", lambda E, t2=tt: E.tensor_scalar_mul(
                        out=h2t[:], in0=psB[:, 0:256], scalar1=oscu[:, t2:t2 + 1]))
                    gv = op("vector", "sV", lambda E, p2=tt % 2: E.tensor_mul(
                        out=g2t[p2][:], in0=h1t[:], in1=h2t[:]))
                    evB = gv
                    h1_use = gv
                    opq("vector", lambda E, p2=tt % 2: E.reduce_max(
                        out=tmp_col, in_=g2t[p2][:], axis=AxX,
                        apply_absolute_value=True))
                    av = op("vector", "sV", lambda E, t2=tt: E.tensor_max(
                        out=amax2[:, t2:t2 + 1], in0=amax2[:, t2:t2 + 1],
                        in1=tmp_col))
                    w("sync", "sV", av)
                    g2st[tt % 2] = dma(
                        "sync", g2_d[128 * tt:128 * (tt + 1), 256 * nb:256 * (nb + 1)],
                        g2t[tt % 2][:])
            M["p6_T"] = C.get("sT")
            M["p6_V"] = C.get("sV")
            M["p6_L"] = snapd()
            if KPH <= 7:
                fin(); return

            w("sync", "sT", M["p6_T"])
            w("sync", M["p6_L"])
            for kk in range(32):
                lv = dma("sync", bigs[kk // 4][:, 1024 * (kk % 4):1024 * (kk % 4) + 1024],
                         wqd_d[128 * kk:128 * (kk + 1), :])
            M["wdT"] = snapd()
            g2l = [gbuf[1][:, 0:4096], gbuf[2][:, 0:4096]]
            gqb = gbuf[3][:, 0:4096]
            u0w = far[:, 0:4096]
            uLw = far[:, 4096:8192]
            s2_col = colw[:, 8:9]
            oscd = colw[:, 9:10]
            u0_done = {0: 0, 1: 0}
            gq_used = 0
            res7 = 0
            psT7 = M["p5_S"]
            for t in range(NTT):
                par = t % 2
                w("sync", "sV", u0_done[par])
                lv = dma("sync", g2l[par][:], g2_d[128 * t:128 * (t + 1), :])
                w("vector", lv)
                if t == 0:
                    w("vector", "sV", M["p6_V"])
                opq("vector", lambda E, t2=t: E.tensor_scalar_max(
                    out=am_col, in0=amax2[:, t2:t2 + 1], scalar1=1e-5))
                opq("vector", lambda E: E.reciprocal(out=s2_col, in_=am_col))
                opq("vector", lambda E: E.tensor_scalar_mul(
                    out=s2_col, in0=s2_col, scalar1=127.0))
                opq("vector", lambda E: E.scalar_tensor_tensor(
                    out=oscd, in0=am_col, scalar=1.0 / 127.0,
                    in1=mcols[:, 2:3], op0=Alu.mult, op1=Alu.mult))
                w("vector", "sT", M.get(f"p7tp{t-1}", 0))
                u0v = op("vector", "sV", lambda E, p=par: E.tensor_scalar(
                    out=u0w, in0=g2l[p][:], scalar1=s2_col, scalar2=128.5,
                    op0=Alu.mult, op1=Alu.add))
                u0_done[par] = u0v
                lastv = op("vector", "sV", lambda E: E.tensor_copy(out=uLw, in_=u0w))
                for bb in LADDER:
                    w("scalar", "sV", lastv)
                    if t == 0 and bb == LADDER[0]:
                        w("scalar", M["wdT"])
                    sv = op("scalar", "sS", lambda E: E.activation(
                        out=dbuf[:], in_=uLw, func=Act.Sign,
                        bias=cbc[:, CI_SGNB:CI_SGNB + 1], scale=1.0))
                    w("vector", "sS", sv)
                    lastv = op("vector", "sV", lambda E, b2=bb: E.scalar_tensor_tensor(
                        out=uLw, in0=dbuf[:], scalar=-b2 / 2.0, in1=uLw,
                        op0=Alu.mult, op1=Alu.add))
                xqv = op("vector", "sV", lambda E: E.scalar_tensor_tensor(
                    out=u0w, in0=uLw, scalar=-1.0, in1=u0w,
                    op0=Alu.mult, op1=Alu.add))
                w("tensor", "sV", xqv)
                for c in range(32):
                    pp = c % 2
                    w("tensor", "sS", tp_ev[pp])
                    tv = op("tensor", "sT", lambda E, cc=c, p=pp: E.transpose(
                        out=psTF[p][:, 0:128],
                        in_=u0w[:, 128 * cc:128 * cc + 128], identity=identF[:]))
                    w("scalar", "sT", tv)
                    if c == 0:
                        w("scalar", "sT", gq_used)
                        if t == 0:
                            w("scalar", "sV", M["p6_V"])
                            w("scalar", M["p6_L"])
                    tp_ev[pp] = op("scalar", "sS", lambda E, cc=c, p=pp: E.activation(
                        out=gqb[:, 128 * cc:128 * cc + 128],
                        in_=psTF[p][:, 0:128],
                        func=Act.Identity, bias=cbc[:, CI_NHALF:CI_NHALF + 1],
                        scale=1.0))
                M[f"p7tp{t}"] = C.get("sT")
                psT7 = C.get("sS")
                w("tensor", "sS", psT7)
                w("tensor", M["wdT"])
                w("tensor", "sV", res7)
                for kk in range(32):
                    op("tensor", "sT", lambda E, k=kk: E.matmul(
                        out=psA[:], lhsT=gqb[:, 128 * k:128 * (k + 1)],
                        rhs=bigs[k // 4][:, 1024 * (k % 4):1024 * (k % 4) + 512],
                        start=(k == 0), stop=(k == 31)))
                    op("tensor", "sT", lambda E, k=kk: E.matmul(
                        out=psB[:], lhsT=gqb[:, 128 * k:128 * (k + 1)],
                        rhs=bigs[k // 4][:, 1024 * (k % 4) + 512:1024 * (k % 4) + 1024],
                        start=(k == 0), stop=(k == 31)))
                gq_used = C.get("sT")
                lv = dma("sync", xres, x1_d[128 * t:128 * (t + 1), :])
                w("vector", "sT", gq_used)
                w("vector", lv)
                if t >= 1:
                    w("vector", "sO", 16 * t)
                for half in range(2):
                    opq("vector", lambda E, hf=half: E.tensor_scalar_mul(
                        out=d2b[:, 512 * hf:512 * hf + 512],
                        in0=(psA if hf == 0 else psB)[:], scalar1=oscd))
                res7 = op("vector", "sV", lambda E: E.tensor_add(
                    out=lnb[:], in0=d2b, in1=xres))
                opq("vector", lambda E: E.reduce_sum(out=mn_col, in_=lnb[:], axis=AxX))
                opq("vector", lambda E: E.tensor_scalar_mul(
                    out=mn_col, in0=mn_col, scalar1=1.0 / D))
                opq("vector", lambda E: E.tensor_single_scalar(
                    out=d2b, in_=lnb[:], scalar=mn_col, op=Alu.subtract))
                opq("vector", lambda E: E.scalar_tensor_tensor(
                    out=lnb[:], in0=d2b, scalar=1.0, in1=d2b,
                    op0=Alu.mult, op1=Alu.mult, accum_out=vr_col))
                vv = op("vector", "sV", lambda E: E.tensor_scalar(
                    out=vr_col, in0=vr_col, scalar1=1.0 / D, scalar2=1e-5,
                    op0=Alu.mult, op1=Alu.add))
                w("gpsimd", "sV", vv)
                gv = op("gpsimd", "sGE", lambda E: E.tensor_tensor(
                    out=rs_col, in0=vr_col, in1=cbc[:, CI_PEXP:CI_PEXP + 1],
                    op=Alu.pow))
                w("vector", "sGE", gv)
                ov = op("vector", "sV", lambda E: E.tensor_scalar_mul(
                    out=lnb[:], in0=d2b, scalar1=rs_col))
                w("sync", "sV", ov)
                dma("sync", out_tok[128 * t:128 * (t + 1), :], lnb[:], sem="sO")
            w("sync", "sO", C.get("sO"))

        blk.sync(lambda E: program(E, "sync"))
        blk.vector(lambda E: program(E, "vector"))
        blk.scalar(lambda E: program(E, "scalar"))
        blk.tensor(lambda E: program(E, "tensor"))
        blk.gpsimd(lambda E: program(E, "gpsimd"))
    return nc


def _host_consts():
    kk = np.arange(128)[:, None]
    qq = np.arange(512)[None, :]
    maskbig = np.concatenate(
        [(kk <= qq - r).astype(ml_dtypes.bfloat16) for r in (0, 128, 256, 384)],
        axis=1)
    ident = np.eye(128, dtype=ml_dtypes.bfloat16)
    p = np.arange(128)[:, None]
    m = np.arange(128)[None, :]
    iot = (m - 64 * (p // 64)).astype(ml_dtypes.bfloat16)
    return maskbig, ident, iot


def build_shared(L0_attn_Wqkv_w, L0_attn_Wqkv_b, L0_attn_out_w,
                 L0_attn_out_b, L0_attn_W_static, L0_attn_Lambda_raw,
                 L0_attn_Gamma, L0_attn_gate_w1, L0_attn_gate_b1,
                 L0_attn_gate_w2, L0_attn_gate_b2, L0_ffn_w_gate,
                 L0_ffn_w_up, L0_ffn_w_down):
    maskbig, ident, iot = _host_consts()
    shared = {
        "wqkv": np.ascontiguousarray(L0_attn_Wqkv_w, dtype=np.float32),
        "bqkv": np.ascontiguousarray(L0_attn_Wqkv_b, dtype=np.float32),
        "outw": np.ascontiguousarray(L0_attn_out_w, dtype=np.float32),
        "outb": np.ascontiguousarray(L0_attn_out_b, dtype=np.float32),
        "wst": np.ascontiguousarray(L0_attn_W_static, dtype=np.float32),
        "lam": np.ascontiguousarray(L0_attn_Lambda_raw, dtype=np.float32),
        "gam": np.ascontiguousarray(L0_attn_Gamma, dtype=np.float32),
        "gw1": np.ascontiguousarray(L0_attn_gate_w1, dtype=np.float32),
        "gb1": np.ascontiguousarray(L0_attn_gate_b1, dtype=np.float32),
        "gw2": np.ascontiguousarray(L0_attn_gate_w2, dtype=np.float32),
        "gb2": np.ascontiguousarray(L0_attn_gate_b2, dtype=np.float32),
        "wgate": np.ascontiguousarray(L0_ffn_w_gate, dtype=np.float32),
        "wup": np.ascontiguousarray(L0_ffn_w_up, dtype=np.float32),
        "wdown": np.ascontiguousarray(L0_ffn_w_down, dtype=np.float32),
        "maskc": maskbig, "identc": ident, "ioc": iot,
        "identf": np.eye(128, dtype=np.float32),
        "cvec": CONSTS.reshape(1, 8).copy(),
        "bqkv_r": np.ascontiguousarray(
            np.asarray(L0_attn_Wqkv_b, dtype=np.float32).reshape(24, 128).T),
        "gb1_r": np.ascontiguousarray(
            np.asarray(L0_attn_gate_b1, dtype=np.float32).reshape(4, 128).T),
        "lamT": np.ascontiguousarray(
            np.asarray(L0_attn_Lambda_raw, dtype=np.float32)
            .transpose(2, 0, 1).reshape(64, 1024)),
        "gamT": np.ascontiguousarray(
            np.asarray(L0_attn_Gamma, dtype=np.float32)
            .transpose(2, 0, 1).reshape(64, 1024)),
        "wstT": np.ascontiguousarray(
            np.asarray(L0_attn_W_static, dtype=np.float32)
            .transpose(2, 0, 1).reshape(64, 1024)),
    }
    return shared


def kernel(L0_attn_x, L0_attn_Wqkv_w, L0_attn_Wqkv_b, L0_attn_out_w,
           L0_attn_out_b, L0_attn_W_static, L0_attn_Lambda_raw, L0_attn_Gamma,
           L0_attn_gate_w1, L0_attn_gate_b1, L0_attn_gate_w2, L0_attn_gate_b2,
           L0_ffn_w_gate, L0_ffn_w_up, L0_ffn_w_down, _n_cores=NCORES):
    shared = build_shared(
        L0_attn_Wqkv_w, L0_attn_Wqkv_b, L0_attn_out_w, L0_attn_out_b,
        L0_attn_W_static, L0_attn_Lambda_raw, L0_attn_Gamma, L0_attn_gate_w1,
        L0_attn_gate_b1, L0_attn_gate_w2, L0_attn_gate_b2, L0_ffn_w_gate,
        L0_ffn_w_up, L0_ffn_w_down)
    nc = build_nc()
    x = np.ascontiguousarray(np.asarray(L0_attn_x, dtype=np.float32))
    in_maps = []
    for c in range(_n_cores):
        m = dict(shared)
        m["x"] = np.ascontiguousarray(x[BPC * c:BPC * (c + 1)])
        in_maps.append(m)
    res = run_bass_kernel_spmd(nc, in_maps, core_ids=list(range(_n_cores)))
    return np.concatenate([r["out"] for r in res.results], axis=0)


if __name__ == "__main__":
    nc = build_nc()
    print("build ok; instructions:", nc.n_instructions())


# version 05 (final)
# speedup vs baseline: 1.2463x; 1.1621x over previous
import os
import numpy as np
import ml_dtypes
import concourse.bass as bass
import concourse.mybir as mybir
from concourse.bass_utils import run_bass_kernel_spmd
from contextlib import ExitStack

F32 = mybir.dt.float32
BF16 = mybir.dt.bfloat16
Alu = mybir.AluOpType
Act = mybir.ActivationFunctionType
AxX = mybir.AxisListType.X

KPH = int(os.environ.get("KPH", "8"))
KDBG = os.environ.get("KDBG", "0") == "1"

NCORES = 8
BPC = 2
L = 2048
D = 1024
H = 16
DH = 64
T = BPC * L
NTT = T // 128
G1 = 512
FI = 4096
NQT = L // 512
NKC = L // 128

LADDER = [128.0, 64.0, 32.0, 16.0, 8.0, 4.0, 2.0, 1.0]

CI_EXPB, CI_SGNB, CI_NHALF, CI_ONE, CI_PEXP = 0, 1, 2, 3, 4
CONSTS = np.zeros(8, np.float32)
CONSTS[CI_EXPB] = -10.0
CONSTS[CI_SGNB] = -128.0
CONSTS[CI_NHALF] = -0.5
CONSTS[CI_ONE] = 1.0
CONSTS[CI_PEXP] = -0.5


class Ctr:
    def __init__(self):
        self.n = {}
        self.m = {}

    def inc(self, name, k=1):
        self.n[name] = self.n.get(name, 0) + k
        return self.n[name]

    def get(self, name):
        return self.n.get(name, 0)


def build_nc():
    nc = bass.Bass("TRN2", target_bir_lowering=False)

    x_in = nc.dram_tensor("x", [BPC, L, D], F32, kind="ExternalInput").ap()
    wqkv = nc.dram_tensor("wqkv", [D, 3 * D], F32, kind="ExternalInput").ap()
    bqkv = nc.dram_tensor("bqkv", [3 * D], F32, kind="ExternalInput").ap()
    outw = nc.dram_tensor("outw", [D, D], F32, kind="ExternalInput").ap()
    outb = nc.dram_tensor("outb", [D], F32, kind="ExternalInput").ap()
    wst = nc.dram_tensor("wst", [H, DH, DH], F32, kind="ExternalInput").ap()
    lam = nc.dram_tensor("lam", [H, DH, DH], F32, kind="ExternalInput").ap()
    gam = nc.dram_tensor("gam", [H, DH, DH], F32, kind="ExternalInput").ap()
    gw1 = nc.dram_tensor("gw1", [D, G1], F32, kind="ExternalInput").ap()
    gb1 = nc.dram_tensor("gb1", [G1], F32, kind="ExternalInput").ap()
    gw2 = nc.dram_tensor("gw2", [G1, H], F32, kind="ExternalInput").ap()
    gb2 = nc.dram_tensor("gb2", [H], F32, kind="ExternalInput").ap()
    wgate = nc.dram_tensor("wgate", [FI, D], F32, kind="ExternalInput").ap()
    wup = nc.dram_tensor("wup", [FI, D], F32, kind="ExternalInput").ap()
    wdown = nc.dram_tensor("wdown", [D, FI], F32, kind="ExternalInput").ap()
    maskc = nc.dram_tensor("maskc", [128, 2048], BF16, kind="ExternalInput").ap()
    selc = nc.dram_tensor("selc", [128, 192], BF16, kind="ExternalInput").ap()
    bqkv_r = nc.dram_tensor("bqkv_r", [128, 24], F32, kind="ExternalInput").ap()
    gb1_r = nc.dram_tensor("gb1_r", [128, 4], F32, kind="ExternalInput").ap()
    lamT = nc.dram_tensor("lamT", [64, 1024], F32, kind="ExternalInput").ap()
    gamT = nc.dram_tensor("gamT", [64, 1024], F32, kind="ExternalInput").ap()
    wstT = nc.dram_tensor("wstT", [64, 1024], F32, kind="ExternalInput").ap()
    ident_d = nc.dram_tensor("identc", [128, 128], BF16, kind="ExternalInput").ap()
    identf_d = nc.dram_tensor("identf", [128, 128], F32, kind="ExternalInput").ap()
    io_d = nc.dram_tensor("ioc", [128, 128], BF16, kind="ExternalInput").ap()
    cvec = nc.dram_tensor("cvec", [1, 8], F32, kind="ExternalInput").ap()
    out = nc.dram_tensor("out", [BPC, L, D], F32, kind="ExternalOutput").ap()

    _K = "ExternalOutput" if KDBG else "Internal"
    q_d = nc.dram_tensor("q_d", [BPC, H, DH, L], BF16, kind=_K).ap()
    k_d = nc.dram_tensor("k_d", [BPC, H, DH, L], BF16, kind=_K).ap()
    v_d = nc.dram_tensor("v_d", [BPC, H, DH, L], BF16, kind=_K).ap()
    vtm_d = nc.dram_tensor("vtm_d", [BPC, H, L, DH], BF16, kind=_K).ap()
    al_d = nc.dram_tensor("al_d", [H, T], BF16, kind=_K).ap()
    den_d = nc.dram_tensor("den_d", [BPC, H, L], F32, kind=_K).ap()
    x1_d = nc.dram_tensor("x1_d", [T, D], F32, kind=_K).ap()
    cc_d = nc.dram_tensor("cc_d", [4, 2048], BF16, kind=_K).ap()
    g2_d = nc.dram_tensor("g2_d", [T, FI], BF16, kind=_K).ap()
    wqg_d = nc.dram_tensor("wqg_d", [D, FI], BF16, kind=_K).ap()
    wqu_d = nc.dram_tensor("wqu_d", [D, FI], BF16, kind=_K).ap()
    wqd_d = nc.dram_tensor("wqd_d", [FI, D], BF16, kind=_K).ap()
    m_d = nc.dram_tensor("m_d", [1, 4], F32, kind=_K).ap()

    x_tok = x_in.rearrange("b l d -> (b l) d")
    out_tok = out.rearrange("b l d -> (b l) d")

    with ExitStack() as es:
        e = es.enter_context

        bigs = [e(nc.sbuf_tensor(f"big{i}", [128, 4096], BF16)) for i in range(8)]
        gbuf = [e(nc.sbuf_tensor(f"gwk{i}", [128, 4096], BF16)) for i in range(4)]
        far = e(nc.sbuf_tensor("far", [128, 8192], F32))
        dbuf = e(nc.sbuf_tensor("dbuf", [128, 4096], BF16))
        lnar = e(nc.sbuf_tensor("lnar", [128, 3072], F32))
        outw_sb = e(nc.sbuf_tensor("outw_sb", [128, 8192], BF16))
        ident = e(nc.sbuf_tensor("ident", [128, 128], BF16))
        selb = e(nc.sbuf_tensor("selb", [128, 192], BF16))
        identF = e(nc.sbuf_tensor("identF", [128, 128], F32))
        iot = e(nc.sbuf_tensor("iot", [128, 128], BF16))
        sel = [e(nc.sbuf_tensor(f"sel{i}", [128, 128], BF16)) for i in range(2)]
        r_all = e(nc.sbuf_tensor("r_all", [128, 1024], F32))
        g_all = e(nc.sbuf_tensor("g_all", [128, 1024], F32))
        wt2 = e(nc.sbuf_tensor("wt2", [128, 1024], BF16))
        wblk = e(nc.sbuf_tensor("wblk", [128, 128], BF16))
        outb_bc = e(nc.sbuf_tensor("outb_bc", [128, 1024], BF16))
        k12 = e(nc.sbuf_tensor("k12", [128, 2048], BF16))
        q12 = e(nc.sbuf_tensor("q12", [128, 2048], BF16))
        ysm = [e(nc.sbuf_tensor(f"ysm{i}", [64, 2048], BF16)) for i in range(2)]
        prb = [e(nc.sbuf_tensor(f"prb{i}", [128, 512], BF16)) for i in range(2)]
        v65 = [e(nc.sbuf_tensor(f"v65{i}", [128, 65], BF16)) for i in range(2)]
        cbc = e(nc.sbuf_tensor("cbc", [128, 8], F32))
        qb_col = e(nc.sbuf_tensor("qb_col", [128, 24], F32))
        b1c = e(nc.sbuf_tensor("b1c", [128, 4], F32))
        b2c = e(nc.sbuf_tensor("b2c", [16, 1], F32))
        mcols = e(nc.sbuf_tensor("mcols", [128, 8], F32))
        a16 = e(nc.sbuf_tensor("a16", [16, 512], BF16))
        amax2 = e(nc.sbuf_tensor("amax2", [128, 32], F32))
        oscg = e(nc.sbuf_tensor("oscg", [128, 32], F32))
        oscu = e(nc.sbuf_tensor("oscu", [128, 32], F32))
        colw = e(nc.sbuf_tensor("colw", [128, 16], F32))

        psA = e(nc.psum_tensor("psA", [128, 512], F32))
        psB = e(nc.psum_tensor("psB", [128, 512], F32))
        psY = e(nc.psum_tensor("psY", [128, 2048], F32))
        psAV = psY[:, 0:512]
        psTF = [e(nc.psum_tensor("psTF0", [128, 512], F32)),
                e(nc.psum_tensor("psTF1", [128, 512], F32))]
        psAV4 = psTF[0]

        SEM = {}
        SYNC_POOL = [f"sl{i}" for i in range(12)]
        GP_POOL = [f"sg{i}" for i in range(4)]
        for s in SYNC_POOL + GP_POOL + ["sV", "sS", "sT", "sGE", "sO"]:
            SEM[s] = e(nc.semaphore(s))
        POOLS = {"sync": SYNC_POOL, "gpsimd": GP_POOL}

        blk = e(nc.Block())

        def program(eng, role):
            C = Ctr()
            M = C.m

            def dma(q, outap, inap, sem=None, transpose=False):
                if sem == "sO":
                    v = C.inc("sO", 16)
                    if role == q:
                        eng.dma_start(out=outap, in_=inap).then_inc(SEM["sO"], 16)
                    return v
                pool = POOLS[q]
                idx = C.inc(f"_rr_{q}") % len(pool)
                sm = pool[idx]
                prev = C.get(sm)
                v = C.inc(sm, 16)
                if role == q:
                    if prev > 0:
                        eng.wait_ge(SEM[sm], prev)
                    if transpose:
                        eng.dma_start_transpose(out=outap, in_=inap).then_inc(SEM[sm], 16)
                    else:
                        eng.dma_start(out=outap, in_=inap).then_inc(SEM[sm], 16)
                return (sm, v)

            def snapd():
                return tuple((sm, C.get(sm))
                             for sm in SYNC_POOL + GP_POOL if C.get(sm) > 0)

            def op(r, sem, fn):
                v = C.inc(sem, 1)
                if role == r:
                    fn(eng).then_inc(SEM[sem], 1)
                    if r in ("vector", "gpsimd"):
                        eng.wait_ge(SEM[sem], v)
                return v

            def opn(r, sem, fn):
                v = C.inc(sem, 1)
                if role == r:
                    fn(eng).then_inc(SEM[sem], 1)
                return v

            def opq(r, fn):
                if r in ("vector", "gpsimd"):
                    sem = "sV" if r == "vector" else "sGE"
                    v = C.inc(sem, 1)
                    if role == r:
                        fn(eng).then_inc(SEM[sem], 1)
                        eng.wait_ge(SEM[sem], v)
                else:
                    if role == r:
                        fn(eng)

            def w(r, sem, v=None):
                if role != r:
                    return
                if v is None:
                    mk = sem
                    if not mk:
                        return
                    if isinstance(mk, tuple) and mk and isinstance(mk[0], tuple):
                        for sm, vv in mk:
                            eng.wait_ge(SEM[sm], vv)
                    elif isinstance(mk, tuple):
                        eng.wait_ge(SEM[mk[0]], mk[1])
                else:
                    if v and v > 0:
                        eng.wait_ge(SEM[sem], v)

            def fin():
                if role == "sync":
                    for sm in SYNC_POOL:
                        if C.get(sm) > 0:
                            eng.wait_ge(SEM[sm], C.get(sm))
                    if C.get("sO") > 0:
                        eng.wait_ge(SEM["sO"], C.get("sO"))
                if role == "gpsimd":
                    for sm in GP_POOL:
                        if C.get(sm) > 0:
                            eng.wait_ge(SEM[sm], C.get(sm))

            tp_ev = {0: 0, 1: 0}

            dma("sync", gbuf[0][:, 0:2048], maskc)
            dma("sync", selb[:], selc)
            dma("sync", ident[:], ident_d)
            dma("sync", identF[:], identf_d)
            dma("sync", iot[:], io_d)
            dma("sync", cbc[:], cvec.broadcast_to([128, 8]))
            dma("sync", qb_col[:], bqkv_r)
            dma("sync", b1c[:], gb1_r)
            dma("sync", b2c[:], gb2.unsqueeze(1))
            M["outb"] = dma("gpsimd", outb_bc[:], outb.unsqueeze(0).broadcast_to([128, 1024]))
            dma("sync", r_all[0:64, :], lamT)
            dma("sync", r_all[64:128, :], lamT)
            dma("sync", g_all[0:64, :], gamT)
            dma("sync", g_all[64:128, :], gamT)
            dma("sync", lnar[0:64, 0:1024], wstT)
            dma("sync", lnar[64:128, 0:1024], wstT)
            M["p0"] = snapd()
            w("scalar", M["p0"])
            vsg = op("scalar", "sS", lambda E: E.activation(
                out=r_all[:], in_=r_all[:], func=Act.Sigmoid))
            w("vector", "sS", vsg)
            opq("vector", lambda E: E.tensor_scalar(
                out=r_all[:], in0=r_all[:], scalar1=-1.0, scalar2=1.0,
                op0=Alu.mult, op1=Alu.add))
            opq("vector", lambda E: E.tensor_scalar_mul(
                out=g_all[:], in0=g_all[:], scalar1=0.125))
            w("vector", M["p0"])
            M["rg"] = op("vector", "sV", lambda E: E.tensor_copy(
                out=wt2[:], in_=lnar[:, 0:1024]))
            if KPH <= 0:
                fin(); return

            wtile = far[:, 0:4096]
            p1b = gbuf[1]
            n1b = gbuf[2]
            wqbF = far[:, 4096:8192]
            acc_col = colw[:, 0:1]
            tmp_col = colw[:, 1:2]

            KWS = int(os.environ.get("KWS", "3"))
            def wq_phase(widx, w_ap, wq_ap_d, nrows, ncols):
                nt = nrows // 128
                nel = nrows * ncols
                opq("vector", lambda E: E.memset(acc_col, 0.0))
                last_rd = 0
                for t in range(nt):
                    if t >= 1:
                        w("sync", "sV", M[f"wqa{widx}_{t-1}"])
                    lv = dma("sync", wtile[:, 0:ncols], w_ap[128 * t:128 * (t + 1), :])
                    w("vector", lv)
                    opq("vector", lambda E: E.reduce_sum(
                        out=tmp_col, in_=wtile[:, 0:ncols], axis=AxX,
                        apply_absolute_value=True))
                    last_rd = op("vector", "sV", lambda E: E.tensor_add(
                        out=acc_col, in0=acc_col, in1=tmp_col))
                    M[f"wqa{widx}_{t}"] = last_rd
                    if KDBG and widx == 0:
                        w("sync", "sV", last_rd)
                        snp = dma("sync", den_d[1, 1 + t // 16, (t % 16) * 128:
                                  (t % 16) * 128 + 128], tmp_col)
                        w("vector", snp)
                if KWS <= 1:
                    return
                w("tensor", "sV", last_rd)
                w("tensor", M["p0"])
                if widx > 0:
                    w("tensor", "sV", M[f"wqm{widx-1}"])
                tv = op("tensor", "sT", lambda E: E.matmul(
                    out=psA[0:1, 0:1], lhsT=acc_col, rhs=cbc[:, CI_ONE:CI_ONE + 1],
                    start=True, stop=True))
                w("vector", "sT", tv)
                mv = op("vector", "sV", lambda E: E.tensor_scalar(
                    out=tmp_col[0:1, :], in0=psA[0:1, 0:1],
                    scalar1=1.0 / nel, scalar2=1e-5, op0=Alu.mult, op1=Alu.max))
                M[f"wqm{widx}"] = mv
                w("sync", "sV", mv)
                stm = dma("sync", m_d[0:1, widx:widx + 1], tmp_col[0:1, :])
                w("sync", stm)
                lv = dma("sync", mcols[:, widx:widx + 1],
                         m_d[0:1, widx:widx + 1].broadcast_to([128, 1]))
                w("vector", lv)
                iv = op("vector", "sV", lambda E: E.reciprocal(
                    out=mcols[:, 4 + widx:5 + widx], in_=mcols[:, widx:widx + 1]))
                if KDBG and widx == 0:
                    dma("sync", den_d[0, 0, 0:128], acc_col)
                    dma("sync", den_d[0, 0, 128:256], cbc[:, CI_ONE:CI_ONE + 1])
                if KWS <= 2:
                    return
                nblk = ncols // 128
                KWT = int(os.environ.get("KWT", "99"))
                for t in range(nt):
                    if t >= KWT:
                        continue
                    if t >= 1:
                        w("sync", "sV", M[f"wqv{widx}_{t-1}"])
                    lv = dma("sync", wtile[:, 0:ncols], w_ap[128 * t:128 * (t + 1), :])
                    w("vector", lv)
                    w("vector", "sV", iv)
                    if t >= 1:
                        w("vector", "sT", M[f"wqt{widx}_{t-1}"])
                    opq("vector", lambda E: E.tensor_scalar(
                        out=p1b[:, 0:ncols], in0=wtile[:, 0:ncols],
                        scalar1=mcols[:, 4 + widx:5 + widx], scalar2=0.5,
                        op0=Alu.mult, op1=Alu.is_gt))
                    opq("vector", lambda E: E.tensor_scalar(
                        out=n1b[:, 0:ncols], in0=wtile[:, 0:ncols],
                        scalar1=mcols[:, 4 + widx:5 + widx], scalar2=-0.5,
                        op0=Alu.mult, op1=Alu.is_lt))
                    vv = op("vector", "sV", lambda E: E.tensor_sub(
                        out=wqbF[:, 0:ncols], in0=p1b[:, 0:ncols], in1=n1b[:, 0:ncols]))
                    M[f"wqv{widx}_{t}"] = vv
                    w("tensor", "sV", vv)
                    for cb in range(nblk):
                        pp = cb % 2
                        w("tensor", "sS", tp_ev[pp])
                        tv = op("tensor", "sT", lambda E, c=cb, p=pp: E.transpose(
                            out=psTF[p][:, 0:128],
                            in_=wqbF[:, 128 * c:128 * c + 128], identity=identF[:]))
                        w("scalar", "sT", tv)
                        if cb == 0:
                            prev = (M.get(f"wqs{widx}_{t-1}") if t >= 1
                                    else M.get(f"wqs{widx-1}_{(FI if widx <= 2 else D)//128 - 1}")
                                    if widx >= 1 else None)
                            w("scalar", prev)
                        tp_ev[pp] = op(
                            "scalar", "sS", lambda E, c=cb, p=pp: E.copy(
                                out=dbuf[:, 128 * c:128 * c + 128],
                                in_=psTF[p][:, 0:128]))
                    M[f"wqt{widx}_{t}"] = C.get("sT")
                    M[f"wqe{widx}_{t}"] = C.get("sS")
                    w("sync", "sS", M[f"wqe{widx}_{t}"])
                    if os.environ.get("KWX", "0") != "1":
                        for cb in range(nblk):
                            dma("sync", wq_ap_d[128 * cb:128 * cb + 128,
                                                128 * t:128 * (t + 1)],
                                dbuf[:, 128 * cb:128 * cb + 128])
                    M[f"wqs{widx}_{t}"] = snapd()
                return

            def wq_phase_guarded(widx, w_ap, wq_ap_d, nrows, ncols):
                wq_phase(widx, w_ap, wq_ap_d, nrows, ncols)

            KW = int(os.environ.get("KW", "3"))
            if KW >= 1:
                wq_phase_guarded(0, wgate, wqg_d, FI, D)
            if KW >= 2:
                wq_phase_guarded(1, wup, wqu_d, FI, D)
            if KW >= 3:
                wq_phase_guarded(2, wdown, wqd_d, D, FI)
            if KW < 3:
                fin(); return
            M["p1_S"] = C.get("sS")
            M["p1_T"] = C.get("sT")
            M["p1_V"] = C.get("sV")
            if KPH <= 1:
                fin(); return

            xt_f = lnar[:, 0:1024]
            for t in range(NTT):
                if t >= 1:
                    w("sync", "sT", M[f"xft{t-1}"])
                else:
                    w("sync", "sV", M["rg"])
                lv = dma("sync", xt_f, x_tok[128 * t:128 * (t + 1), :])
                w("tensor", lv)
                for c in range(8):
                    pp = c % 2
                    w("tensor", "sS", tp_ev[pp])
                    tv = op("tensor", "sT", lambda E, cc=c, p=pp: E.transpose(
                        out=psTF[p][:, 0:128],
                        in_=xt_f[:, 128 * cc:128 * cc + 128], identity=identF[:]))
                    w("scalar", "sT", tv)
                    tp_ev[pp] = op("scalar", "sS", lambda E, cc=c, tt=t, p=pp: E.copy(
                        out=bigs[cc][:, 128 * tt:128 * tt + 128],
                        in_=psTF[p][:, 0:128]))
                M[f"xft{t}"] = C.get("sT")
            M["xfm"] = C.get("sS")
            if KPH <= 2:
                fin(); return

            wqv_f32 = [lnar[:, 1024:2048], lnar[:, 2048:3072]]
            wqv_b16 = [gbuf[1][:, 0:1024], gbuf[1][:, 1024:2048]]
            qkv_sb = [gbuf[2][:, 0:512], gbuf[2][:, 512:1024]]
            vtm_sb = [gbuf[2][:, 1024:1152], gbuf[2][:, 1152:1280]]
            ev_bank = {0: 0, 1: 0}
            st_mark = {0: 0, 1: 0}
            vt_mark = {0: 0, 1: 0}
            vstage_use = M[f"xft{NTT-1}"]
            for f in range(24):
                par = f % 2
                if f >= 2:
                    w("sync", "sV", M[f"qkvw{f-2}"])
                lv = dma("sync",
                         wqv_f32[par].rearrange("p (c o) -> p c o", c=8),
                         wqkv[:, 128 * f:128 * (f + 1)]
                         .rearrange("(c p) o -> p c o", p=128))
                w("vector", lv)
                cv = op("vector", "sV", lambda E, p=par: E.tensor_copy(
                    out=wqv_b16[p][:], in_=wqv_f32[p][:]))
                M[f"qkvw{f}"] = cv
                w("tensor", "sV", cv)
                if f == 0:
                    w("tensor", "sS", M["xfm"])
                    w("tensor", "sV", M["wqm2"])
                for tt in range(8):
                    bk = tt % 2
                    w("tensor", "sS", ev_bank[bk])
                    for kc in range(8):
                        op("tensor", "sT", lambda E, k=kc, b2=bk, p=par, t8=tt: E.matmul(
                            out=(psA if b2 == 0 else psB)[:],
                            lhsT=wqv_b16[p][:, 128 * k:128 * (k + 1)],
                            rhs=bigs[k][:, 512 * t8:512 * (t8 + 1)],
                            start=(k == 0), stop=(k == 7)))
                    tv = C.get("sT")
                    w("scalar", "sT", tv)
                    w("scalar", st_mark[bk])
                    sv = op("scalar", "sS", lambda E, b2=bk, ff=f: E.activation(
                        out=qkv_sb[b2][:], in_=(psA if b2 == 0 else psB)[:],
                        func=Act.Identity, bias=qb_col[:, ff:ff + 1], scale=1.0))
                    ev_bank[bk] = sv
                    w("sync", "sS", sv)
                    bidx = tt // 4
                    ts0 = (tt % 4) * 512
                    dst = (q_d if f < 8 else (k_d if f < 16 else v_d))
                    hh = 2 * (f % 8)
                    st_mark[bk] = dma(
                        "sync", dst[bidx, hh:hh + 2, :, ts0:ts0 + 512]
                        .rearrange("h d t -> (h d) t"), qkv_sb[bk][:])
                    if f >= 16:
                        w("scalar", "sT", vstage_use)
                        sv2 = op("scalar", "sS", lambda E, b2=bk, ff=f: E.activation(
                            out=lnar[:, 0:512], in_=(psA if b2 == 0 else psB)[:],
                            func=Act.Identity, bias=qb_col[:, ff:ff + 1], scale=1.0))
                        ev_bank[bk] = sv2
                        w("tensor", "sS", sv2)
                        for c4 in range(4):
                            sl = c4 % 2
                            w("tensor", "sS", tp_ev[sl])
                            tv = op("tensor", "sT", lambda E, c=c4, s=sl: E.transpose(
                                out=psTF[s][:, 0:128],
                                in_=lnar[:, 128 * c:128 * c + 128],
                                identity=identF[:]))
                            w("scalar", "sT", tv)
                            w("scalar", vt_mark[sl])
                            sv = op("scalar", "sS", lambda E, s=sl: E.copy(
                                out=vtm_sb[s][:], in_=psTF[s][:, 0:128]))
                            tp_ev[sl] = sv
                            w("sync", "sS", sv)
                            tb = ts0 + 128 * c4
                            vt_mark[sl] = (
                                dma("sync", vtm_d[bidx, hh, tb:tb + 128, :],
                                    vtm_sb[sl][:, 0:64]),
                                dma("sync", vtm_d[bidx, hh + 1, tb:tb + 128, :],
                                    vtm_sb[sl][:, 64:128]))
                        vstage_use = C.get("sT")
            M["qkv_L"] = snapd()
            M["qkv_T"] = C.get("sT")
            M["qkv_S"] = C.get("sS")
            if KPH <= 3:
                fin(); return

            for c in range(4):
                if c >= 1:
                    w("sync", "sV", M[f"gw1c{c-1}"])
                lv = dma("sync",
                         lnar[:, 1024:2048].rearrange("p (k o) -> p k o", k=8),
                         gw1[:, 128 * c:128 * (c + 1)]
                         .rearrange("(k p) o -> p k o", p=128))
                w("vector", lv)
                M[f"gw1c{c}"] = op("vector", "sV", lambda E, cc=c: E.tensor_copy(
                    out=gbuf[3][:, 1024 * cc:1024 * (cc + 1)], in_=lnar[:, 1024:2048]))
            w("sync", "sV", M["gw1c3"])
            lv = dma("sync",
                     lnar[:, 1024:1088].rearrange("p (c h) -> p c h", c=4),
                     gw2.rearrange("(c p) h -> p c h", p=128))
            w("vector", lv)
            M["gw2"] = op("vector", "sV", lambda E: E.tensor_copy(
                out=k12[:, 0:64], in_=lnar[:, 1024:1088]))
            gtile = gbuf[1][:, 2048:2560]
            av_ev = 0
            gt_use = 0
            al_st = None
            for tt in range(8):
                for c in range(4):
                    bk = c % 2
                    w("tensor", "sV", M[f"gw1c{c}"])
                    w("tensor", "sV", M["gw2"])
                    w("tensor", "sS", ev_bank[bk])
                    for kc in range(8):
                        op("tensor", "sT", lambda E, k=kc, b2=bk, cc=c, t8=tt: E.matmul(
                            out=(psA if b2 == 0 else psB)[:],
                            lhsT=gbuf[3][:, 1024 * cc + 128 * k:1024 * cc + 128 * (k + 1)],
                            rhs=bigs[k][:, 512 * t8:512 * (t8 + 1)],
                            start=(k == 0), stop=(k == 7)))
                    tv = C.get("sT")
                    w("scalar", "sT", tv)
                    w("scalar", "sT", gt_use)
                    sv = op("scalar", "sS", lambda E, b2=bk, cc=c: E.activation(
                        out=gtile[:], in_=(psA if b2 == 0 else psB)[:],
                        func=Act.Gelu, bias=b1c[:, cc:cc + 1], scale=1.0))
                    ev_bank[bk] = sv
                    w("tensor", "sS", sv)
                    if c == 0:
                        w("tensor", "sS", av_ev)
                    op("tensor", "sT", lambda E, cc=c: E.matmul(
                        out=psAV[0:16, :], lhsT=k12[:, 16 * cc:16 * (cc + 1)],
                        rhs=gtile[:], start=(cc == 0), stop=(cc == 3)))
                    gt_use = C.get("sT")
                w("scalar", "sT", gt_use)
                w("scalar", al_st)
                av_ev = op("scalar", "sS", lambda E: E.activation(
                    out=a16[:, 0:512], in_=psAV[0:16, :], func=Act.Sigmoid,
                    bias=b2c[:], scale=1.0))
                w("sync", "sS", av_ev)
                al_st = dma("sync", al_d[:, 512 * tt:512 * (tt + 1)], a16[:, 0:512])
            M["alpha_L"] = snapd()
            M["p3_psAB"] = max(ev_bank[0], ev_bank[1])
            M["p3_psAV"] = av_ev
            M["p3_gtuse"] = gt_use
            if KPH <= 4:
                fin(); return

            last_kq_use = M["p3_gtuse"]
            prb_use = {0: 0, 1: 0}
            v65_use = {0: 0, 1: 0}
            sc_ev = {0: M["p3_psAB"], 1: M["p3_psAB"]}
            ST = {"av": 0, "first": True}
            blend_done = 0
            u_mark = {0: 0, 1: 0}
            scan_use = {0: 0, 1: 0}
            sel_mark = {0: 0, 1: 0}
            red_mark = {0: 0, 1: 0}
            tt_mark = {0: 0, 1: 0}
            wblk_use = 0
            KSG = int(os.environ.get("KSG", "0"))
            KWG = int(os.environ.get("KWG", "1"))
            KUG = os.environ.get("KUG", "1") == "1"
            ueng = "gpsimd" if KUG else "vector"
            usem = "sGE" if KUG else "sV"
            gscan_rg = {"done": False}
            w("sync", M["qkv_L"])
            w("vector", M["qkv_L"])
            w("gpsimd", M["qkv_L"])
            w("sync", M["alpha_L"])
            KH = int(os.environ.get("KH", str(H)))
            u_f = [far[:, 0:2048], far[:, 2048:4096]]
            z_f = [far[:, 4096:6144], far[:, 6144:8192]]
            vrep = [gbuf[1][:, 0:2048], gbuf[1][:, 2048:4096]]
            wbf = [gbuf[2][:, 0:2048], gbuf[2][:, 2048:4096]]

            def emit_attn(h, step, alm):
                b, qt, kc = step
                pp = kc % 2
                diag = kc >= 4 * qt
                w("tensor", "sS", sc_ev[pp])
                op("tensor", "sT", lambda E, k=kc, q=qt, bb=b, p=pp, dg=diag: E.matmul(
                    out=(psA if p == 0 else psB)[:],
                    lhsT=k12[64 * bb:64 * bb + 64, 128 * k:128 * (k + 1)],
                    rhs=q12[64 * bb:64 * bb + 64, 512 * q:512 * (q + 1)],
                    start=True, stop=not dg))
                if diag:
                    r = 128 * kc - 512 * qt
                    op("tensor", "sT", lambda E, p=pp, rr=r: E.matmul(
                        out=(psA if p == 0 else psB)[:], lhsT=ident[:],
                        rhs=gbuf[0][:, 512 * (rr // 128):512 * (rr // 128) + 512],
                        start=False, stop=True))
                qkmm = C.get("sT")
                w("scalar", "sT", qkmm)
                w("scalar", "sT", prb_use[pp])
                sv = op("scalar", "sS", lambda E, p=pp: E.activation(
                    out=prb[p][:], in_=(psA if p == 0 else psB)[:],
                    func=Act.Exp, bias=cbc[:, CI_EXPB:CI_EXPB + 1],
                    scale=0.125))
                sc_ev[pp] = sv
                w("sync", "sT", v65_use[pp])
                lv = dma("sync", v65[pp][:, 0:64],
                         vtm_d[b, h, 128 * kc:128 * (kc + 1), :])
                w("tensor", lv)
                w("tensor", "sS", sv)
                if kc == 0:
                    if ST["first"]:
                        w("tensor", "sS", tp_ev[0])
                        ST["first"] = False
                    w("tensor", "sS", ST["av"])
                avm = op("tensor", "sT", lambda E, p=pp, k=kc, q=qt: E.matmul(
                    out=psAV4[0:65, :], lhsT=v65[p][:], rhs=prb[p][:],
                    start=(k == 0), stop=(k == 4 * q + 3)))
                prb_use[pp] = avm
                v65_use[pp] = avm
                if kc == 4 * qt + 3:
                    w("scalar", "sT", avm)
                    if qt == 0 and h >= 1:
                        w("scalar", "sV", M[f"h{h-1}bl{b}"])
                    op("scalar", "sS", lambda E, bb=b, q=qt: E.copy(
                        out=ysm[bb][:, 512 * q:512 * (q + 1)], in_=psAV4[0:64, :]))
                    ST["av"] = op("scalar", "sS", lambda E, bb=b, q=qt: E.copy(
                        out=lnar[64 + 32 * bb:65 + 32 * bb, 512 * q:512 * (q + 1)],
                        in_=psAV4[64:65, :]))
                    if qt == NQT - 1:
                        M[f"h{h}ysmd{b}"] = ST["av"]

            def emit_coeff(h, b, alm):
                w("vector", "sS", M[f"h{h}ysmd{b}"])
                if h >= 1:
                    w("vector", M[f"h{h-1}cst{b}"])
                opq("vector", lambda E, bb=b: E.reciprocal(
                    out=lnar[64 + 32 * bb:65 + 32 * bb, 0:2048],
                    in_=lnar[64 + 32 * bb:65 + 32 * bb, 0:2048]))
                w("vector", alm[b])
                opq("vector", lambda E, r0=64 + 32 * b: E.tensor_scalar(
                    out=outw_sb[r0:r0 + 1, 2048:4096],
                    in0=outw_sb[r0:r0 + 1, 0:2048],
                    scalar1=-1.0, scalar2=1.0, op0=Alu.mult, op1=Alu.add))
                crm = op("vector", "sV", lambda E, r0=64 + 32 * b: E.tensor_mul(
                    out=outw_sb[r0:r0 + 1, 4096:6144],
                    in0=outw_sb[r0:r0 + 1, 0:2048],
                    in1=lnar[r0:r0 + 1, 0:2048]))
                M[f"h{h}cr{b}"] = crm
                if h >= 1:
                    w("sync", M[f"h{h-1}bc{b}"])
                w("sync", "sV", crm)
                st1 = dma("sync", cc_d[2 * b:2 * b + 1, :],
                          outw_sb[64 + 32 * b:65 + 32 * b, 4096:6144])
                st2 = dma("sync", cc_d[2 * b + 1:2 * b + 2, :],
                          outw_sb[64 + 32 * b:65 + 32 * b, 2048:4096])
                M[f"h{h}cst{b}"] = (st1, st2)
                w("sync", st1)
                w("sync", st2)
                if h >= 1:
                    w("sync", "sV", M[f"h{h-1}bl{b}"])
                b1m = dma("sync", dbuf[0:64, 2048 * b:2048 * (b + 1)],
                          cc_d[2 * b:2 * b + 1, :].broadcast_to([64, 2048]))
                b2m = dma("sync", gbuf[3][0:64, 2048 * b:2048 * (b + 1)],
                          cc_d[2 * b + 1:2 * b + 2, :].broadcast_to([64, 2048]))
                M[f"h{h}bc{b}"] = (b1m, b2m)

            def emit_wterm(h, j, stop):
                pj = j % 2
                weng = "gpsimd" if (j % 4) < KWG else "vector"
                wsem = "sGE" if weng == "gpsimd" else "sV"
                if j == 0:
                    w(weng, M[f"h{h}kq"])
                w(weng, scan_use[pj])
                w(weng, "sT", red_mark[pj])
                tt_mark[pj] = (wsem, opn(weng, wsem, lambda E, p=pj:
                                         E.tensor_mul(
                    out=wbf[p][:], in0=z_f[p][:], in1=q12[:])))
                w("tensor", tt_mark[pj])
                w("tensor", sel_mark[pj])
                for qt4 in range(4):
                    op("tensor", "sT", lambda E, q=qt4, p=pj, s=stop: E.matmul(
                        out=psY[:, 512 * q:512 * (q + 1)], lhsT=sel[p][:],
                        rhs=wbf[p][:, 512 * q:512 * (q + 1)],
                        start=False, stop=s))
                red_mark[pj] = C.get("sT")

            for h in range(KH):
                w("sync", "sT", last_kq_use)
                kqm = []
                for b in range(BPC):
                    kqm.append(dma("sync", k12[64 * b:64 * b + 64, :], k_d[b, h, :, :]))
                    kqm.append(dma("sync", q12[64 * b:64 * b + 64, :], q_d[b, h, :, :]))
                M[f"h{h}kq"] = tuple(kqm)
                if h == 0:
                    w("sync", "sT", M["p3_gtuse"])
                alm = {}
                for b in range(BPC):
                    if h >= 1:
                        w("sync", "sV", M[f"h{h-1}cr{b}"])
                    alm[b] = dma("sync", outw_sb[64 + 32 * b:65 + 32 * b, 0:2048],
                                 al_d[h:h + 1, 2048 * b:2048 * (b + 1)])
                w("vector", "sT", wblk_use)
                opq("vector", lambda E: E.memset(wblk[:], 0.0))
                opq("vector", lambda E, hh=h: E.tensor_copy(
                    out=wblk[0:64, 0:64], in_=wt2[0:64, 64 * hh:64 * hh + 64]))
                wbv = op("vector", "sV", lambda E, hh=h: E.tensor_copy(
                    out=wblk[64:128, 64:128], in_=wt2[64:128, 64 * hh:64 * hh + 64]))
                if h == 0:
                    opq("vector", lambda E: E.memset(v65[0][:, 64:65], 1.0))
                    opq("vector", lambda E: E.memset(v65[1][:, 64:65], 1.0))
                    w("vector", "sT", M["qkv_T"])
                    w("gpsimd", "sT", M["qkv_T"])
                w("tensor", M[f"h{h}kq"])
                w("tensor", "sV", wbv)
                w("tensor", "sV", blend_done)
                if h == 0:
                    w("tensor", "sS", M["p3_psAV"])
                for qt4 in range(4):
                    op("tensor", "sT", lambda E, q=qt4: E.matmul(
                        out=psY[:, 512 * q:512 * (q + 1)], lhsT=wblk[:],
                        rhs=q12[:, 512 * q:512 * (q + 1)], start=True, stop=False))
                wblk_use = C.get("sT")

                steps = [(b, qt, kc) for b in range(BPC) for qt in range(NQT)
                         for kc in range(4 * qt + 4)]
                for i in range(DH):
                    par = i % 2
                    if h == 0 and i < 2:
                        w("sync", "sT", M["p3_gtuse"])
                    w("sync", u_mark[par])
                    vmks = tuple(
                        dma("sync", vrep[par][64 * b:64 * b + 64, :],
                            v_d[b, h, i, :].unsqueeze(0)
                            .broadcast_to([64, 2048])) for b in range(BPC))
                    w(ueng, vmks)
                    if i == 0:
                        w(ueng, M[f"h{h}kq"])
                    w(ueng, scan_use[par])
                    u_mark[par] = (usem, opn(ueng, usem, lambda E, p=par:
                                             E.tensor_mul(
                        out=u_f[p][:], in0=vrep[p][:], in1=k12[:])))
                    w("vector", "sT", red_mark[par])
                    sel_mark[par] = ("sV", opn(
                        "vector", "sV", lambda E, ii=i, p=par, hh=h:
                        E.tensor_scalar_mul(
                            out=sel[p][:], in0=selb[:, 63 - ii:191 - ii],
                            scalar1=g_all[:, 64 * hh + ii:64 * hh + ii + 1])))
                    if i >= 1:
                        emit_wterm(h, i - 1, stop=False)
                    seng = "gpsimd" if (i % 4) < KSG else "vector"
                    ssem = "sGE" if seng == "gpsimd" else "sV"
                    if seng == "gpsimd" and not gscan_rg["done"]:
                        w("gpsimd", "sV", M["rg"])
                        gscan_rg["done"] = True
                    w(seng, u_mark[par])
                    w(seng, tt_mark[par])
                    scan_use[par] = (ssem, opn(
                        seng, ssem, lambda E, p=par, hh=h, ii=i: E.tensor_tensor_scan(
                            out=z_f[p][:],
                            data0=r_all[:, 64 * hh + ii:64 * hh + ii + 1]
                            .broadcast_to([128, 2048]),
                            data1=u_f[p][:], initial=0.0, op0=Alu.mult, op1=Alu.add)))
                    if 2 * i < len(steps):
                        emit_attn(h, steps[2 * i], alm)
                    if 2 * i + 1 < len(steps):
                        emit_attn(h, steps[2 * i + 1], alm)
                    if i == 22:
                        emit_coeff(h, 0, alm)
                    if i == 44:
                        emit_coeff(h, 1, alm)
                emit_wterm(h, DH - 1, stop=True)
                M[f"h{h}scan"] = red_mark[(DH - 1) % 2]
                last_kq_use = red_mark[(DH - 1) % 2]

                for b in range(BPC):
                    w("vector", M[f"h{h}bc{b}"])
                    w("vector", "sT", M[f"h{h}scan"])
                    if h <= 1:
                        w("vector", "sT", M["p3_gtuse"])
                    opq("vector", lambda E, bb=b: E.tensor_mul(
                        out=dbuf[0:64, 2048 * bb:2048 * (bb + 1)], in0=ysm[bb][:],
                        in1=dbuf[0:64, 2048 * bb:2048 * (bb + 1)]))
                    opq("vector", lambda E, bb=b: E.scalar_tensor_tensor(
                        out=gbuf[3][0:64, 2048 * bb:2048 * (bb + 1)],
                        in0=psY[64 * bb:64 * bb + 64, :], scalar=1.0,
                        in1=gbuf[3][0:64, 2048 * bb:2048 * (bb + 1)],
                        op0=Alu.mult, op1=Alu.mult))
                    rows0 = (h % 2) * 64
                    M[f"h{h}bl{b}"] = op(
                        "vector", "sV", lambda E, bb=b, r0=rows0, hh=h: E.tensor_add(
                            out=bigs[hh // 2][r0:r0 + 64, 2048 * bb:2048 * (bb + 1)],
                            in0=dbuf[0:64, 2048 * bb:2048 * (bb + 1)],
                            in1=gbuf[3][0:64, 2048 * bb:2048 * (bb + 1)]))
                blend_done = M[f"h{h}bl{1}"]
            tp_ev[0] = ST["av"]
            M["p4_V"] = C.get("sV")
            M["p4_psAB"] = max(sc_ev[0], sc_ev[1])
            if KPH <= 5:
                if KDBG:
                    w("sync", "sV", M["p4_V"])
                    for c in range(8):
                        dma("sync", g2_d[128 * c:128 * (c + 1), :], bigs[c][:])
                fin(); return

            w("sync", "sV", M["p4_V"])
            for kk in range(8):
                if kk >= 1:
                    w("sync", "sV", M[f"ow{kk-1}"])
                lv = dma("sync", lnar[:, 0:1024], outw[128 * kk:128 * (kk + 1), :])
                w("vector", lv)
                w("vector", "sV", M["p4_V"])
                M[f"ow{kk}"] = op("vector", "sV", lambda E, k=kk: E.tensor_copy(
                    out=outw_sb[:, 1024 * k:1024 * (k + 1)], in_=lnar[:, 0:1024]))
            xres = lnar[:, 0:1024]
            lnb = lnar[:, 1024:2048]
            d2b = lnar[:, 2048:3072]
            u0b = far[:, 0:1024]
            uLb = far[:, 1024:2048]
            xqF = far[:, 2048:3072]
            s_col = colw[:, 2:3]
            am_col = colw[:, 3:4]
            mn_col = colw[:, 4:5]
            vr_col = colw[:, 5:6]
            rs_col = colw[:, 6:7]
            res_done = 0
            x1st = None
            for t in range(NTT):
                w("tensor", "sV", M["ow7"])
                w("tensor", "sS", M["p4_psAB"])
                w("tensor", "sV", res_done)
                for half in range(2):
                    for kk in range(8):
                        op("tensor", "sT", lambda E, k=kk, hf=half, tt=t: E.matmul(
                            out=(psA if hf == 0 else psB)[:],
                            lhsT=bigs[k][:, 128 * tt:128 * (tt + 1)],
                            rhs=outw_sb[:, 1024 * k + 512 * hf:1024 * k + 512 * hf + 512],
                            start=(k == 0), stop=(k == 7)))
                mmv = C.get("sT")
                M[f"p5mm{t}"] = mmv
                if t == 0:
                    w("sync", "sV", M["ow7"])
                lv = dma("sync", xres, x_tok[128 * t:128 * (t + 1), :])
                w("vector", "sT", mmv)
                w("vector", lv)
                w("vector", x1st)
                for half in range(2):
                    opq("vector", lambda E, hf=half: E.scalar_tensor_tensor(
                        out=lnb[:, 512 * hf:512 * hf + 512],
                        in0=(psA if hf == 0 else psB)[:], scalar=1.0,
                        in1=xres[:, 512 * hf:512 * hf + 512],
                        op0=Alu.mult, op1=Alu.add))
                if t == 0:
                    w("vector", M["outb"])
                res_done = op("vector", "sV", lambda E: E.tensor_add(
                    out=lnb[:], in0=lnb[:], in1=outb_bc[:]))
                opq("vector", lambda E: E.reduce_sum(out=mn_col, in_=lnb[:], axis=AxX))
                opq("vector", lambda E: E.tensor_scalar_mul(
                    out=mn_col, in0=mn_col, scalar1=1.0 / D))
                opq("vector", lambda E: E.tensor_single_scalar(
                    out=d2b, in_=lnb[:], scalar=mn_col, op=Alu.subtract))
                opq("vector", lambda E: E.scalar_tensor_tensor(
                    out=lnb[:], in0=d2b, scalar=1.0, in1=d2b,
                    op0=Alu.mult, op1=Alu.mult, accum_out=vr_col))
                vv = op("vector", "sV", lambda E: E.tensor_scalar(
                    out=vr_col, in0=vr_col, scalar1=1.0 / D, scalar2=1e-5,
                    op0=Alu.mult, op1=Alu.add))
                w("gpsimd", "sV", vv)
                gv = op("gpsimd", "sGE", lambda E: E.tensor_tensor(
                    out=rs_col, in0=vr_col, in1=cbc[:, CI_PEXP:CI_PEXP + 1],
                    op=Alu.pow))
                w("vector", "sGE", gv)
                x1v = op("vector", "sV", lambda E: E.tensor_scalar_mul(
                    out=lnb[:], in0=d2b, scalar1=rs_col))
                w("sync", "sV", x1v)
                x1st = dma("sync", x1_d[128 * t:128 * (t + 1), :], lnb[:])
                opq("vector", lambda E: E.reduce_max(
                    out=am_col, in_=lnb[:], axis=AxX, apply_absolute_value=True))
                opq("vector", lambda E: E.tensor_scalar_max(
                    out=am_col, in0=am_col, scalar1=1e-5))
                opq("vector", lambda E: E.reciprocal(out=s_col, in_=am_col))
                opq("vector", lambda E: E.tensor_scalar_mul(
                    out=s_col, in0=s_col, scalar1=127.0))
                opq("vector", lambda E, tt=t: E.scalar_tensor_tensor(
                    out=oscg[:, tt:tt + 1], in0=am_col, scalar=1.0 / 127.0,
                    in1=mcols[:, 0:1], op0=Alu.mult, op1=Alu.mult))
                opq("vector", lambda E, tt=t: E.scalar_tensor_tensor(
                    out=oscu[:, tt:tt + 1], in0=am_col, scalar=1.0 / 127.0,
                    in1=mcols[:, 1:2], op0=Alu.mult, op1=Alu.mult))
                opq("vector", lambda E: E.tensor_scalar(
                    out=u0b, in0=lnb[:], scalar1=s_col, scalar2=128.0,
                    op0=Alu.mult, op1=Alu.add))
                w("vector", "sT", M.get(f"p5tp{t-1}", 0))
                xqv = op("vector", "sV", lambda E: E.tensor_scalar(
                    out=xqF, in0=u0b, scalar1=12582912.0, scalar2=-12582912.0,
                    op0=Alu.add, op1=Alu.add))
                w("tensor", "sV", xqv)
                for c in range(8):
                    pp = c % 2
                    w("tensor", "sS", tp_ev[pp])
                    tv = op("tensor", "sT", lambda E, cc=c, p=pp: E.transpose(
                        out=psTF[p][:, 0:128],
                        in_=xqF[:, 128 * cc:128 * cc + 128],
                        identity=identF[:]))
                    w("scalar", "sT", tv)
                    tp_ev[pp] = op("scalar", "sS", lambda E, cc=c, tt=t, p=pp: E.activation(
                        out=bigs[cc][:, 128 * tt:128 * tt + 128],
                        in_=psTF[p][:, 0:128], func=Act.Identity,
                        bias=cbc[:, CI_SGNB:CI_SGNB + 1], scale=1.0))
                M[f"p5tp{t}"] = C.get("sT")
                M[f"p5q{t}"] = C.get("sS")
            M["p5_S"] = C.get("sS")
            M["p5_V"] = C.get("sV")
            if KPH <= 6:
                fin(); return

            wgv = [gbuf[1][:, 0:2048], gbuf[1][:, 2048:4096]]
            wuv = [gbuf[2][:, 0:2048], gbuf[2][:, 2048:4096]]
            h1t = gbuf[3][:, 0:256]
            h2t = gbuf[3][:, 256:512]
            g2t = [gbuf[3][:, 512:768], gbuf[3][:, 768:1024]]
            opq("vector", lambda E: E.memset(amax2[:], 0.0))
            evA = M["p4_psAB"]
            evB = M["p4_psAB"]
            h1_use = 0
            wblk_use = {0: 0, 1: 0}
            g2st = {0: 0, 1: 0}
            for nb in range(FI // 256):
                par = nb % 2
                if nb == 0:
                    w("sync", "sT", M[f"h{H-1}scan"])
                    w("sync", "sV", M["p4_V"])
                w("sync", "sT", wblk_use[par])
                lg = dma("sync",
                         wgv[par].rearrange("p (c o) -> p c o", c=8),
                         wqg_d[:, 256 * nb:256 * (nb + 1)]
                         .rearrange("(c p) o -> p c o", p=128))
                lu = dma("sync",
                         wuv[par].rearrange("p (c o) -> p c o", c=8),
                         wqu_d[:, 256 * nb:256 * (nb + 1)]
                         .rearrange("(c p) o -> p c o", p=128))
                for tt in range(NTT):
                    w("tensor", lg)
                    w("tensor", lu)
                    if nb == 0 and tt == 0:
                        w("tensor", "sS", M["p5_S"])
                        w("tensor", "sV", res_done)
                    w("tensor", "sS", evA)
                    w("tensor", "sV", evB)
                    for kk in range(8):
                        op("tensor", "sT", lambda E, k=kk, p=par, t2=tt: E.matmul(
                            out=psA[:, 0:256],
                            lhsT=bigs[k][:, 128 * t2:128 * (t2 + 1)],
                            rhs=wgv[p][:, 256 * k:256 * (k + 1)],
                            start=(k == 0), stop=(k == 7)))
                        op("tensor", "sT", lambda E, k=kk, p=par, t2=tt: E.matmul(
                            out=psB[:, 0:256],
                            lhsT=bigs[k][:, 128 * t2:128 * (t2 + 1)],
                            rhs=wuv[p][:, 256 * k:256 * (k + 1)],
                            start=(k == 0), stop=(k == 7)))
                    mmv = C.get("sT")
                    wblk_use[par] = mmv
                    w("scalar", "sT", mmv)
                    if nb == 0 and tt == 0:
                        w("scalar", "sV", M["p5_V"])
                    w("scalar", "sV", h1_use)
                    evA = op("scalar", "sS", lambda E, t2=tt: E.activation(
                        out=h1t[:], in_=psA[:, 0:256], func=Act.Silu,
                        scale=oscg[:, t2:t2 + 1]))
                    w("vector", "sS", evA)
                    w("vector", "sT", mmv)
                    w("vector", g2st[tt % 2])
                    opq("vector", lambda E, t2=tt: E.tensor_scalar_mul(
                        out=h2t[:], in0=psB[:, 0:256], scalar1=oscu[:, t2:t2 + 1]))
                    gv = op("vector", "sV", lambda E, p2=tt % 2: E.tensor_mul(
                        out=g2t[p2][:], in0=h1t[:], in1=h2t[:]))
                    evB = gv
                    h1_use = gv
                    opq("vector", lambda E, p2=tt % 2: E.reduce_max(
                        out=tmp_col, in_=g2t[p2][:], axis=AxX,
                        apply_absolute_value=True))
                    av = op("vector", "sV", lambda E, t2=tt: E.tensor_max(
                        out=amax2[:, t2:t2 + 1], in0=amax2[:, t2:t2 + 1],
                        in1=tmp_col))
                    w("sync", "sV", av)
                    g2st[tt % 2] = dma(
                        "sync", g2_d[128 * tt:128 * (tt + 1), 256 * nb:256 * (nb + 1)],
                        g2t[tt % 2][:])
            M["p6_T"] = C.get("sT")
            M["p6_V"] = C.get("sV")
            M["p6_L"] = snapd()
            if KPH <= 7:
                fin(); return

            w("sync", "sT", M["p6_T"])
            w("sync", M["p6_L"])
            for kk in range(32):
                lv = dma("sync", bigs[kk // 4][:, 1024 * (kk % 4):1024 * (kk % 4) + 1024],
                         wqd_d[128 * kk:128 * (kk + 1), :])
            M["wdT"] = snapd()
            g2l = [gbuf[1][:, 0:4096], gbuf[2][:, 0:4096]]
            gqb = gbuf[3][:, 0:4096]
            u0w = far[:, 0:4096]
            uLw = far[:, 4096:8192]
            s2_col = colw[:, 8:9]
            oscd = colw[:, 9:10]
            u0_done = {0: 0, 1: 0}
            gq_used = 0
            res7 = 0
            psT7 = M["p5_S"]
            for t in range(NTT):
                par = t % 2
                w("sync", "sV", u0_done[par])
                lv = dma("sync", g2l[par][:], g2_d[128 * t:128 * (t + 1), :])
                w("vector", lv)
                if t == 0:
                    w("vector", "sV", M["p6_V"])
                opq("vector", lambda E, t2=t: E.tensor_scalar_max(
                    out=am_col, in0=amax2[:, t2:t2 + 1], scalar1=1e-5))
                opq("vector", lambda E: E.reciprocal(out=s2_col, in_=am_col))
                opq("vector", lambda E: E.tensor_scalar_mul(
                    out=s2_col, in0=s2_col, scalar1=127.0))
                opq("vector", lambda E: E.scalar_tensor_tensor(
                    out=oscd, in0=am_col, scalar=1.0 / 127.0,
                    in1=mcols[:, 2:3], op0=Alu.mult, op1=Alu.mult))
                w("vector", "sT", M.get(f"p7tp{t-1}", 0))
                u0v = op("vector", "sV", lambda E, p=par: E.tensor_scalar(
                    out=u0w, in0=g2l[p][:], scalar1=s2_col, scalar2=128.0,
                    op0=Alu.mult, op1=Alu.add))
                u0_done[par] = u0v
                xqv = op("vector", "sV", lambda E: E.tensor_scalar(
                    out=u0w, in0=u0w, scalar1=12582912.0, scalar2=-12582912.0,
                    op0=Alu.add, op1=Alu.add))
                w("tensor", "sV", xqv)
                for c in range(32):
                    pp = c % 2
                    w("tensor", "sS", tp_ev[pp])
                    tv = op("tensor", "sT", lambda E, cc=c, p=pp: E.transpose(
                        out=psTF[p][:, 0:128],
                        in_=u0w[:, 128 * cc:128 * cc + 128], identity=identF[:]))
                    w("scalar", "sT", tv)
                    if c == 0:
                        w("scalar", "sT", gq_used)
                        if t == 0:
                            w("scalar", "sV", M["p6_V"])
                            w("scalar", M["p6_L"])
                    tp_ev[pp] = op("scalar", "sS", lambda E, cc=c, p=pp: E.activation(
                        out=gqb[:, 128 * cc:128 * cc + 128],
                        in_=psTF[p][:, 0:128],
                        func=Act.Identity, bias=cbc[:, CI_SGNB:CI_SGNB + 1],
                        scale=1.0))
                M[f"p7tp{t}"] = C.get("sT")
                psT7 = C.get("sS")
                w("tensor", "sS", psT7)
                w("tensor", M["wdT"])
                w("tensor", "sV", res7)
                for kk in range(32):
                    op("tensor", "sT", lambda E, k=kk: E.matmul(
                        out=psA[:], lhsT=gqb[:, 128 * k:128 * (k + 1)],
                        rhs=bigs[k // 4][:, 1024 * (k % 4):1024 * (k % 4) + 512],
                        start=(k == 0), stop=(k == 31)))
                    op("tensor", "sT", lambda E, k=kk: E.matmul(
                        out=psB[:], lhsT=gqb[:, 128 * k:128 * (k + 1)],
                        rhs=bigs[k // 4][:, 1024 * (k % 4) + 512:1024 * (k % 4) + 1024],
                        start=(k == 0), stop=(k == 31)))
                gq_used = C.get("sT")
                lv = dma("sync", xres, x1_d[128 * t:128 * (t + 1), :])
                w("vector", "sT", gq_used)
                w("vector", lv)
                if t >= 1:
                    w("vector", "sO", 16 * t)
                for half in range(2):
                    opq("vector", lambda E, hf=half: E.tensor_scalar_mul(
                        out=d2b[:, 512 * hf:512 * hf + 512],
                        in0=(psA if hf == 0 else psB)[:], scalar1=oscd))
                res7 = op("vector", "sV", lambda E: E.tensor_add(
                    out=lnb[:], in0=d2b, in1=xres))
                opq("vector", lambda E: E.reduce_sum(out=mn_col, in_=lnb[:], axis=AxX))
                opq("vector", lambda E: E.tensor_scalar_mul(
                    out=mn_col, in0=mn_col, scalar1=1.0 / D))
                opq("vector", lambda E: E.tensor_single_scalar(
                    out=d2b, in_=lnb[:], scalar=mn_col, op=Alu.subtract))
                opq("vector", lambda E: E.scalar_tensor_tensor(
                    out=lnb[:], in0=d2b, scalar=1.0, in1=d2b,
                    op0=Alu.mult, op1=Alu.mult, accum_out=vr_col))
                vv = op("vector", "sV", lambda E: E.tensor_scalar(
                    out=vr_col, in0=vr_col, scalar1=1.0 / D, scalar2=1e-5,
                    op0=Alu.mult, op1=Alu.add))
                w("gpsimd", "sV", vv)
                gv = op("gpsimd", "sGE", lambda E: E.tensor_tensor(
                    out=rs_col, in0=vr_col, in1=cbc[:, CI_PEXP:CI_PEXP + 1],
                    op=Alu.pow))
                w("vector", "sGE", gv)
                ov = op("vector", "sV", lambda E: E.tensor_scalar_mul(
                    out=lnb[:], in0=d2b, scalar1=rs_col))
                w("sync", "sV", ov)
                dma("sync", out_tok[128 * t:128 * (t + 1), :], lnb[:], sem="sO")
            w("sync", "sO", C.get("sO"))

        blk.sync(lambda E: program(E, "sync"))
        blk.vector(lambda E: program(E, "vector"))
        blk.scalar(lambda E: program(E, "scalar"))
        blk.tensor(lambda E: program(E, "tensor"))
        blk.gpsimd(lambda E: program(E, "gpsimd"))
    return nc


def _host_consts():
    kk = np.arange(128)[:, None]
    qq = np.arange(512)[None, :]
    maskbig = np.concatenate(
        [np.where(kk <= qq - r, 0.0, -704.0).astype(ml_dtypes.bfloat16)
         for r in (0, 128, 256, 384)], axis=1)
    ident = np.eye(128, dtype=ml_dtypes.bfloat16)
    p = np.arange(128)[:, None]
    m = np.arange(128)[None, :]
    iot = (m - 64 * (p // 64)).astype(ml_dtypes.bfloat16)
    dd = np.arange(192)[None, :]
    selc = (dd == 63 + 64 * (p // 64)).astype(ml_dtypes.bfloat16)
    return maskbig, ident, iot, selc


def build_shared(L0_attn_Wqkv_w, L0_attn_Wqkv_b, L0_attn_out_w,
                 L0_attn_out_b, L0_attn_W_static, L0_attn_Lambda_raw,
                 L0_attn_Gamma, L0_attn_gate_w1, L0_attn_gate_b1,
                 L0_attn_gate_w2, L0_attn_gate_b2, L0_ffn_w_gate,
                 L0_ffn_w_up, L0_ffn_w_down):
    maskbig, ident, iot, selc = _host_consts()
    shared = {
        "wqkv": np.ascontiguousarray(L0_attn_Wqkv_w, dtype=np.float32),
        "bqkv": np.ascontiguousarray(L0_attn_Wqkv_b, dtype=np.float32),
        "outw": np.ascontiguousarray(L0_attn_out_w, dtype=np.float32),
        "outb": np.ascontiguousarray(L0_attn_out_b, dtype=np.float32),
        "wst": np.ascontiguousarray(L0_attn_W_static, dtype=np.float32),
        "lam": np.ascontiguousarray(L0_attn_Lambda_raw, dtype=np.float32),
        "gam": np.ascontiguousarray(L0_attn_Gamma, dtype=np.float32),
        "gw1": np.ascontiguousarray(L0_attn_gate_w1, dtype=np.float32),
        "gb1": np.ascontiguousarray(L0_attn_gate_b1, dtype=np.float32),
        "gw2": np.ascontiguousarray(L0_attn_gate_w2, dtype=np.float32),
        "gb2": np.ascontiguousarray(L0_attn_gate_b2, dtype=np.float32),
        "wgate": np.ascontiguousarray(L0_ffn_w_gate, dtype=np.float32),
        "wup": np.ascontiguousarray(L0_ffn_w_up, dtype=np.float32),
        "wdown": np.ascontiguousarray(L0_ffn_w_down, dtype=np.float32),
        "maskc": maskbig, "identc": ident, "ioc": iot, "selc": selc,
        "identf": np.eye(128, dtype=np.float32),
        "cvec": CONSTS.reshape(1, 8).copy(),
        "bqkv_r": np.ascontiguousarray(
            np.asarray(L0_attn_Wqkv_b, dtype=np.float32).reshape(24, 128).T),
        "gb1_r": np.ascontiguousarray(
            np.asarray(L0_attn_gate_b1, dtype=np.float32).reshape(4, 128).T),
        "lamT": np.ascontiguousarray(
            np.asarray(L0_attn_Lambda_raw, dtype=np.float32)
            .transpose(2, 0, 1).reshape(64, 1024)),
        "gamT": np.ascontiguousarray(
            np.asarray(L0_attn_Gamma, dtype=np.float32)
            .transpose(2, 0, 1).reshape(64, 1024)),
        "wstT": np.ascontiguousarray(
            np.asarray(L0_attn_W_static, dtype=np.float32)
            .transpose(2, 0, 1).reshape(64, 1024)),
    }
    return shared


def kernel(L0_attn_x, L0_attn_Wqkv_w, L0_attn_Wqkv_b, L0_attn_out_w,
           L0_attn_out_b, L0_attn_W_static, L0_attn_Lambda_raw, L0_attn_Gamma,
           L0_attn_gate_w1, L0_attn_gate_b1, L0_attn_gate_w2, L0_attn_gate_b2,
           L0_ffn_w_gate, L0_ffn_w_up, L0_ffn_w_down, _n_cores=NCORES):
    shared = build_shared(
        L0_attn_Wqkv_w, L0_attn_Wqkv_b, L0_attn_out_w, L0_attn_out_b,
        L0_attn_W_static, L0_attn_Lambda_raw, L0_attn_Gamma, L0_attn_gate_w1,
        L0_attn_gate_b1, L0_attn_gate_w2, L0_attn_gate_b2, L0_ffn_w_gate,
        L0_ffn_w_up, L0_ffn_w_down)
    nc = build_nc()
    x = np.ascontiguousarray(np.asarray(L0_attn_x, dtype=np.float32))
    in_maps = []
    for c in range(_n_cores):
        m = dict(shared)
        m["x"] = np.ascontiguousarray(x[BPC * c:BPC * (c + 1)])
        in_maps.append(m)
    res = run_bass_kernel_spmd(nc, in_maps, core_ids=list(range(_n_cores)))
    return np.concatenate([r["out"] for r in res.results], axis=0)


if __name__ == "__main__":
    nc = build_nc()
    print("build ok; instructions:", nc.n_instructions())
